# version 08
# baseline (speedup 1.0000x reference)
import contextlib
import numpy as np
import ml_dtypes
import concourse.bass as bass
import concourse.mybir as mybir
from concourse.bass_utils import run_bass_kernel_spmd

F32 = mybir.dt.float32
BF16 = mybir.dt.bfloat16
I32 = mybir.dt.int32
AF = mybir.ActivationFunctionType
ALU = mybir.AluOpType
AX = mybir.AxisListType
NPBF16 = ml_dtypes.bfloat16

NCORES = 8
B, T, DM = 2, 8192, 1024
SLAB = 2048
DI = 2048
CH = 128
NCH_CORE = SLAB // CH
NCH = T // CH
LN2 = float(np.log(2.0))



class Ph:
    ENGS = ("sync", "gpsimd", "vector", "scalar", "tensor")

    def __init__(self, name):
        self.nc = bass.Bass("TRN2", target_bir_lowering=False,
                            detect_race_conditions=False)
        self.name = name
        self.q = {e: [] for e in self.ENGS}
        self.cnt = {}
        self.sems = {}
        self._stack = contextlib.ExitStack()
        self._uid = 0

    def dram(self, name, shape, dt=F32, kind="ExternalInput"):
        return self.nc.dram_tensor(name, list(shape), dt, kind=kind).ap()

    def sb(self, name, shape, dt=F32):
        return self._stack.enter_context(self.nc.sbuf_tensor(name, list(shape), dt))

    def ps(self, name, shape, dt=F32):
        return self._stack.enter_context(self.nc.psum_tensor(name, list(shape), dt))

    def sem(self, name):
        if name not in self.sems:
            self.sems[name] = self._stack.enter_context(self.nc.semaphore(name))
            self.cnt[name] = 0
        return self.sems[name]

    CHAIN = {"vector": "SV", "scalar": "SS"}

    def emit(self, eng, fn, wait=(), inc=None, inc_amt=1):
        for s, _ in wait:
            self.sem(s)
        chain = self.CHAIN.get(eng)
        prev = 0
        after = None
        if chain is not None:
            self.sem(chain)
            assert inc is None or inc == chain, (eng, inc)
            prev = self.cnt[chain]
            self.cnt[chain] = prev + 1
            after = prev + 1
            incname = None
            amt = 0
        else:
            incname = inc
            amt = inc_amt
            if inc is not None:
                self.sem(inc)
                self.cnt[inc] += inc_amt
                after = self.cnt[inc]
        waits = tuple(wait)

        def thunk(h):
            if chain is not None and prev > 0:
                h.wait_ge(self.sems[chain], prev)
            for s, v in waits:
                h.wait_ge(self.sems[s], v)
            r = fn(h)
            if r is None:
                r = h.nop()
            if chain is not None:
                r.then_inc(self.sems[chain], 1)
            elif incname is not None:
                r.then_inc(self.sems[incname], amt)
        self.q[eng].append(thunk)
        return after

    def dma(self, eng, out, in_, wait=(), inc=None, **kw):
        return self.emit(eng, lambda h: h.dma_start(out=out, in_=in_, **kw),
                         wait=wait, inc=inc, inc_amt=16)

    def wait(self, eng, *waits):
        for s, _ in waits:
            self.sem(s)
        self.q[eng].append(lambda h, w=tuple(waits): [h.wait_ge(self.sems[s], v) for s, v in w])

    def c(self, sem):
        self.sem(sem)
        return self.cnt[sem]

    def finish(self):
        nc = self.nc
        with nc.Block() as block:
            for e in self.ENGS:
                deco = getattr(block, e)
                qe = self.q[e]
                def body(h, qe=qe):
                    for t in qe:
                        t(h)
                deco(body)
        self._stack.close()
        return nc


_PROG_CACHE = {}


def run_phase(build_fn, in_maps, cache_key=None):
    key = cache_key or build_fn.__name__
    if key not in _PROG_CACHE:
        ph = build_fn()
        _PROG_CACHE[key] = ph.finish()
    nc = _PROG_CACHE[key]
    res = run_bass_kernel_spmd(nc, in_maps, core_ids=list(range(len(in_maps))))
    return res.results


def bf(x):
    return np.ascontiguousarray(x).astype(NPBF16)


def f32(x):
    return np.ascontiguousarray(x, dtype=np.float32)


def build_ln():
    p = Ph("ln")
    xs = p.dram("xs", (SLAB, DM))
    xn = p.dram("xn", (SLAB, DM), kind="ExternalOutput")
    NT = SLAB // 128
    xt = [p.sb(f"xt{i}", (128, DM)) for i in range(2)]
    xo = [p.sb(f"xo{i}", (128, DM)) for i in range(2)]
    sq = p.sb("sq", (128, DM))
    ssum = p.sb("ssum", (128, 1))
    msq = p.sb("msq", (128, 1))
    mean = p.sb("mean", (128, 1))
    var = p.sb("var", (128, 1))
    rstd = p.sb("rstd", (128, 1))
    eps = p.sb("eps", (128, 1))
    p.emit("vector", lambda h: h.memset(eps[:], 1e-5), inc="SV")
    for i in range(NT):
        b = i % 2
        w = [("SO", 16 * (i - 1))] if i >= 2 else []
        p.dma("sync", xt[b][:], xs[128 * i:128 * (i + 1), :], wait=w, inc="LD")
        ld = p.c("LD")
        p.emit("vector", lambda h, b=b: h.tensor_mul(out=sq[:], in0=xt[b][:], in1=xt[b][:]),
               wait=[("LD", ld), ("SS", p.c("SS")), ("SV", p.c("SV"))])
        p.emit("vector", lambda h: h.tensor_reduce(out=ssum[:], in_=sq[:],
                                                   axis=AX.X, op=ALU.add, opt_input=False, opt_output=False),
               inc="SV")
        p.emit("vector", lambda h, b=b: h.tensor_reduce(out=mean[:], in_=xt[b][:],
                                                        axis=AX.X, op=ALU.add, opt_input=False, opt_output=False),
               inc="SV")
        p.emit("vector", lambda h: h.tensor_scalar_mul(out=mean[:], in0=mean[:],
                                                       scalar1=1.0 / DM),
               wait=[("SV", p.c("SV"))])
        p.emit("vector", lambda h: h.tensor_mul(out=msq[:], in0=mean[:], in1=mean[:]))
        p.emit("vector", lambda h: h.scalar_tensor_tensor(out=var[:], in0=ssum[:],
                                                          scalar=1.0 / DM, in1=msq[:],
                                                          op0=ALU.mult, op1=ALU.subtract),
               inc="SV")
        p.emit("scalar", lambda h: h.activation(out=var[:], in_=var[:], func=AF.Ln,
                                                bias=eps[:]),
               wait=[("SV", p.c("SV"))])
        p.emit("scalar", lambda h: h.activation(out=rstd[:], in_=var[:], func=AF.Exp,
                                                scale=-0.5), inc="SS")
        p.emit("vector", lambda h, b=b: h.tensor_scalar(out=xo[b][:], in0=xt[b][:],
                                                        scalar1=mean[:], scalar2=rstd[:],
                                                        op0=ALU.subtract, op1=ALU.mult),
               wait=[("SS", p.c("SS"))], inc="SV")
        p.dma("sync", xn[128 * i:128 * (i + 1), :], xo[b][:],
              wait=[("SV", p.c("SV"))], inc="SO")
    p.wait("sync", ("SO", p.c("SO")))
    return p


def quant_scratch(p, tag=""):
    NB = DM // 32
    names = [("ab", DM, F32), ("amax", NB, F32), ("ei", NB, I32), ("ef", NB, F32),
             ("exf", NB, F32),
             ("sc", NB, F32), ("inv", NB, F32), ("xs_", DM, F32), ("sgn", DM, F32),
             ("aa", DM, F32), ("g2", DM, F32), ("g4", DM, F32), ("rec", DM, F32),
             ("stp", DM, F32), ("tq", DM, F32), ("ri", DM, I32), ("rf", DM, F32)]
    return {n: p.sb(f"q_{n}{tag}", (128, d), t) for n, d, t in names}


def _quant_tile(p, src_ap, dst_ap, nrows, delta_col, sc_):
    NB = DM // 32
    ab, amax, ei, ef = sc_["ab"], sc_["amax"], sc_["ei"], sc_["ef"]
    exf = sc_["exf"]
    sc, inv, xs_, sgn = sc_["sc"], sc_["inv"], sc_["xs_"], sc_["sgn"]
    aa, g2, g4, rec = sc_["aa"], sc_["g2"], sc_["g4"], sc_["rec"]
    stp, tq, ri, rf = sc_["stp"], sc_["tq"], sc_["ri"], sc_["rf"]

    r = slice(0, nrows)
    v = lambda t: t[r, :]
    p.emit("vector", lambda h: h.tensor_reduce(
        out=v(amax), in_=src_ap[r, :].rearrange("p (a b) -> p a b", b=32),
        axis=AX.X, op=ALU.max, apply_absolute_value=True,
        opt_input=False, opt_output=False), inc="SV")
    p.emit("vector", lambda h: h.tensor_scalar_max(out=v(amax), in0=v(amax),
                                                   scalar1=1e-30),
           wait=[("SV", p.c("SV"))], inc="SV")
    p.emit("scalar", lambda h: h.activation(out=v(ef), in_=v(amax), func=AF.Ln),
           wait=[("SV", p.c("SV"))], inc="SS")
    p.emit("vector", lambda h: h.tensor_scalar(out=v(ei), in0=v(ef), scalar1=1.0 / LN2,
                                               scalar2=0.5, op0=ALU.mult, op1=ALU.subtract),
           wait=[("SS", p.c("SS"))])
    p.emit("vector", lambda h: h.tensor_copy(out=v(exf), in_=v(ei)))
    p.emit("vector", lambda h: h.tensor_sub(out=v(exf), in0=v(exf), in1=v(delta_col)),
           inc="SV")
    p.emit("scalar", lambda h: h.activation(out=v(sc), in_=v(exf), func=AF.Exp, scale=LN2),
           wait=[("SV", p.c("SV"))])
    p.emit("scalar", lambda h: h.activation(out=v(inv), in_=v(exf), func=AF.Exp, scale=-LN2),
           inc="SS")
    p.emit("vector", lambda h: h.tensor_tensor(
        out=xs_[r, :].rearrange("p (a b) -> p a b", b=32),
        in0=src_ap[r, :].rearrange("p (a b) -> p a b", b=32),
        in1=inv[r, :].unsqueeze(2).broadcast_to((nrows, NB, 32)),
        op=ALU.mult), wait=[("SS", p.c("SS"))], inc="SV")
    p.emit("scalar", lambda h: h.activation(out=v(sgn), in_=v(xs_), func=AF.Sign),
           wait=[("SV", p.c("SV"))])
    p.emit("scalar", lambda h: h.activation(out=v(aa), in_=v(xs_), func=AF.Abs))
    p.emit("vector", lambda h: h.tensor_scalar_min(out=v(aa), in0=v(aa), scalar1=6.0),
           wait=[("SS", p.c("SS"))])
    p.emit("vector", lambda h: h.tensor_single_scalar(out=v(g2), in_=v(aa), scalar=2.0,
                                                      op=ALU.is_ge))
    p.emit("vector", lambda h: h.tensor_single_scalar(out=v(g4), in_=v(aa), scalar=4.0,
                                                      op=ALU.is_ge))
    p.emit("vector", lambda h: h.tensor_scalar(out=v(rec), in0=v(g2), scalar1=-1.0,
                                               scalar2=2.0, op0=ALU.mult, op1=ALU.add))
    p.emit("vector", lambda h: h.scalar_tensor_tensor(out=v(rec), in0=v(g4), scalar=-0.5,
                                                      in1=v(rec), op0=ALU.mult, op1=ALU.add))
    p.emit("vector", lambda h: h.tensor_scalar(out=v(stp), in0=v(g2), scalar1=0.5,
                                               scalar2=0.5, op0=ALU.mult, op1=ALU.add))
    p.emit("vector", lambda h: h.tensor_add(out=v(stp), in0=v(stp), in1=v(g4)))
    p.emit("vector", lambda h: h.tensor_mul(out=v(tq), in0=v(aa), in1=v(rec)))
    p.emit("vector", lambda h: h.tensor_copy(out=v(ri), in_=v(tq)))
    p.emit("vector", lambda h: h.tensor_copy(out=v(rf), in_=v(ri)))
    p.emit("vector", lambda h: h.tensor_mul(out=v(rf), in0=v(rf), in1=v(stp)))
    p.emit("vector", lambda h: h.tensor_mul(out=v(rf), in0=v(rf), in1=v(sgn)),
           wait=[("SS", p.c("SS"))])
    p.emit("vector", lambda h: h.tensor_tensor(
        out=dst_ap[r, :].rearrange("p (a b) -> p a b", b=32),
        in0=rf[r, :].rearrange("p (a b) -> p a b", b=32),
        in1=sc[r, :].unsqueeze(2).broadcast_to((nrows, NB, 32)),
        op=ALU.mult), inc="SV")


def build_quant():
    p = Ph("quant")
    NB = DM // 32
    xq_in = p.dram("xq_in", (SLAB, DM))
    wrows = p.dram("wrows", (128, DM))
    dact = p.dram("dact", (1, 1))
    dw = p.dram("dw", (128, NB))
    onesr = p.dram("onesr", (1, 128))
    xq = p.dram("xq", (SLAB, DM), BF16, kind="ExternalOutput")
    wq = p.dram("wq", (128, DM), BF16, kind="ExternalOutput")

    xt = p.sb("xt", (128, DM))
    xo = p.sb("xo", (128, DM), BF16)
    wt = p.sb("wt", (128, DM))
    wo = p.sb("wo", (128, DM), BF16)
    dac = p.sb("dac", (1, 1))
    onr = p.sb("onr", (1, 128))
    daccol = p.sb("daccol", (128, 1))
    dacb = p.sb("dacb", (128, NB))
    dwt = p.sb("dwt", (128, NB))
    dwc = p.sb("dwc", (128, NB))
    pdb = p.ps("pdb", (128, 1))

    p.dma("sync", dac[:], dact[:], inc="LD")
    p.dma("sync", dwt[:], dw[:], inc="LD")
    p.dma("sync", onr[:], onesr[:], inc="LD")
    p.emit("scalar", lambda h: h.activation(out=dac[:], in_=dac[:], func=AF.Tanh, scale=4.0),
           wait=[("LD", p.c("LD"))])
    p.emit("vector", lambda h: h.tensor_scalar(out=dac[:], in0=dac[:], scalar1=-0.5,
                                               scalar2=2.0, op0=ALU.mult, op1=ALU.add),
           wait=[("SS", p.c("SS"))])
    p.emit("tensor", lambda h: h.matmul(out=pdb[:], lhsT=onr[:], rhs=dac[:],
                                        start=True, stop=True),
           wait=[("SV", p.c("SV"))], inc="ST")
    p.emit("vector", lambda h: h.tensor_copy(out=daccol[:], in_=pdb[:]),
           wait=[("ST", p.c("ST"))], inc="SV")
    p.emit("vector", lambda h: h.memset(dacb[:], 0.0))
    p.emit("vector", lambda h: h.tensor_scalar_add(out=dacb[:], in0=dacb[:],
                                                   scalar1=daccol[:]),
           wait=[("SV", p.c("SV"))], inc="SV")
    p.emit("scalar", lambda h: h.activation(out=dwc[:], in_=dwt[:], func=AF.Tanh, scale=4.0),
           inc="SS")
    p.emit("vector", lambda h: h.tensor_scalar(out=dwc[:], in0=dwc[:], scalar1=-0.5,
                                               scalar2=2.0, op0=ALU.mult, op1=ALU.add),
           wait=[("SS", p.c("SS"))], inc="SV")

    p.dma("sync", wt[:], wrows[:], inc="LD")
    p.wait("vector", ("LD", p.c("LD")), ("SV", p.c("SV")))
    p.wait("scalar", ("LD", p.c("LD")), ("SV", p.c("SV")))
    qs = quant_scratch(p)
    _quant_tile(p, wt, wo, 128, dwc, qs)
    p.dma("sync", wq[:], wo[:], wait=[("SV", p.c("SV"))], inc="SO")

    for i in range(SLAB // 128):
        p.dma("sync", xt[:], xq_in[128 * i:128 * (i + 1), :],
              wait=[("SV", p.c("SV")), ("SO", p.c("SO"))], inc="LD")
        p.wait("vector", ("LD", p.c("LD")))
        p.wait("scalar", ("LD", p.c("LD")))
        _quant_tile(p, xt, xo, 128, dacb, qs)
        p.dma("sync", xq[128 * i:128 * (i + 1), :], xo[:],
              wait=[("SV", p.c("SV"))], inc="SO")
    p.wait("sync", ("SO", p.c("SO")))
    return p


def build_ffnmm():
    p = Ph("ffnmm")
    xqT = p.dram("xqT", (DM, SLAB), BF16)
    wqT = p.dram("wqT", (DM, DM), BF16)
    fbias = p.dram("fbias", (1, DM))
    ffn = p.dram("ffn", (SLAB, DM), kind="ExternalOutput")
    NT = SLAB // 128
    xqs = p.sb("xqs", (128, 8 * SLAB), BF16)
    wqs = p.sb("wqs", (128, 8 * DM), BF16)
    br = p.sb("br", (1, DM), BF16)
    ones1 = p.sb("ones1", (1, 128), BF16)
    xo = [p.sb(f"xo{i}", (128, DM)) for i in range(2)]
    pss = [p.ps(f"ps{i}", (128, DM)) for i in range(2)]
    p.dma("sync", xqs[:].rearrange("p (c n) -> p c n", c=8), xqT.rearrange("(c p) n -> p c n", p=128), inc="LD")
    p.dma("sync", wqs[:].rearrange("p (c n) -> p c n", c=8), wqT.rearrange("(c p) n -> p c n", p=128), inc="LD")
    p.dma("gpsimd", br[:], fbias[:], inc="GD")
    p.emit("vector", lambda h: h.memset(ones1[:], 1.0))
    ld0 = p.c("LD")
    gd0 = p.c("GD")
    sv0 = p.c("SV")
    evac_sv = {}
    for t in range(NT):
        pb = t % 2
        w = [("LD", ld0), ("GD", gd0), ("SV", sv0)]
        if t >= 2:
            w.append(("SV", evac_sv[t - 2]))
        for oc in range(2):
            for k in range(8):
                p.emit("tensor", lambda h, t=t, pb=pb, oc=oc, k=k: h.matmul(
                    out=pss[pb][:, 512 * oc:512 * (oc + 1)],
                    lhsT=xqs[:, k * SLAB + t * 128:k * SLAB + (t + 1) * 128],
                    rhs=wqs[:, k * DM + 512 * oc:k * DM + 512 * (oc + 1)],
                    start=(k == 0), stop=False), wait=(w if k == 0 and oc == 0 else ()))
            p.emit("tensor", lambda h, pb=pb, oc=oc: h.matmul(
                out=pss[pb][:, 512 * oc:512 * (oc + 1)], lhsT=ones1[:],
                rhs=br[:, 512 * oc:512 * (oc + 1)], start=False, stop=True),
                inc="ST")
        st = p.c("ST")
        wev = [("ST", st)] + ([("SO", 16 * (t - 1))] if t >= 2 else [])
        evac_sv[t] = p.emit("vector", lambda h, pb=pb: h.tensor_copy(out=xo[pb][:],
                                                                     in_=pss[pb][:]),
                            wait=wev)
        p.dma("sync", ffn[128 * t:128 * (t + 1), :], xo[pb][:],
              wait=[("SV", evac_sv[t])], inc="SO")
    p.wait("sync", ("SO", p.c("SO")))
    return p


def build_resid():
    p = Ph("resid")
    yfm = p.dram("yfm", (DM, SLAB), BF16)
    owT = p.dram("owT", (DM, DM))
    ob = p.dram("ob", (1, DM))
    xs = p.dram("xs", (SLAB, DM))
    ffn = p.dram("ffn", (SLAB, DM))
    x1 = p.dram("x1", (SLAB, DM), kind="ExternalOutput")
    NT = SLAB // 128
    ys = p.sb("ys", (128, 8 * SLAB), BF16)
    ws = p.sb("ws", (128, 8 * DM), BF16)
    br = p.sb("br", (1, DM), BF16)
    ones1 = p.sb("ones1", (1, 128), BF16)
    xt = [p.sb(f"xt{i}", (128, DM)) for i in range(2)]
    ft = [p.sb(f"ft{i}", (128, DM)) for i in range(2)]
    xo = [p.sb(f"xo{i}", (128, DM)) for i in range(2)]
    pss = [p.ps(f"ps{i}", (128, DM)) for i in range(2)]
    p.dma("sync", ys[:].rearrange("p (c n) -> p c n", c=8), yfm.rearrange("(c p) n -> p c n", p=128), inc="LD")
    p.dma("gpsimd", ws[:].rearrange("p (c n) -> p c n", c=8), owT.rearrange("(c p) n -> p c n", p=128), inc="GD")
    p.dma("gpsimd", br[:], ob[:], inc="GD")
    p.emit("vector", lambda h: h.memset(ones1[:], 1.0))
    ld0 = p.c("LD")
    gd0 = p.c("GD")
    sv0 = p.c("SV")
    for t in range(NT):
        pb = t % 2
        wso = [("SO", 16 * (t - 1))] if t >= 2 else []
        p.dma("sync", xt[pb][:], xs[128 * t:128 * (t + 1), :], wait=wso, inc="LD")
        p.dma("sync", ft[pb][:], ffn[128 * t:128 * (t + 1), :], inc="LD")
        ldt = p.c("LD")
        w = [("LD", ld0), ("GD", gd0), ("SV", sv0)]
        for oc in range(2):
            for k in range(8):
                p.emit("tensor", lambda h, t=t, pb=pb, oc=oc, k=k: h.matmul(
                    out=pss[pb][:, 512 * oc:512 * (oc + 1)],
                    lhsT=ys[:, k * SLAB + t * 128:k * SLAB + (t + 1) * 128],
                    rhs=ws[:, k * DM + 512 * oc:k * DM + 512 * (oc + 1)],
                    start=(k == 0), stop=False),
                    wait=(w + ([("SV", p.c("SV"))] if t >= 2 else [])
                          if k == 0 and oc == 0 else ()))
            p.emit("tensor", lambda h, pb=pb, oc=oc: h.matmul(
                out=pss[pb][:, 512 * oc:512 * (oc + 1)], lhsT=ones1[:],
                rhs=br[:, 512 * oc:512 * (oc + 1)], start=False, stop=True),
                inc="ST")
        st = p.c("ST")
        p.emit("vector", lambda h, pb=pb: h.tensor_add(out=xo[pb][:], in0=pss[pb][:],
                                                       in1=xt[pb][:]),
               wait=[("ST", st), ("LD", ldt)])
        p.emit("vector", lambda h, pb=pb: h.tensor_add(out=xo[pb][:], in0=xo[pb][:],
                                                       in1=ft[pb][:]))
        p.dma("sync", x1[128 * t:128 * (t + 1), :], xo[pb][:],
              wait=[("SV", p.c("SV"))], inc="SO")
    p.wait("sync", ("SO", p.c("SO")))
    return p


def build_ln1():
    p = Ph("ln1")
    xs = p.dram("xs", (SLAB, DM))
    wr = p.dram("wr", (DM, 16))
    xn = p.dram("xn", (SLAB, DM), kind="ExternalOutput")
    rpart = p.dram("rpart", (1, 16), kind="ExternalOutput")
    NT = SLAB // 128
    xt = [p.sb(f"xt{i}", (128, DM)) for i in range(2)]
    xo = [p.sb(f"xo{i}", (128, DM)) for i in range(2)]
    xob = [p.sb(f"xob{i}", (128, DM), BF16) for i in range(2)]
    sq = p.sb("sq", (128, DM))
    ssum = p.sb("ssum", (128, 1))
    msq = p.sb("msq", (128, 1))
    mean = p.sb("mean", (128, 1))
    var = p.sb("var", (128, 1))
    rstd = p.sb("rstd", (128, 1))
    eps = p.sb("eps", (128, 1))
    onesc = p.sb("onesc", (128, 1), BF16)
    wrs = p.sb("wrs", (128, 8 * 16), BF16)
    csum = p.sb("csum", (128, 8), BF16)
    csumf = p.sb("csumf", (128, 8))
    rp = p.sb("rp", (1, 16))
    psc = p.ps("psc", (128, 8))
    psr = p.ps("psr", (1, 16))
    p.emit("vector", lambda h: h.memset(eps[:], 1e-5))
    p.emit("vector", lambda h: h.memset(onesc[:], 1.0))
    p.emit("vector", lambda h: h.memset(csumf[:], 0.0))
    p.dma("gpsimd", wrs[:].rearrange("p (c n) -> p c n", c=8), wr.rearrange("(c p) n -> p c n", p=128), inc="GD")
    for i in range(NT):
        b = i % 2
        w = [("SO", 16 * (i - 1))] if i >= 2 else []
        p.dma("sync", xt[b][:], xs[128 * i:128 * (i + 1), :], wait=w, inc="LD")
        ld = p.c("LD")
        p.emit("vector", lambda h, b=b: h.tensor_mul(out=sq[:], in0=xt[b][:], in1=xt[b][:]),
               wait=[("LD", ld), ("SS", p.c("SS"))])
        p.emit("vector", lambda h: h.tensor_reduce(out=ssum[:], in_=sq[:],
                                                   axis=AX.X, op=ALU.add,
                                                   opt_input=False, opt_output=False))
        p.emit("vector", lambda h, b=b: h.tensor_reduce(out=mean[:], in_=xt[b][:],
                                                        axis=AX.X, op=ALU.add,
                                                        opt_input=False, opt_output=False))
        p.emit("vector", lambda h: h.tensor_scalar_mul(out=mean[:], in0=mean[:],
                                                       scalar1=1.0 / DM))
        p.emit("vector", lambda h: h.tensor_mul(out=msq[:], in0=mean[:], in1=mean[:]))
        p.emit("vector", lambda h: h.scalar_tensor_tensor(out=var[:], in0=ssum[:],
                                                          scalar=1.0 / DM, in1=msq[:],
                                                          op0=ALU.mult, op1=ALU.subtract))
        p.emit("scalar", lambda h: h.activation(out=var[:], in_=var[:], func=AF.Ln,
                                                bias=eps[:]),
               wait=[("SV", p.c("SV"))])
        p.emit("scalar", lambda h: h.activation(out=rstd[:], in_=var[:], func=AF.Exp,
                                                scale=-0.5))
        p.emit("vector", lambda h, b=b: h.tensor_scalar(out=xo[b][:], in0=xt[b][:],
                                                        scalar1=mean[:], scalar2=rstd[:],
                                                        op0=ALU.subtract, op1=ALU.mult),
               wait=[("SS", p.c("SS")), ("ST", p.c("ST"))])
        p.emit("vector", lambda h, b=b: h.tensor_copy(out=xob[b][:], in_=xo[b][:]))
        svx = p.c("SV")
        p.dma("sync", xn[128 * i:128 * (i + 1), :], xo[b][:],
              wait=[("SV", svx)], inc="SO")
        for k in range(8):
            p.emit("tensor", lambda h, b=b, k=k: h.matmul(
                out=psc[:, k:k + 1], lhsT=xob[b][:, 128 * k:128 * (k + 1)],
                rhs=onesc[:], start=True, stop=True),
                wait=([("SV", svx)] if k == 0 else ()),
                inc=("ST" if k == 7 else None))
        p.emit("vector", lambda h: h.tensor_add(out=csumf[:], in0=csumf[:],
                                                in1=psc[:]),
               wait=[("ST", p.c("ST"))])
    p.emit("vector", lambda h: h.tensor_copy(out=csum[:], in_=csumf[:]))
    for k in range(8):
        p.emit("tensor", lambda h, k=k: h.matmul(
            out=psr[:], lhsT=csum[:, k:k + 1], rhs=wrs[:, 16 * k:16 * (k + 1)],
            start=(k == 0), stop=(k == 7)),
            wait=([("SV", p.c("SV")), ("GD", p.c("GD"))] if k == 0 else ()),
            inc=("ST" if k == 7 else None))
    p.emit("vector", lambda h: h.tensor_copy(out=rp[:], in_=psr[:]),
           wait=[("ST", p.c("ST"))])
    p.dma("sync", rpart[:], rp[:], wait=[("SV", p.c("SV"))], inc="SO")
    p.wait("sync", ("SO", p.c("SO")))
    return p


def build_final():
    p = Ph("final")
    x1 = p.dram("x1", (SLAB, DM))
    at1 = p.dram("at1", (SLAB, DM))
    hT = p.dram("hT", (256, SLAB), BF16)
    wo = p.dram("wo", (256, DM))
    bo = p.dram("bo", (1, DM))
    out = p.dram("out", (SLAB, DM), kind="ExternalOutput")
    NT = SLAB // 128
    hs = p.sb("hs", (128, 2 * SLAB), BF16)
    ws = p.sb("ws", (128, 2 * DM), BF16)
    br = p.sb("br", (1, DM), BF16)
    ones1 = p.sb("ones1", (1, 128), BF16)
    xt = [p.sb(f"xt{i}", (128, DM)) for i in range(2)]
    at = [p.sb(f"att{i}", (128, DM)) for i in range(2)]
    xo = [p.sb(f"xo{i}", (128, DM)) for i in range(2)]
    pss = [p.ps(f"ps{i}", (128, DM)) for i in range(2)]
    p.dma("sync", hs[:].rearrange("p (c n) -> p c n", c=2), hT.rearrange("(c p) n -> p c n", p=128), inc="LD")
    p.dma("gpsimd", ws[:].rearrange("p (c n) -> p c n", c=2), wo.rearrange("(c p) n -> p c n", p=128), inc="GD")
    p.dma("gpsimd", br[:], bo[:], inc="GD")
    p.emit("vector", lambda h: h.memset(ones1[:], 1.0))
    ld0 = p.c("LD")
    gd0 = p.c("GD")
    sv0 = p.c("SV")
    for t in range(NT):
        pb = t % 2
        wso = [("SO", 16 * (t - 1))] if t >= 2 else []
        p.dma("sync", xt[pb][:], x1[128 * t:128 * (t + 1), :], wait=wso, inc="LD")
        p.dma("sync", at[pb][:], at1[128 * t:128 * (t + 1), :], inc="LD")
        ldt = p.c("LD")
        w = [("LD", ld0), ("GD", gd0), ("SV", sv0)]
        for oc in range(2):
            for k in range(2):
                p.emit("tensor", lambda h, t=t, pb=pb, oc=oc, k=k: h.matmul(
                    out=pss[pb][:, 512 * oc:512 * (oc + 1)],
                    lhsT=hs[:, k * SLAB + t * 128:k * SLAB + (t + 1) * 128],
                    rhs=ws[:, k * DM + 512 * oc:k * DM + 512 * (oc + 1)],
                    start=(k == 0), stop=False),
                    wait=(w + ([("SV", p.c("SV"))] if t >= 2 else [])
                          if k == 0 and oc == 0 else ()))
            p.emit("tensor", lambda h, pb=pb, oc=oc: h.matmul(
                out=pss[pb][:, 512 * oc:512 * (oc + 1)], lhsT=ones1[:],
                rhs=br[:, 512 * oc:512 * (oc + 1)], start=False, stop=True),
                inc="ST")
        st = p.c("ST")
        p.emit("vector", lambda h, pb=pb: h.tensor_add(out=xo[pb][:], in0=pss[pb][:],
                                                       in1=xt[pb][:]),
               wait=[("ST", st), ("LD", ldt)])
        p.emit("vector", lambda h, pb=pb: h.tensor_add(out=xo[pb][:], in0=xo[pb][:],
                                                       in1=at[pb][:]))
        p.dma("sync", out[128 * t:128 * (t + 1), :], xo[pb][:],
              wait=[("SV", p.c("SV"))], inc="SO")
    p.wait("sync", ("SO", p.c("SO")))
    return p


def build_l0attn():
    p = Ph("l0attn")
    TT = T
    xnT = p.dram("xnT", (DM, TT))
    wqkvT = p.dram("wqkvT", (DM, 768))
    bqkv = p.dram("bqkv", (768, 1))
    bqkvr = p.dram("bqkvr", (1, 768))
    masks = p.dram("masks", (4, 128, 512))
    iden64 = p.dram("iden64", (64, 64))
    y = p.dram("y", (256, TT), BF16, kind="ExternalOutput")

    xns = p.sb("xns", (128, 8 * TT), BF16)
    wqs = p.sb("wqs", (128, 8 * 768), BF16)
    bcol = p.sb("bcol", (128, 6))
    q_sb = p.sb("q_sb", (128, TT), BF16)
    k_sb = p.sb("k_sb", (128, TT), BF16)
    v_sb = p.sb("v_sb", (128, 64 * 64), BF16)
    vT_sb = p.sb("vT_sb", (64, 1024), BF16)
    idn64 = p.sb("idn64", (64, 64), BF16)
    msk = p.sb("msk", (128, 4 * 512))
    s_sb = [p.sb(f"s_sb{i}", (128, 512), BF16) for i in range(4)]
    o_sb = p.sb("o_sb", (64, 512))
    o_bf = p.sb("o_bf", (64, 512), BF16)
    ones1 = p.sb("ones1", (1, 128), BF16)
    brow = p.sb("brow", (1, 768), BF16)
    ps_s = [p.ps(f"ps_s{i}", (128, 512)) for i in range(4)]
    ps_oA = p.ps("ps_oA", (128, 512))
    ps_oB = p.ps("ps_oB", (128, 512))
    ps_p = p.ps("ps_p", (128, 512))
    ps_v = p.ps("ps_v", (128, 64), BF16)

    p.dma("gpsimd", xns[:].rearrange("p (c n) -> p c n", c=8),
          xnT.rearrange("(c p) n -> p c n", p=128), inc="GD")
    p.dma("gpsimd", wqs[:].rearrange("p (c n) -> p c n", c=8),
          wqkvT.rearrange("(c p) n -> p c n", p=128), inc="GD")
    p.dma("sync", msk[:].rearrange("p (c n) -> p c n", c=4),
          masks.rearrange("c p n -> p c n"), inc="LD")
    p.dma("gpsimd", brow[:], bqkvr[:], inc="GD")
    p.dma("gpsimd", idn64[:], iden64[:], inc="GD")
    p.emit("vector", lambda h: h.memset(ones1[:], 1.0))
    gd0, ld0 = p.c("GD"), p.c("LD")

    NSTRIP = TT // 512
    for hh in range(4):
        for j, off in enumerate((hh * 64, 256 + hh * 64, 512 + hh * 64)):
            p.dma("sync", bcol[0:64, j:j + 1], bqkv[off:off + 64, :], inc="LD")
        ldb = p.c("LD")
        for s in range(NSTRIP):
            for k in range(8):
                p.emit("tensor", lambda h, s=s, k=k, hh=hh: h.matmul(
                    out=ps_p[0:64, :],
                    lhsT=wqs[:, k * 768 + hh * 64:k * 768 + hh * 64 + 64],
                    rhs=xns[:, k * TT + 512 * s:k * TT + 512 * (s + 1)],
                    start=(k == 0), stop=(k == 7)),
                    wait=([("GD", gd0), ("LD", ldb), ("SV", p.c("SV"))]
                          if k == 0 else ()), inc=("ST" if k == 7 else None))
            p.emit("vector", lambda h, s=s: h.tensor_scalar_add(
                out=q_sb[0:64, 512 * s:512 * (s + 1)], in0=ps_p[0:64, :],
                scalar1=bcol[0:64, 0:1]), wait=[("ST", p.c("ST"))])
            p.emit("vector", lambda h, s=s: h.tensor_copy(
                out=q_sb[64:128, 512 * s:512 * (s + 1)],
                in_=q_sb[0:64, 512 * s:512 * (s + 1)]))
            for k in range(8):
                p.emit("tensor", lambda h, s=s, k=k, hh=hh: h.matmul(
                    out=ps_p[0:64, :],
                    lhsT=wqs[:, k * 768 + 256 + hh * 64:k * 768 + 256 + hh * 64 + 64],
                    rhs=xns[:, k * TT + 512 * s:k * TT + 512 * (s + 1)],
                    start=(k == 0), stop=(k == 7)),
                    wait=([("SV", p.c("SV"))] if k == 0 else ()),
                    inc=("ST" if k == 7 else None))
            p.emit("vector", lambda h, s=s: h.tensor_scalar(
                out=k_sb[0:64, 512 * s:512 * (s + 1)], in0=ps_p[0:64, :],
                scalar1=bcol[0:64, 1:2], scalar2=0.125,
                op0=ALU.add, op1=ALU.mult), wait=[("ST", p.c("ST"))])
            p.emit("vector", lambda h, s=s: h.tensor_copy(
                out=k_sb[64:128, 512 * s:512 * (s + 1)],
                in_=k_sb[0:64, 512 * s:512 * (s + 1)]))
        for s in range(NSTRIP):
            db = 512 * (s % 2)
            for k in range(8):
                p.emit("tensor", lambda h, s=s, k=k, hh=hh: h.matmul(
                    out=ps_p[0:64, :],
                    lhsT=wqs[:, k * 768 + 512 + hh * 64:k * 768 + 512 + hh * 64 + 64],
                    rhs=xns[:, k * TT + 512 * s:k * TT + 512 * (s + 1)],
                    start=(k == 0), stop=(k == 7)),
                    wait=([("SV", p.c("SV"))] if k == 0 else ()),
                    inc=("ST" if k == 7 else None))
            p.emit("vector", lambda h, s=s, db=db: h.tensor_scalar_add(
                out=vT_sb[0:64, db:db + 512], in0=ps_p[0:64, :],
                scalar1=bcol[0:64, 2:3]), wait=[("ST", p.c("ST"))])
            for tt in range(4):
                t = 4 * s + tt
                p.emit("tensor", lambda h, tt=tt, db=db: h.transpose(
                    out=ps_v[:, 0:64], in_=vT_sb[0:64, db + 128 * tt:db + 128 * (tt + 1)],
                    identity=idn64[0:64, 0:64]),
                    wait=[("SV", p.c("SV"))], inc="ST")
                p.emit("vector", lambda h, t=t: h.tensor_copy(
                    out=v_sb[:, 64 * t:64 * (t + 1)], in_=ps_v[:, 0:64]),
                    wait=[("ST", p.c("ST"))])
        for s in range(NSTRIP):
            nkb = 4 * s + 4
            evac_cnt = {}
            evac_eng = {}
            for step in range(nkb + 3):
                kb = step
                if kb < nkb:
                    buf = kb % 4
                    wv = []
                    if kb >= 4:
                        e4, c4 = evac_eng[kb - 4], evac_cnt[kb - 4]
                        wv.append(("SV" if e4 == "vector" else "SS", c4))
                    if kb == 0:
                        wv += [("SV", p.c("SV")), ("SS", p.c("SS"))]
                    halfS = kb % 2
                    p.emit("tensor", lambda h, kb=kb, s=s, buf=buf, halfS=halfS: h.matmul(
                        out=ps_s[buf][0:128, :],
                        lhsT=k_sb[64 * halfS:64 * (halfS + 1), 128 * kb:128 * (kb + 1)],
                        rhs=q_sb[64 * halfS:64 * (halfS + 1), 512 * s:512 * (s + 1)],
                        start=True, stop=True, tile_position=(64 * halfS, 0)),
                        wait=wv, inc="ST")
                    st = p.c("ST")
                    rel = kb - 4 * s
                    if rel >= 0:
                        evac_cnt[kb] = p.emit("vector", lambda h, buf=buf, rel=rel: h.tensor_mul(
                            out=s_sb[buf][:], in0=ps_s[buf][:],
                            in1=msk[:, 512 * rel:512 * (rel + 1)]),
                            wait=[("ST", st)])
                        evac_eng[kb] = "vector"
                    elif kb % 2 == 0:
                        evac_cnt[kb] = p.emit("vector", lambda h, buf=buf: h.tensor_copy(
                            out=s_sb[buf][:], in_=ps_s[buf][:]),
                            wait=[("ST", st)])
                        evac_eng[kb] = "vector"
                    else:
                        evac_cnt[kb] = p.emit("scalar", lambda h, buf=buf: h.copy(
                            out=s_sb[buf][:], in_=ps_s[buf][:]),
                            wait=[("ST", st)])
                        evac_eng[kb] = "scalar"
                kbav = step - 3
                if 0 <= kbav < nkb:
                    half = kbav % 2
                    po = ps_oA if half == 0 else ps_oB
                    e, cnt = evac_eng[kbav], evac_cnt[kbav]
                    p.emit("tensor", lambda h, kbav=kbav, half=half, po=po, nkb=nkb: h.matmul(
                        out=po[64 * half:64 * (half + 1), :],
                        lhsT=v_sb[:, 64 * kbav:64 * (kbav + 1)],
                        rhs=s_sb[kbav % 4][:],
                        start=(kbav < 2), stop=(kbav >= nkb - 2),
                        tile_position=(0, 64 * half)),
                        wait=[("SV" if e == "vector" else "SS", cnt)], inc="ST")
            st = p.c("ST")
            p.emit("vector", lambda h: h.tensor_copy(out=o_sb[:], in_=ps_oB[64:128, :]),
                   wait=[("ST", st)])
            p.emit("vector", lambda h: h.tensor_add(out=o_bf[:], in0=ps_oA[0:64, :],
                                                    in1=o_sb[:]))
            p.dma("sync", y[64 * hh:64 * (hh + 1), 512 * s:512 * (s + 1)], o_bf[:],
                  wait=[("SV", p.c("SV"))], inc="SO")
    p.wait("sync", ("SO", p.c("SO")))
    return p


def _xw(p, *sems):
    return [(s, p.c(s)) for s in sems]


def _scan16(p, out_t, data1_t, ones_t, wait0):
    for pc in range(16):
        p.emit("vector", lambda h, pc=pc: h.tensor_tensor_scan(
            out=out_t[:, 128 * pc:128 * (pc + 1)],
            data0=ones_t[:], data1=data1_t[:, 128 * pc:128 * (pc + 1)],
            initial=0.0, op0=ALU.mult, op1=ALU.add),
            wait=(wait0 if pc == 0 else ()))


def build_l1(recon):
    p = Ph("l1rec" if recon else "l1a")
    xn1T = p.dram("xn1T", (DM, SLAB))
    winxiT = p.dram("winxiT", (DM, DI))
    wlT = p.dram("wlT", (DI, DI))
    wvT = p.dram("wvT", (DI, DI))
    omc = p.dram("omc", (128, 16))
    blc = p.dram("blc", (128, 16))
    bvc = p.dram("bvc", (128, 16))
    m0col = p.dram("m0col", (128, 1))
    consts = p.dram("consts", (9, 128, 128))
    if recon:
        carries = p.dram("carries", (NCH_CORE, 128, 16))
        winzT = p.dram("winzT", (DM, DI))
        woTq = p.dram("woTq", (DI, DM))
        at1 = p.dram("at1", (SLAB, DM), kind="ExternalOutput")
    else:
        aGt_d = p.dram("aGt", (NCH_CORE, 128, DI), BF16, kind="ExternalOutput")
        bfm_d = p.dram("bfm", (NCH_CORE, 128, DI), BF16, kind="ExternalOutput")
        TU_d = p.dram("TU", (NCH_CORE, 128, 128), kind="ExternalOutput")
        Gd_d = p.dram("Gd", (NCH_CORE, 128, 16), kind="ExternalOutput")
        dv_d = p.dram("dv", (NCH_CORE, 128, 16), kind="ExternalOutput")
        a_d = p.dram("a_d", (NCH_CORE, 128, DI), BF16, kind="ExternalOutput")
        E_d = p.dram("E_d", (NCH_CORE, 128, DI), BF16, kind="ExternalOutput")
        LI_d = p.dram("LI_d", (NCH_CORE, 128, DI), BF16, kind="ExternalOutput")
        mc_d = p.dram("mc_d", (NCH_CORE, 128, 1), kind="ExternalOutput")
    Ldram = p.dram("Lscr", (NCH_CORE, 128, DI), BF16, kind="Internal")
    vdram = p.dram("vscr", (NCH_CORE, 128, DI), BF16, kind="Internal")

    xnb = p.sb("xnb", (128, 8 * 512), BF16)
    sst4 = p.sb("sst4", (128, 4 * DI), BF16)
    wbuf = p.sb("wbuf", (128, 16 * DI), BF16)
    xi = p.sb("xi", (128, NCH_CORE * DI), BF16)
    cst = p.sb("cst", (128, 9 * 128))
    cstb = p.sb("cstb", (128, 128), BF16)
    som = p.sb("som", (128, 16))
    blc_s = p.sb("blc_s", (128, 16))
    bvc_s = p.sb("bvc_s", (128, 16))
    m0 = p.sb("m0", (128, 1))
    Lc = p.sb("Lc", (128, DI), BF16)
    CumI = sst4[:].bitcast(F32)[:, 0:DI]
    Ef = sst4[:].bitcast(F32)[:, DI:2 * DI]
    tmpf = p.sb("tmpf", (128, DI))
    lamt = p.sb("lamt", (128, DI), BF16)
    LEb = p.sb("LEb", (128, DI), BF16)
    LEi = p.sb("LEi", (128, DI), BF16)
    vcb = p.sb("vcb", (128, DI), BF16)
    vnb = p.sb("vnb", (128, DI), BF16)
    a_b = p.sb("a_b", (128, DI), BF16)
    b_b = p.sb("b_b", (128, DI), BF16)
    LIi = LEi
    e_b = lamt
    Gf = p.sb("Gf", (128, 16))
    rnb = p.sb("rnb", (128, 128))
    rnr = p.sb("rnr", (1, 128))
    onesc = p.sb("onesc", (128, 1))
    mrow = p.sb("mrow", (1, 128))
    mcol = p.sb("mcol", (128, 1))
    blocf = p.sb("blocf", (128, 1))
    blocb = p.sb("blocb", (128, 1), BF16)
    names = ("SL SU PL PU Q2L Q2U Q4L Q4U TLa TUa TLb TUb MLa MUa MLb MUb "
             "CLt CUt").split()
    hier = {n: p.sb(n, (128, 128)) for n in names}
    TUf = tmpf[:, 0:128]
    if recon:
        g0 = p.sb("g0", (128, 16))
        g0b = p.sb("g0b", (128, 16), BF16)
        qc = p.sb("qc", (128, 1))
        brow = p.sb("brow", (1, 128))
        brb = p.sb("brb", (128, 128))
        CumW = tmpf
        zb = vnb
        xoA = tmpf
        Wt = a_b
        ub = zb
    else:
        aG = vnb
        aGtk = vcb
        dvf = p.sb("dvf", (128, 16))
    psA = p.ps("psA", (128, 512))
    psB = p.ps("psB", (128, 128))
    psC = p.ps("psC", (128, 128))
    psR = p.ps("psR", (1, 128))
    psT = p.ps("psT", (128, 128), BF16)
    psM = p.ps("psM", (128, 16))

    CS = lambda i: cst[:, 128 * i:128 * (i + 1)]
    STRIL, IDENT, M8, D16, D32, D64, D128, ONES, STRIU = range(9)

    p.dma("sync", cst[:].rearrange("p (c n) -> p c n", c=9),
          consts.rearrange("c p n -> p c n"), inc="LD")
    p.dma("sync", som[:], omc[:], inc="LD")
    p.dma("sync", blc_s[:], blc[:], inc="LD")
    p.dma("sync", bvc_s[:], bvc[:], inc="LD")
    p.dma("sync", m0[:], m0col[:], inc="LD")
    p.dma("gpsimd", wbuf[:, :8 * DI].rearrange("p (c n) -> p c n", c=8),
          winxiT.rearrange("(c p) n -> p c n", p=128), inc="GD")
    p.emit("vector", lambda h: h.memset(onesc[:], 1.0))
    p.emit("vector", lambda h: h.tensor_copy(out=cstb[:], in_=CS(IDENT)),
           wait=_xw(p, "LD"))
    sp1 = p.sb("sp1", (128, 16))
    sp2 = p.sb("sp2", (128, 16))
    p.emit("scalar", lambda h: h.activation(out=sp1[:], in_=som[:], func=AF.Relu),
           wait=_xw(p, "LD"))
    p.emit("scalar", lambda h: h.activation(out=sp2[:], in_=som[:], func=AF.Abs))
    p.emit("scalar", lambda h: h.activation(out=sp2[:], in_=sp2[:], func=AF.Exp,
                                            scale=-1.0))
    p.emit("scalar", lambda h: h.activation(out=sp2[:], in_=sp2[:], func=AF.Ln,
                                            bias=1.0))
    p.emit("vector", lambda h: h.tensor_add(out=som[:], in0=sp1[:], in1=sp2[:]),
           wait=_xw(p, "SS"))
    p.emit("vector", lambda h: h.tensor_scalar_mul(out=som[:], in0=som[:],
                                                   scalar1=-8.0))

    for sc in range(NCH_CORE // 4):
        p.dma("gpsimd", xnb[:].rearrange("p (c n) -> p c n", c=8),
              xn1T.rearrange("(c p) n -> p c n", p=128)[:, :, 512 * sc:512 * (sc + 1)],
              wait=[("ST", p.c("ST"))], inc="GD")
        for es in range(16):
            for k in range(8):
                p.emit("tensor", lambda h, es=es, k=k: h.matmul(
                    out=psA[:],
                    lhsT=wbuf[:, k * DI + 128 * es:k * DI + 128 * (es + 1)],
                    rhs=xnb[:, k * 512:(k + 1) * 512],
                    start=(k == 0), stop=(k == 7)),
                    wait=(_xw(p, "GD", "LD", "SV") if k == 0 else ()),
                    inc=("ST" if k == 7 else None))
            p.emit("vector", lambda h, sc=sc, es=es: h.tensor_copy(
                out=xi[:].rearrange("p (c n) -> p c n", c=NCH_CORE)
                [:, 4 * sc:4 * sc + 4, 128 * es:128 * (es + 1)],
                in_=psA[:].rearrange("p (c n) -> p c n", c=4)),
                wait=[("ST", p.c("ST"))])

    for dram_w, outdram, isL in ((wlT, Ldram, True), (wvT, vdram, False)):
        p.dma("gpsimd", wbuf[:].rearrange("p (c n) -> p c n", c=16),
              dram_w.rearrange("(c p) n -> p c n", p=128),
              wait=_xw(p, "SV", "ST"), inc="GD")
        for sc in range(NCH_CORE // 4):
            wso = [("SO", p.c("SO"))]
            for js in range(16):
                for k in range(16):
                    p.emit("tensor", lambda h, sc=sc, js=js, k=k: h.matmul(
                        out=psA[:],
                        lhsT=wbuf[:, k * DI + 128 * js:k * DI + 128 * (js + 1)],
                        rhs=xi[:].rearrange("p (c n) -> p c n", c=NCH_CORE)
                        [:, 4 * sc:4 * sc + 4, 128 * k:128 * (k + 1)],
                        start=(k == 0), stop=(k == 15)),
                        wait=(_xw(p, "GD", "SV") if k == 0 else ()),
                        inc=("ST" if k == 15 else None))
                st = p.c("ST")
                if isL:
                    p.emit("scalar", lambda h, js=js: h.activation(
                        out=tmpf[:, 0:512], in_=psA[:],
                        func=AF.Sigmoid, bias=blc_s[:, js:js + 1]),
                        wait=[("ST", st)])
                    p.emit("vector", lambda h, js=js: h.tensor_scalar_mul(
                        out=sst4[:].rearrange("p (c n) -> p c n", c=4)[:, :, 128 * js:128 * (js + 1)],
                        in0=tmpf[:, 0:512].rearrange("p (c n) -> p c n", c=4),
                        scalar1=som[:, js:js + 1]),
                        wait=_xw(p, "SS") + (wso if js == 0 else []))
                else:
                    p.emit("vector", lambda h, js=js: h.tensor_scalar_add(
                        out=sst4[:].rearrange("p (c n) -> p c n", c=4)[:, :, 128 * js:128 * (js + 1)],
                        in0=psA[:].rearrange("p (c n) -> p c n", c=4),
                        scalar1=bvc_s[:, js:js + 1]),
                        wait=[("ST", st)] + (wso if js == 0 else []))
            for j4 in range(4):
                tgt = (Ldram if isL else vdram)[4 * sc + j4]
                if isL:
                    p.dma("sync", tgt, sst4[:, 2048 * j4:2048 * (j4 + 1)],
                          wait=_xw(p, "SV"), inc="SO")
                else:
                    p.dma("gpsimd", tgt, sst4[:, 2048 * j4:2048 * (j4 + 1)],
                          wait=_xw(p, "SV"), inc="GD")

    for ch in range(NCH_CORE):
        ch0 = (ch == 0)
        p.dma("sync", Lc[:], Ldram[ch], wait=_xw(p, "SV", "SS", "SO"), inc="LD")
        p.dma("sync", vcb[:], vdram[ch], inc="LD")
        if recon:
            p.dma("sync", g0[:], carries[ch], inc="LD")
        ldc = p.c("LD")
        p.wait("vector", ("LD", ldc))
        if ch0:
            p.emit("vector", lambda h: h.tensor_scalar_mul(
                out=Lc[:].rearrange("p (c n) -> p c n", c=16)[:, :, 0:1],
                in0=Lc[:].rearrange("p (c n) -> p c n", c=16)[:, :, 0:1], scalar1=m0[:]))
        _scan16(p, CumI, Lc, CS(ONES), [])
        p.emit("vector", lambda h: h.tensor_sub(out=tmpf[:], in0=CumI, in1=Lc[:]))
        p.emit("scalar", lambda h: h.activation(out=LEb[:], in_=tmpf[:], func=AF.Exp),
               wait=_xw(p, "SV"))
        p.emit("scalar", lambda h: h.activation(out=LEi[:], in_=tmpf[:], func=AF.Exp,
                                                scale=-1.0))
        p.emit("scalar", lambda h: h.activation(out=lamt[:], in_=Lc[:], func=AF.Exp))
        p.emit("scalar", lambda h: h.activation(
            out=Gf[:].unsqueeze(2),
            in_=CumI.rearrange("p (c n) -> p c n", c=16)[:, :, 127:128],
            func=AF.Exp))
        p.emit("scalar", lambda h: h.activation(out=tmpf[:], in_=vcb[:], func=AF.Square))
        for pc in range(16):
            p.emit("tensor", lambda h, pc=pc: h.matmul(
                out=psR[:], lhsT=onesc[:], rhs=tmpf[:, 128 * pc:128 * (pc + 1)],
                start=(pc == 0), stop=(pc == 15)),
                wait=(_xw(p, "SS", "SV") if pc == 0 else ()),
                inc=("ST" if pc == 15 else None))
        p.emit("vector", lambda h: h.tensor_copy(out=rnr[:], in_=psR[:]),
               wait=[("ST", p.c("ST"))])
        p.emit("scalar", lambda h: h.activation(out=rnr[:], in_=rnr[:], func=AF.Ln),
               wait=_xw(p, "SV"))
        p.emit("scalar", lambda h: h.activation(out=rnr[:], in_=rnr[:], func=AF.Exp,
                                                scale=-0.5))
        p.emit("tensor", lambda h: h.matmul(out=psC[:], lhsT=CS(ONES)[0:1, :],
                                            rhs=rnr[:], start=True, stop=True),
               wait=_xw(p, "SS"), inc="ST")
        p.emit("vector", lambda h: h.tensor_copy(out=rnb[:], in_=psC[:]),
               wait=[("ST", p.c("ST"))])
        p.emit("vector", lambda h: h.tensor_tensor(
            out=vnb[:].rearrange("p (c n) -> p c n", c=16),
            in0=vcb[:].rearrange("p (c n) -> p c n", c=16),
            in1=rnb[:].unsqueeze(1).broadcast_to((128, 16, 128)),
            op=ALU.mult))
        if ch0:
            p.emit("vector", lambda h: h.tensor_scalar_mul(
                out=vnb[:].rearrange("p (c n) -> p c n", c=16)[:, :, 0:1],
                in0=vnb[:].rearrange("p (c n) -> p c n", c=16)[:, :, 0:1], scalar1=m0[:]))
            p.emit("vector", lambda h: h.tensor_scalar_mul(
                out=lamt[:].rearrange("p (c n) -> p c n", c=16)[:, :, 0:1],
                in0=lamt[:].rearrange("p (c n) -> p c n", c=16)[:, :, 0:1], scalar1=m0[:]),
                wait=_xw(p, "SS"))
        p.emit("vector", lambda h: h.tensor_mul(out=a_b[:], in0=vnb[:], in1=LEi[:]),
               wait=_xw(p, "SS"))
        p.emit("vector", lambda h: h.tensor_mul(out=b_b[:], in0=vnb[:], in1=LEb[:]))
        p.emit("scalar", lambda h: h.activation(out=LIi[:], in_=CumI, func=AF.Exp,
                                                scale=-1.0), wait=_xw(p, "SV"))
        p.emit("vector", lambda h: h.tensor_scalar(out=lamt[:], in0=lamt[:],
                                                   scalar1=-1.0, scalar2=1.0,
                                                   op0=ALU.mult, op1=ALU.add))
        p.emit("vector", lambda h, ch=ch: h.tensor_mul(out=lamt[:], in0=lamt[:],
                                                       in1=xi[:, ch * DI:(ch + 1) * DI]))
        p.emit("vector", lambda h: h.tensor_mul(out=e_b[:], in0=lamt[:], in1=LIi[:]),
               wait=_xw(p, "SS"))
        _scan16(p, Ef, e_b, CS(ONES), [])
        p.emit("vector", lambda h: h.tensor_sub(out=tmpf[:], in0=Ef, in1=e_b[:]))
        p.emit("vector", lambda h: h.tensor_mul(out=tmpf[:], in0=b_b[:], in1=tmpf[:]))
        for pc in range(16):
            p.emit("tensor", lambda h, pc=pc: h.matmul(
                out=psR[:], lhsT=onesc[:], rhs=tmpf[:, 128 * pc:128 * (pc + 1)],
                start=(pc == 0), stop=(pc == 15)),
                wait=(_xw(p, "SV") if pc == 0 else ()),
                inc=("ST" if pc == 15 else None))
        p.emit("vector", lambda h: h.tensor_copy(out=mrow[:], in_=psR[:]),
               wait=[("ST", p.c("ST"))])
        for pc in range(16):
            p.emit("tensor", lambda h, pc=pc: h.matmul(
                out=psB[:], lhsT=b_b[:, 128 * pc:128 * (pc + 1)],
                rhs=a_b[:, 128 * pc:128 * (pc + 1)],
                start=(pc == 0), stop=(pc == 15)),
                wait=(_xw(p, "SV") if pc == 0 else ()),
                inc=("ST" if pc == 15 else None))
        st = p.c("ST")
        p.emit("vector", lambda h: h.scalar_tensor_tensor(
            out=hier["SL"][:], in0=psB[:], scalar=2.0, in1=CS(STRIL),
            op0=ALU.mult, op1=ALU.mult), wait=[("ST", st)])
        for pc in range(16):
            p.emit("tensor", lambda h, pc=pc: h.matmul(
                out=psC[:], lhsT=a_b[:, 128 * pc:128 * (pc + 1)],
                rhs=b_b[:, 128 * pc:128 * (pc + 1)],
                start=(pc == 0), stop=(pc == 15)),
                wait=(_xw(p, "SV") if pc == 0 else ()),
                inc=("ST" if pc == 15 else None))
        st = p.c("ST")
        p.emit("vector", lambda h: h.scalar_tensor_tensor(
            out=hier["SU"][:], in0=psC[:], scalar=2.0, in1=CS(STRIU),
            op0=ALU.mult, op1=ALU.mult), wait=[("ST", st)])

        def mmT(dst, lhsT_t, rhs_t, mask=None, add_t=None, sub=False):
            p.emit("tensor", lambda h: h.matmul(out=psB[:], lhsT=lhsT_t, rhs=rhs_t,
                                                start=True, stop=True),
                   wait=_xw(p, "SV"), inc="ST")
            st = p.c("ST")
            if add_t is None:
                if mask is None:
                    p.emit("vector", lambda h: h.tensor_copy(out=dst, in_=psB[:]),
                           wait=[("ST", st)])
                else:
                    p.emit("vector", lambda h: h.tensor_mul(out=dst, in0=psB[:],
                                                            in1=mask),
                           wait=[("ST", st)])
            elif sub:
                p.emit("vector", lambda h: h.tensor_sub(out=dst, in0=add_t,
                                                        in1=psB[:]),
                       wait=[("ST", st)])
            else:
                p.emit("vector", lambda h: h.tensor_add(out=dst, in0=add_t,
                                                        in1=psB[:]),
                       wait=[("ST", st)])

        H = hier
        p.emit("vector", lambda h: h.tensor_mul(out=H["PL"][:], in0=H["SL"][:], in1=CS(M8)))
        p.emit("vector", lambda h: h.tensor_mul(out=H["PU"][:], in0=H["SU"][:], in1=CS(M8)))
        mmT(H["Q2L"][:], H["PU"][:], H["PL"][:])
        mmT(H["Q2U"][:], H["PL"][:], H["PU"][:])
        p.emit("vector", lambda h: h.tensor_sub(out=H["TLa"][:], in0=cstb[:], in1=H["PL"][:]))
        p.emit("vector", lambda h: h.tensor_add(out=H["TLa"][:], in0=H["TLa"][:], in1=H["Q2L"][:]))
        mmT(H["TLa"][:], H["PU"][:], H["Q2L"][:], add_t=H["TLa"][:], sub=True)
        p.emit("vector", lambda h: h.tensor_sub(out=H["TUa"][:], in0=cstb[:], in1=H["PU"][:]))
        p.emit("vector", lambda h: h.tensor_add(out=H["TUa"][:], in0=H["TUa"][:], in1=H["Q2U"][:]))
        mmT(H["TUa"][:], H["Q2L"][:], H["PU"][:], add_t=H["TUa"][:], sub=True)
        mmT(H["Q4L"][:], H["Q2U"][:], H["Q2L"][:])
        mmT(H["Q4U"][:], H["Q2L"][:], H["Q2U"][:])
        mmT(H["TLb"][:], H["TUa"][:], H["Q4L"][:], add_t=H["TLa"][:])
        mmT(H["TUb"][:], H["Q4L"][:], H["TUa"][:], add_t=H["TUa"][:])
        p.emit("vector", lambda h: h.tensor_mul(out=H["TLb"][:], in0=H["TLb"][:], in1=CS(M8)))
        p.emit("vector", lambda h: h.tensor_mul(out=H["TUb"][:], in0=H["TUb"][:], in1=CS(M8)))
        mmT(H["MLa"][:], H["TUb"][:], H["SL"][:], add_t=H["TLb"][:])
        mmT(H["MUa"][:], H["SL"][:], H["TUb"][:], add_t=H["TUb"][:])
        TL_cur, TU_cur = H["TLb"], H["TUb"]
        ML_cur, MU_cur = H["MLa"], H["MUa"]
        ML_alt, MU_alt = H["MLb"], H["MUb"]
        TL_alt, TU_alt = H["TLa"], H["TUa"]
        for li, dmask in enumerate((D16, D32, D64, D128)):
            p.emit("vector", lambda h, ML_cur=ML_cur, dmask=dmask: h.tensor_mul(
                out=H["CLt"][:], in0=ML_cur[:], in1=CS(dmask)))
            p.emit("vector", lambda h, MU_cur=MU_cur, dmask=dmask: h.tensor_mul(
                out=H["CUt"][:], in0=MU_cur[:], in1=CS(dmask)))
            mmT(TL_alt[:], H["CUt"][:], TL_cur[:], add_t=TL_cur[:], sub=True)
            mmT(TU_alt[:], TL_cur[:], H["CUt"][:], add_t=TU_cur[:], sub=True)
            if li < 3:
                mmT(ML_alt[:], H["CUt"][:], ML_cur[:], add_t=ML_cur[:], sub=True)
                mmT(MU_alt[:], ML_cur[:], H["CUt"][:], add_t=MU_cur[:], sub=True)
                ML_cur, ML_alt = ML_alt, ML_cur
                MU_cur, MU_alt = MU_alt, MU_cur
            TL_cur, TL_alt = TL_alt, TL_cur
            TU_cur, TU_alt = TU_alt, TU_cur
        p.emit("vector", lambda h, TU_cur=TU_cur: h.tensor_copy(out=TUf,
                                                                in_=TU_cur[:]))
        p.emit("tensor", lambda h: h.transpose(out=psC[0:128, 0:1], in_=mrow[:],
                                               identity=CS(ONES)[0:1, 0:1]),
               wait=_xw(p, "SV"), inc="ST")
        p.emit("vector", lambda h: h.tensor_copy(out=mcol[:], in_=psC[0:128, 0:1]),
               wait=[("ST", p.c("ST"))])
        if recon:
            p.emit("vector", lambda h: h.tensor_copy(out=g0b[:], in_=g0[:]))
            for pc in range(16):
                p.emit("tensor", lambda h, pc=pc: h.matmul(
                    out=psC[:, 1:2], lhsT=b_b[:, 128 * pc:128 * (pc + 1)],
                    rhs=g0b[:, pc:pc + 1], start=(pc == 0), stop=(pc == 15)),
                    wait=(_xw(p, "SV") if pc == 0 else ()),
                    inc=("ST" if pc == 15 else None))
            p.emit("vector", lambda h: h.tensor_copy(out=qc[:], in_=psC[:, 1:2]),
                   wait=[("ST", p.c("ST"))])
            p.emit("vector", lambda h: h.tensor_add(out=mcol[:], in0=mcol[:], in1=qc[:]))
        p.emit("tensor", lambda h: h.matmul(out=psC[:, 0:1], lhsT=TUf, rhs=mcol[:],
                                            start=True, stop=True),
               wait=_xw(p, "SV"), inc="ST")
        p.emit("vector", lambda h: h.tensor_copy(out=blocf[:], in_=psC[:, 0:1]),
               wait=[("ST", p.c("ST"))])
        p.emit("vector", lambda h: h.tensor_copy(out=blocb[:], in_=blocf[:]))
        if not recon:
            for pc in range(16):
                p.emit("vector", lambda h, pc=pc: h.tensor_scalar_mul(
                    out=aG[:, 128 * pc:128 * (pc + 1)],
                    in0=a_b[:, 128 * pc:128 * (pc + 1)],
                    scalar1=Gf[:, pc:pc + 1]))
            for pc in range(16):
                p.emit("tensor", lambda h, pc=pc: h.transpose(
                    out=psT[:], in_=aG[:, 128 * pc:128 * (pc + 1)], identity=cstb[:]),
                    wait=[("SV", p.c("SV"))], inc="ST")
                p.emit("vector", lambda h, pc=pc: h.tensor_copy(
                    out=aGtk[:, 128 * pc:128 * (pc + 1)], in_=psT[:]),
                    wait=[("ST", p.c("ST"))])
            for pc in range(16):
                p.emit("tensor", lambda h, pc=pc: h.matmul(
                    out=psM[:, pc:pc + 1], lhsT=aGtk[:, 128 * pc:128 * (pc + 1)],
                    rhs=blocb[:], start=(pc == 0), stop=(pc == 15)),
                    wait=(_xw(p, "SV") if pc == 0 else ()),
                    inc=("ST" if pc == 15 else None))
            p.emit("vector", lambda h: h.tensor_tensor(
                out=dvf[:].unsqueeze(2),
                in0=Ef.rearrange("p (c n) -> p c n", c=16)[:, :, 127:128],
                in1=Gf[:].unsqueeze(2), op=ALU.mult),
                wait=[("ST", p.c("ST"))])
            p.emit("vector", lambda h: h.scalar_tensor_tensor(
                out=dvf[:], in0=psM[:], scalar=-2.0, in1=dvf[:],
                op0=ALU.mult, op1=ALU.add))
            p.emit("vector", lambda h: h.tensor_copy(out=LEb[:], in_=Ef))
            p.emit("scalar", lambda h: h.activation(out=LEi[:], in_=CumI,
                                                    func=AF.Exp),
                   wait=_xw(p, "SV"))
            p.wait("vector", ("SS", p.c("SS")))
            sv = p.c("SV")
            p.dma("sync", aGt_d[ch], aGtk[:], wait=[("SV", sv), ("SS", p.c("SS"))],
                  inc="SO")
            p.dma("sync", bfm_d[ch], b_b[:], inc="SO")
            p.dma("sync", TU_d[ch], TUf, inc="SO")
            p.dma("sync", Gd_d[ch], Gf[:], inc="SO")
            p.dma("sync", dv_d[ch], dvf[:], inc="SO")
            p.dma("sync", a_d[ch], a_b[:], inc="SO")
            p.dma("sync", E_d[ch], LEb[:], inc="SO")
            p.dma("sync", LI_d[ch], LEi[:], inc="SO")
            p.dma("sync", mc_d[ch], mcol[:], inc="SO")
        else:
            p.emit("tensor", lambda h: h.transpose(out=psC[0:1, 0:128],
                                                   in_=blocf[:],
                                                   identity=CS(IDENT)),
                   wait=_xw(p, "SV"), inc="ST")
            p.emit("vector", lambda h: h.tensor_copy(out=brow[:], in_=psC[0:1, 0:128]),
                   wait=[("ST", p.c("ST"))])
            p.emit("tensor", lambda h: h.matmul(out=psC[:], lhsT=CS(ONES)[0:1, :],
                                                rhs=brow[:], start=True, stop=True),
                   wait=_xw(p, "SV"), inc="ST")
            p.emit("vector", lambda h: h.tensor_copy(out=brb[:], in_=psC[:]),
                   wait=[("ST", p.c("ST"))])
            p.emit("vector", lambda h: h.tensor_tensor(
                out=Wt[:].rearrange("p (c n) -> p c n", c=16),
                in0=a_b[:].rearrange("p (c n) -> p c n", c=16),
                in1=brb[:].unsqueeze(1).broadcast_to((128, 16, 128)),
                op=ALU.mult))
            _scan16(p, CumW, Wt, CS(ONES), [])
            p.emit("vector", lambda h: h.scalar_tensor_tensor(
                out=CumW[:], in0=CumW[:], scalar=-2.0, in1=Ef,
                op0=ALU.mult, op1=ALU.add))
            for pc in range(16):
                p.emit("vector", lambda h, pc=pc: h.tensor_scalar_add(
                    out=CumW[:, 128 * pc:128 * (pc + 1)],
                    in0=CumW[:, 128 * pc:128 * (pc + 1)],
                    scalar1=g0[:, pc:pc + 1]))
            p.emit("scalar", lambda h: h.activation(out=LEb[:], in_=CumI,
                                                    func=AF.Exp), wait=_xw(p, "SV"))
            p.emit("vector", lambda h, ch=ch: h.tensor_mul(
                out=xi[:, ch * DI:(ch + 1) * DI], in0=CumW[:], in1=LEb[:]),
                wait=_xw(p, "SS"))
    if recon:
        p.dma("gpsimd", wbuf[:, :8 * DI].rearrange("p (c n) -> p c n", c=8),
              winzT.rearrange("(c p) n -> p c n", p=128),
              wait=_xw(p, "SV", "ST"), inc="GD")
        p.dma("gpsimd", wbuf[:, 8 * DI:].rearrange("p (c n) -> p c n", c=16),
              woTq.rearrange("(c p) n -> p c n", p=128), inc="GD")
        for ch in range(NCH_CORE):
            wso = [("SO", p.c("SO"))]
            p.dma("gpsimd", xnb[:].rearrange("p (c n) -> p c n", c=8),
                  xn1T.rearrange("(c p) n -> p c n", p=128)[:, :, 128 * ch:128 * (ch + 1)],
                  wait=[("ST", p.c("ST"))], inc="GD")
            for es in range(16):
                for k in range(8):
                    p.emit("tensor", lambda h, es=es, k=k: h.matmul(
                        out=psB[:],
                        lhsT=wbuf[:, k * DI + 128 * es:k * DI + 128 * (es + 1)],
                        rhs=xnb[:, k * 128:(k + 1) * 128],
                        start=(k == 0), stop=(k == 7)),
                        wait=(_xw(p, "GD", "SV") if k == 0 else ()),
                        inc=("ST" if k == 7 else None))
                p.emit("vector", lambda h, es=es: h.tensor_copy(
                    out=zb[:, 128 * es:128 * (es + 1)], in_=psB[:]),
                    wait=[("ST", p.c("ST"))])
            p.emit("scalar", lambda h: h.activation(out=zb[:], in_=zb[:], func=AF.Silu),
                   wait=_xw(p, "SV"))
            p.emit("vector", lambda h, ch=ch: h.tensor_mul(
                out=ub[:], in0=zb[:], in1=xi[:, ch * DI:(ch + 1) * DI]),
                wait=_xw(p, "SS"))
            for oc in range(2):
                for js in range(16):
                    p.emit("tensor", lambda h, js=js, oc=oc: h.matmul(
                        out=psA[:],
                        lhsT=ub[:, 128 * js:128 * (js + 1)],
                        rhs=wbuf[:, 8 * DI + js * DM + 512 * oc:
                                 8 * DI + js * DM + 512 * (oc + 1)],
                        start=(js == 0), stop=(js == 15)),
                        wait=(_xw(p, "SV") if js == 0 else ()),
                        inc=("ST" if js == 15 else None))
                p.emit("vector", lambda h, oc=oc: h.tensor_copy(
                    out=xoA[:, 512 * oc:512 * (oc + 1)], in_=psA[:]),
                    wait=[("ST", p.c("ST"))] + (wso if oc == 0 else []))
            p.dma("sync", at1[128 * ch:128 * (ch + 1), :], xoA[:, 0:DM],
                  wait=_xw(p, "SV"), inc="SO")
    p.wait("sync", ("SO", p.c("SO")))
    return p


def build_l1a():
    return build_l1(False)


def build_l1rec():
    return build_l1(True)


def build_carry():
    p = Ph("carry")
    aGt = p.dram("aGt", (NCH, 128, DI), BF16)
    bfm = p.dram("bfm", (NCH, 128, DI), BF16)
    TUc = p.dram("TUc", (NCH, 128, 128))
    Gc = p.dram("Gc", (NCH, 128, 16))
    dvc = p.dram("dvc", (NCH, 128, 16))
    carr = p.dram("carr", (NCH, 128, 16), kind="ExternalOutput")
    ag = [p.sb(f"ag{i}", (128, DI), BF16) for i in range(2)]
    bf_ = [p.sb(f"bf{i}", (128, DI), BF16) for i in range(2)]
    tu = [p.sb(f"tu{i}", (128, 128)) for i in range(2)]
    gg = [p.sb(f"gg{i}", (128, 16)) for i in range(2)]
    dv_ = [p.sb(f"dv{i}", (128, 16)) for i in range(2)]
    hh = p.sb("hh", (128, 16))
    hbb = p.sb("hbb", (128, 16), BF16)
    qc = p.sb("qc", (128, 1))
    q2 = p.sb("q2", (128, 1), BF16)
    wf = p.sb("wf", (128, 16))
    psQ = p.ps("psQ", (128, 2))
    psW = p.ps("psW", (128, 16))
    p.emit("vector", lambda h: h.memset(hh[:], 0.0))
    for ch in range(NCH):
        b = ch % 2
        p.dma("sync", ag[b][:], aGt[ch], wait=[("ST", p.c("ST"))], inc="LD")
        p.dma("sync", bf_[b][:], bfm[ch], inc="LD")
        p.dma("sync", tu[b][:], TUc[ch], inc="LD")
        p.dma("sync", gg[b][:], Gc[ch], inc="LD")
        p.dma("sync", dv_[b][:], dvc[ch], inc="LD")
        ldc = p.c("LD")
        p.dma("sync", carr[ch], hh[:], wait=[("SV", p.c("SV"))], inc="SO")
        so = p.c("SO")
        p.emit("vector", lambda h: h.tensor_copy(out=hbb[:], in_=hh[:]),
               wait=[("LD", ldc)])
        for pc in range(16):
            p.emit("tensor", lambda h, pc=pc, b=b: h.matmul(
                out=psQ[:, 0:1], lhsT=bf_[b][:, 128 * pc:128 * (pc + 1)],
                rhs=hbb[:, pc:pc + 1], start=(pc == 0), stop=(pc == 15)),
                wait=(_xw(p, "SV") if pc == 0 else ()),
                inc=("ST" if pc == 15 else None))
        p.emit("vector", lambda h: h.tensor_copy(out=qc[:], in_=psQ[:, 0:1]),
               wait=[("ST", p.c("ST"))])
        p.emit("tensor", lambda h, b=b: h.matmul(out=psQ[:, 1:2], lhsT=tu[b][:],
                                                 rhs=qc[:], start=True, stop=True),
               wait=_xw(p, "SV"), inc="ST")
        p.emit("vector", lambda h: h.tensor_copy(out=q2[:], in_=psQ[:, 1:2]),
               wait=[("ST", p.c("ST"))])
        for pc in range(16):
            p.emit("tensor", lambda h, pc=pc, b=b: h.matmul(
                out=psW[:, pc:pc + 1], lhsT=ag[b][:, 128 * pc:128 * (pc + 1)],
                rhs=q2[:], start=(pc == 0), stop=(pc == 15)),
                wait=(_xw(p, "SV") if pc == 0 else ()),
                inc=("ST" if pc == 15 else None))
        p.emit("vector", lambda h, b=b: h.tensor_mul(out=hh[:], in0=hh[:],
                                                     in1=gg[b][:]),
               wait=[("ST", p.c("ST")), ("SO", so)])
        p.emit("vector", lambda h: h.scalar_tensor_tensor(
            out=hh[:], in0=psW[:], scalar=-2.0, in1=hh[:],
            op0=ALU.mult, op1=ALU.add))
        p.emit("vector", lambda h, b=b: h.tensor_add(out=hh[:], in0=hh[:],
                                                     in1=dv_[b][:]))
    p.wait("sync", ("SO", p.c("SO")))
    return p


def build_moh():
    p = Ph("moh")
    TT = T
    xnT = p.dram("xnT", (DM, TT))
    wqT = p.dram("wqT", (DM, DM))
    wkT = p.dram("wkT", (DM, DM))
    wvT = p.dram("wvT", (DM, DM))
    bqc = p.dram("bqc", (128, 8))
    bkc = p.dram("bkc", (128, 8))
    bvc = p.dram("bvc", (128, 8))
    r8 = p.dram("r8", (8, 16))
    mask8 = p.dram("mask8", (8, 1))
    kcol = p.dram("kcol", (16, 1))
    brc = p.dram("brc", (16, 1))
    kron64T = p.dram("kron64T", (64, DM))
    idc = p.dram("idc", (128, 128))
    oh = p.dram("oh", (64, TT), BF16, kind="ExternalOutput")
    dbg9 = p.dram("dbg9", (16, 8), kind="ExternalOutput")

    xs_ = [p.sb(f"xs{i}", (128, 8 * 512), BF16) for i in range(2)]
    wq_s = p.sb("wq_s", (128, 8 * DM), BF16)
    wk_s = p.sb("wk_s", (128, 8 * DM), BF16)
    wv_s = p.sb("wv_s", (128, 8 * DM), BF16)
    bq_s = p.sb("bq_s", (128, 8))
    q_sb = p.sb("q_sb", (128, TT), BF16)
    k_sb = p.sb("k_sb", (128, TT), BF16)
    v_sb = p.sb("v_sb", (128, 64 * 65), BF16)
    s_sb = [p.sb(f"s_sb{i}", (128, 512), BF16) for i in range(4)]
    ident = p.sb("ident", (128, 128))
    kron = p.sb("kron", (64, DM))
    rr = p.sb("rr", (8, 16))
    m8s = p.sb("m8s", (8, 1))
    kc = p.sb("kc", (16, 1))
    brs = p.sb("brs", (16, 1))
    dcol = p.sb("dcol", (16, 1))
    drow = p.sb("drow", (1, 16))
    dmax = p.sb("dmax", (1, 1))
    dsum = p.sb("dsum", (1, 1))
    Mt = p.sb("Mt", (16, 16))
    rankc = p.sb("rankc", (16, 1))
    selc = p.sb("selc", (16, 1))
    selr = p.sb("selr", (1, 16))
    topm = p.sb("topm", (16, 1))
    expd = p.sb("expd", (16, 1))
    gsc = p.sb("gsc", (1, 1))
    g64 = p.sb("g64", (64, 1))
    pgT = p.sb("pgT", (64, DM))
    pg = p.sb("pg", (128, 8 * 64), BF16)
    wsel = {t: p.sb(f"wsel_{t}", (128, 8 * 64), BF16) for t in "qkv"}
    bsel = {t: p.sb(f"bsel_{t}", (64, 1)) for t in "qkv"}
    bselr = {t: p.sb(f"bselr_{t}", (1, 64)) for t in "qkv"}
    onessm = p.sb("onessm", (128, 1))
    den_r = p.sb("den_r", (1, 512))
    denb = p.sb("denb", (64, 512))
    o_bf = p.sb("o_bf", (64, 512), BF16)
    psP = p.ps("psP", (128, 512))
    psO = p.ps("psO", (65, 512))
    psS = [p.ps(f"psS{i}", (128, 512)) for i in range(4)]
    psX = p.ps("psX", (128, 128))

    p.dma("sync", rr[:], r8[:], inc="LD")
    p.dma("sync", m8s[:], mask8[:], inc="LD")
    p.dma("sync", kc[:], kcol[:], inc="LD")
    p.dma("sync", brs[:], brc[:], inc="LD")
    p.dma("sync", ident[:], idc[:], inc="LD")
    p.dma("sync", kron[:], kron64T[:], inc="LD")
    p.dma("gpsimd", wq_s[:].rearrange("p (c n) -> p c n", c=8),
          wqT.rearrange("(c p) n -> p c n", p=128), inc="GD")
    p.dma("gpsimd", wk_s[:].rearrange("p (c n) -> p c n", c=8),
          wkT.rearrange("(c p) n -> p c n", p=128), inc="GD")
    p.dma("gpsimd", wv_s[:].rearrange("p (c n) -> p c n", c=8),
          wvT.rearrange("(c p) n -> p c n", p=128), inc="GD")
    p.emit("vector", lambda h: h.memset(onessm[:], 1.0), wait=_xw(p, "LD"))

    p.emit("vector", lambda h: h.tensor_scalar_mul(out=rr[:], in0=rr[:], scalar1=m8s[:]),
           wait=_xw(p, "LD"))
    p.emit("tensor", lambda h: h.matmul(out=psX[0:16, 0:1], lhsT=rr[:], rhs=onessm[0:8, :],
                                        start=True, stop=True),
           wait=_xw(p, "SV"), inc="ST")
    p.emit("vector", lambda h: h.tensor_copy(out=dcol[:], in_=psX[0:16, 0:1]),
           wait=[("ST", p.c("ST"))])
    p.emit("vector", lambda h: h.tensor_scalar_mul(out=dcol[:], in0=dcol[:],
                                                   scalar1=1.0 / T))
    p.emit("vector", lambda h: h.tensor_add(out=dcol[:], in0=dcol[:], in1=brs[:]))
    p.emit("tensor", lambda h: h.transpose(out=psX[0:1, 0:16], in_=dcol[:],
                                           identity=ident[0:16, 0:16]),
           wait=_xw(p, "SV"), inc="ST")
    p.emit("vector", lambda h: h.tensor_copy(out=drow[:], in_=psX[0:1, 0:16]),
           wait=[("ST", p.c("ST"))])
    p.emit("vector", lambda h: h.tensor_reduce(out=dmax[:], in_=drow[:], axis=AX.X,
                                               op=ALU.max, opt_input=False,
                                               opt_output=False))
    p.emit("vector", lambda h: h.tensor_scalar_mul(out=dmax[:], in0=dmax[:], scalar1=-1.0))
    p.emit("scalar", lambda h: h.activation(out=drow[:], in_=drow[:], func=AF.Exp,
                                            bias=dmax[0:1, :]), wait=_xw(p, "SV"))
    p.emit("vector", lambda h: h.tensor_reduce(out=dsum[:], in_=drow[:], axis=AX.X,
                                               op=ALU.add, opt_input=False,
                                               opt_output=False), wait=_xw(p, "SS"))
    p.emit("vector", lambda h: h.reciprocal(out=dsum[:], in_=dsum[:]))
    p.emit("vector", lambda h: h.tensor_scalar_mul(out=drow[:], in0=drow[:],
                                                   scalar1=dsum[0:1, :]))
    p.emit("tensor", lambda h: h.transpose(out=psX[0:16, 0:1], in_=drow[:],
                                           identity=ident[0:1, 0:1]),
           wait=_xw(p, "SV"), inc="ST")
    p.emit("vector", lambda h: h.tensor_copy(out=dcol[:], in_=psX[0:16, 0:1]),
           wait=[("ST", p.c("ST"))])
    p.emit("vector", lambda h: h.memset(Mt[:, 0:1], 1.0))
    p.emit("vector", lambda h: h.memset(selr[:], 1.0))
    p.emit("tensor", lambda h: h.matmul(out=psX[0:16, 0:16], lhsT=selr[:],
                                        rhs=drow[:], start=True, stop=True),
           wait=_xw(p, "SV"), inc="ST")
    p.emit("vector", lambda h: h.tensor_scalar(out=Mt[:], in0=psX[0:16, 0:16],
                                               scalar1=dcol[:, 0:1], scalar2=None,
                                               op0=ALU.is_lt),
           wait=[("ST", p.c("ST"))])
    p.emit("tensor", lambda h: h.matmul(out=psX[0:16, 1:2], lhsT=Mt[:],
                                        rhs=onessm[0:16, :], start=True, stop=True),
           wait=_xw(p, "SV"), inc="ST")
    p.emit("vector", lambda h: h.tensor_copy(out=rankc[:], in_=psX[0:16, 1:2]),
           wait=[("ST", p.c("ST"))])
    p.emit("vector", lambda h: h.tensor_tensor(out=selc[:], in0=rankc[:], in1=kc[:],
                                               op=ALU.is_equal))
    p.emit("vector", lambda h: h.tensor_single_scalar(out=topm[:], in_=rankc[:],
                                                      scalar=4.0, op=ALU.is_lt))
    p.emit("scalar", lambda h: h.activation(out=expd[:], in_=dcol[:], func=AF.Exp),
           wait=_xw(p, "SV"))
    p.emit("vector", lambda h: h.tensor_mul(out=expd[:], in0=expd[:], in1=topm[:]),
           wait=_xw(p, "SS"))
    p.emit("tensor", lambda h: h.matmul(out=psX[0:1, 2:3], lhsT=expd[:],
                                        rhs=onessm[0:16, :], start=True, stop=True),
           wait=_xw(p, "SV"), inc="ST")
    p.emit("vector", lambda h: h.tensor_copy(out=dsum[:], in_=psX[0:1, 2:3]),
           wait=[("ST", p.c("ST"))])
    p.emit("vector", lambda h: h.tensor_scalar_add(out=dsum[:], in0=dsum[:], scalar1=12.0))
    p.emit("vector", lambda h: h.reciprocal(out=dsum[:], in_=dsum[:]))
    p.emit("vector", lambda h: h.tensor_mul(out=expd[:], in0=dcol[:], in1=selc[:]))
    p.emit("tensor", lambda h: h.matmul(out=psX[0:1, 3:4], lhsT=expd[:],
                                        rhs=onessm[0:16, :], start=True, stop=True),
           wait=_xw(p, "SV"), inc="ST")
    p.emit("vector", lambda h: h.tensor_copy(out=gsc[:], in_=psX[0:1, 3:4]),
           wait=[("ST", p.c("ST"))])
    p.emit("scalar", lambda h: h.activation(out=gsc[:], in_=gsc[:], func=AF.Exp),
           wait=_xw(p, "SV"))
    p.emit("vector", lambda h: h.tensor_mul(out=gsc[:], in0=gsc[:], in1=dsum[:]),
           wait=_xw(p, "SS"))
    p.emit("vector", lambda h: h.memset(den_r[:], 1.0))
    p.emit("tensor", lambda h: h.matmul(out=psX[0:64, 4:5], lhsT=den_r[0:1, 0:64],
                                        rhs=gsc[:], start=True, stop=True),
           wait=_xw(p, "SV"), inc="ST")
    p.emit("vector", lambda h: h.tensor_copy(out=g64[:], in_=psX[0:64, 4:5]),
           wait=[("ST", p.c("ST"))])
    dbg9t = p.sb("dbg9t", (16, 8))
    p.emit("vector", lambda h: h.memset(dbg9t[:], 0.0))
    p.emit("vector", lambda h: h.tensor_copy(out=dbg9t[:, 0:1], in_=dcol[:]))
    p.emit("vector", lambda h: h.tensor_copy(out=dbg9t[:, 1:2], in_=rankc[:]))
    p.emit("vector", lambda h: h.tensor_copy(out=dbg9t[:, 2:3], in_=selc[:]))
    p.emit("vector", lambda h: h.tensor_copy(out=dbg9t[:, 3:4], in_=topm[:]))
    p.emit("vector", lambda h: h.tensor_copy(out=dbg9t[0:1, 4:5], in_=gsc[:]))
    p.emit("vector", lambda h: h.tensor_copy(out=dbg9t[0:1, 5:6], in_=dsum[:]))
    p.emit("vector", lambda h: h.tensor_copy(out=dbg9t[0:64, 6:7].rearrange("(a b) c -> a (b c)", b=1)[0:16, :], in_=g64[0:16, :])) if False else None
    p.dma("sync", dbg9[:], dbg9t[:], wait=_xw(p, "SV"), inc="SO")
    p.emit("tensor", lambda h: h.transpose(out=psX[0:1, 0:16], in_=selc[:],
                                           identity=ident[0:16, 0:16]),
           wait=_xw(p, "SV"), inc="ST")
    p.emit("vector", lambda h: h.tensor_copy(out=selr[:], in_=psX[0:1, 0:16]),
           wait=[("ST", p.c("ST"))])
    selbig = p.sb("selbig", (1, DM))
    p.emit("vector", lambda h: h.tensor_copy(
        out=selbig[:].rearrange("a (c n) -> a c n", c=16),
        in_=selr[:].unsqueeze(2).broadcast_to((1, 16, 64))))
    for half in range(2):
        p.emit("tensor", lambda h, half=half: h.matmul(
            out=psP[0:64, :], lhsT=den_r[0:1, 0:64],
            rhs=selbig[:, 512 * half:512 * (half + 1)],
            start=True, stop=True),
            wait=_xw(p, "SV"), inc="ST")
        p.emit("vector", lambda h, half=half: h.tensor_mul(
            out=pgT[:, 512 * half:512 * (half + 1)], in0=psP[0:64, :],
            in1=kron[:, 512 * half:512 * (half + 1)]),
            wait=[("ST", p.c("ST"))])
    p.emit("vector", lambda h: h.tensor_scalar_mul(out=pgT[:], in0=pgT[:],
                                                   scalar1=g64[:, 0:1]))
    for ac in range(8):
        p.emit("tensor", lambda h, ac=ac: h.transpose(
            out=psX[:, 0:64], in_=pgT[:, 128 * ac:128 * (ac + 1)],
            identity=ident[0:64, 0:64]),
            wait=[("SV", p.c("SV"))], inc="ST")
        p.emit("vector", lambda h, ac=ac: h.tensor_copy(
            out=pg[:, 64 * ac:64 * (ac + 1)], in_=psX[:, 0:64]),
            wait=[("ST", p.c("ST"))])
    for tn, wt, bt in (("q", wq_s, bqc), ("k", wk_s, bkc), ("v", wv_s, bvc)):
        for dmc in range(8):
            for ac in range(8):
                p.emit("tensor", lambda h, tn=tn, wt=wt, dmc=dmc, ac=ac: h.matmul(
                    out=psX[:, 0:64],
                    lhsT=wt[:, ac * DM + 128 * dmc:ac * DM + 128 * (dmc + 1)],
                    rhs=pg[:, 64 * ac:64 * (ac + 1)],
                    start=(ac == 0), stop=(ac == 7)),
                    wait=(_xw(p, "GD", "SV") if ac == 0 else ()),
                    inc=("ST" if ac == 7 else None))
            p.emit("vector", lambda h, tn=tn, dmc=dmc: h.tensor_copy(
                out=wsel[tn][:, 64 * dmc:64 * (dmc + 1)], in_=psX[:, 0:64]),
                wait=[("ST", p.c("ST"))])
        bcolsb = p.sb(f"bcol_{tn}", (128, 8), BF16)
        p.dma("gpsimd", bcolsb[:], bt[:], inc="GD")
        for ac in range(8):
            p.emit("tensor", lambda h, ac=ac, bcolsb=bcolsb: h.matmul(
                out=psX[0:1, 64:128], lhsT=bcolsb[:, ac:ac + 1],
                rhs=pg[:, 64 * ac:64 * (ac + 1)], start=(ac == 0), stop=(ac == 7)),
                wait=(_xw(p, "SV", "GD") if ac == 0 else ()),
                inc=("ST" if ac == 7 else None))
        p.emit("vector", lambda h, tn=tn: h.tensor_copy(out=bselr[tn][:],
                                                        in_=psX[0:1, 64:128]),
               wait=[("ST", p.c("ST"))])
        p.emit("tensor", lambda h, tn=tn: h.transpose(
            out=psX[0:64, 5:6], in_=bselr[tn][:], identity=ident[0:1, 0:1]),
            wait=_xw(p, "SV"), inc="ST")
        p.emit("vector", lambda h, tn=tn: h.tensor_copy(out=bsel[tn][:],
                                                        in_=psX[0:64, 5:6]),
               wait=[("ST", p.c("ST"))])
    NSTRIP = TT // 512
    for s in range(NSTRIP):
        sb = xs_[s % 2]
        p.dma("gpsimd", sb[:].rearrange("p (c n) -> p c n", c=8),
              xnT.rearrange("(c p) n -> p c n", p=128)[:, :, 512 * s:512 * (s + 1)],
              wait=[("ST", p.c("ST"))], inc="GD")
        gds = p.c("GD")
        for k in range(8):
            p.emit("tensor", lambda h, s=s, k=k, sb=sb: h.matmul(
                out=psP[0:64, :], lhsT=wsel["q"][:, 64 * k:64 * (k + 1)],
                rhs=sb[:, 512 * k:512 * (k + 1)],
                start=(k == 0), stop=(k == 7)),
                wait=(_xw(p, "SV") + [("GD", gds)] if k == 0 else ()),
                inc=("ST" if k == 7 else None))
        p.emit("vector", lambda h, s=s: h.tensor_scalar_add(
            out=q_sb[0:64, 512 * s:512 * (s + 1)], in0=psP[0:64, :],
            scalar1=bsel["q"][:, 0:1]), wait=[("ST", p.c("ST"))])
        p.emit("vector", lambda h, s=s: h.tensor_copy(
            out=q_sb[64:128, 512 * s:512 * (s + 1)],
            in_=q_sb[0:64, 512 * s:512 * (s + 1)]))
        for k in range(8):
            p.emit("tensor", lambda h, s=s, k=k, sb=sb: h.matmul(
                out=psP[0:64, :], lhsT=wsel["k"][:, 64 * k:64 * (k + 1)],
                rhs=sb[:, 512 * k:512 * (k + 1)],
                start=(k == 0), stop=(k == 7)),
                wait=(_xw(p, "SV") if k == 0 else ()),
                inc=("ST" if k == 7 else None))
        p.emit("vector", lambda h, s=s: h.tensor_scalar(
            out=k_sb[0:64, 512 * s:512 * (s + 1)], in0=psP[0:64, :],
            scalar1=bsel["k"][:, 0:1], scalar2=0.125,
            op0=ALU.add, op1=ALU.mult), wait=[("ST", p.c("ST"))])
        p.emit("vector", lambda h, s=s: h.tensor_copy(
            out=k_sb[64:128, 512 * s:512 * (s + 1)],
            in_=k_sb[0:64, 512 * s:512 * (s + 1)]))
        for t4 in range(4):
            t = 4 * s + t4
            for k in range(8):
                p.emit("tensor", lambda h, t4=t4, k=k, sb=sb: h.matmul(
                    out=psX[:, 0:64],
                    lhsT=sb[:, 512 * k + 128 * t4:512 * k + 128 * (t4 + 1)],
                    rhs=wsel["v"][:, 64 * k:64 * (k + 1)],
                    start=(k == 0), stop=False),
                    wait=(_xw(p, "SV") if k == 0 else ()))
            p.emit("tensor", lambda h: h.matmul(
                out=psX[:, 0:64], lhsT=den_r[0:1, 0:128],
                rhs=bselr["v"][:], start=False, stop=True), inc="ST")
            p.emit("vector", lambda h, t=t: h.tensor_copy(
                out=v_sb[:, 65 * t:65 * t + 64], in_=psX[:, 0:64]),
                wait=[("ST", p.c("ST"))])
            p.emit("vector", lambda h, t=t: h.memset(v_sb[:, 65 * t + 64:65 * (t + 1)], 1.0))
    NKB = TT // 128
    for s in range(NSTRIP):
        mins = {}
        for step in range(NKB + 3):
            kb = step
            if kb < NKB:
                buf = kb % 4
                half = kb % 2
                wv = [("SV", mins[kb - 4])] if kb >= 4 else []
                if kb == 0:
                    wv += [("SV", p.c("SV")), ("SS", p.c("SS"))]
                p.emit("tensor", lambda h, kb=kb, s=s, buf=buf, half=half: h.matmul(
                    out=psS[buf][:],
                    lhsT=k_sb[64 * half:64 * (half + 1), 128 * kb:128 * (kb + 1)],
                    rhs=q_sb[64 * half:64 * (half + 1), 512 * s:512 * (s + 1)],
                    start=True, stop=True, tile_position=(64 * half, 0)),
                    wait=wv, inc="ST")
                st = p.c("ST")
                p.emit("scalar", lambda h, buf=buf: h.activation(
                    out=s_sb[buf][:], in_=psS[buf][:], func=AF.Exp),
                    wait=[("ST", st)])
                mins[kb] = p.emit("vector", lambda h, buf=buf: h.tensor_scalar_min(
                    out=s_sb[buf][:], in0=s_sb[buf][:], scalar1=1e30),
                    wait=[("SS", p.c("SS"))])
            kbav = step - 3
            if 0 <= kbav < NKB:
                p.emit("tensor", lambda h, kbav=kbav: h.matmul(
                    out=psO[:], lhsT=v_sb[:, 65 * kbav:65 * (kbav + 1)],
                    rhs=s_sb[kbav % 4][:],
                    start=(kbav == 0), stop=(kbav == NKB - 1)),
                    wait=[("SV", mins[kbav])], inc="ST")
        st = p.c("ST")
        p.emit("vector", lambda h: h.reciprocal(out=den_r[:], in_=psO[64:65, :]),
               wait=[("ST", st)])
        p.emit("vector", lambda h: h.memset(denb[0:1, :], 1.0))
        p.emit("tensor", lambda h: h.matmul(out=psP[0:64, :], lhsT=denb[0:1, 0:64],
                                            rhs=den_r[:], start=True, stop=True),
               wait=_xw(p, "SV"), inc="ST")
        p.emit("vector", lambda h: h.tensor_copy(out=denb[:], in_=psP[0:64, :]),
               wait=[("ST", p.c("ST"))])
        p.emit("vector", lambda h: h.tensor_mul(out=o_bf[:], in0=psO[0:64, :],
                                                in1=denb[:]))
        p.dma("sync", oh[:, 512 * s:512 * (s + 1)], o_bf[:],
              wait=[("SV", p.c("SV"))], inc="SO")
    p.wait("sync", ("SO", p.c("SO")))
    return p



def _consts_l1():
    c = np.zeros((9, 128, 128), np.float32)
    idx = np.arange(128)
    c[0] = (idx[:, None] > idx[None, :])
    c[1] = np.eye(128)
    def bd(m):
        M = np.zeros((128, 128), np.float32)
        for i in range(0, 128, m):
            M[i:i + m, i:i + m] = 1
        return M
    c[2] = bd(8); c[3] = bd(16) - bd(8); c[4] = bd(32) - bd(16)
    c[5] = bd(64) - bd(32); c[6] = bd(128) - bd(64)
    c[7] = 1.0
    c[8] = (idx[:, None] < idx[None, :])
    return c


def _fmcol(v):
    return np.ascontiguousarray(v.reshape(16, 128).T.astype(np.float32))


def kernel(L0_attn_x, L0_attn_Wqkv_w, L0_attn_Wqkv_b, L0_attn_out_w, L0_attn_out_b,
           L0_ffn_weight, L0_ffn_bias, L0_ffn_delta_w, L0_ffn_delta_act,
           L1_attn_omega, L1_attn_Win, L1_attn_Wl, L1_attn_bl, L1_attn_Wv,
           L1_attn_bv, L1_attn_Wout, L1_ffn_Wq, L1_ffn_bq, L1_ffn_Wk, L1_ffn_bk,
           L1_ffn_Wv, L1_ffn_bv, L1_ffn_Wr, L1_ffn_br, L1_ffn_Wo, L1_ffn_bo):
    x = f32(L0_attn_x).reshape(-1, DM)
    C = np.ascontiguousarray

    def sl(a, c):
        return C(a[c * SLAB:(c + 1) * SLAB])

    res = run_phase(build_ln, [{"xs": sl(x, c)} for c in range(NCORES)])
    xn = np.concatenate([r["xn"] for r in res], 0)

    W = f32(L0_ffn_weight)
    dw = f32(L0_ffn_delta_w)
    maps = [{"xq_in": sl(xn, c), "wrows": C(W[c * 128:(c + 1) * 128]),
             "dact": f32(L0_ffn_delta_act).reshape(1, 1),
             "dw": C(dw[c * 128:(c + 1) * 128]),
             "onesr": np.ones((1, 128), np.float32)} for c in range(NCORES)]
    res = run_phase(build_quant, maps)
    xq = np.concatenate([r["xq"] for r in res], 0)
    wq = np.concatenate([r["wq"] for r in res], 0)

    wqT = C(wq.T)
    fb = f32(L0_ffn_bias).reshape(1, -1)
    maps = [{"xqT": C(xq[c * SLAB:(c + 1) * SLAB].T), "wqT": wqT, "fbias": fb}
            for c in range(NCORES)]
    res = run_phase(build_ffnmm, maps)
    ffn = np.concatenate([r["ffn"] for r in res], 0)

    Wqkv = f32(L0_attn_Wqkv_w)
    bqkv = f32(L0_attn_Wqkv_b)
    masks = np.zeros((4, 128, 512), np.float32)
    for r_ in range(4):
        for kk in range(128):
            masks[r_, kk, :] = (kk + 128 * r_ <= np.arange(512)).astype(np.float32)
    maps = []
    for c in range(NCORES):
        b = c // 4
        heads = [4 * (c % 4) + j for j in range(4)]
        rows = [Wqkv[base + hh * 64:base + (hh + 1) * 64]
                for base in (0, 1024, 2048) for hh in heads]
        slc = np.concatenate(rows, 0)
        bias = np.concatenate([bqkv[base + hh * 64:base + (hh + 1) * 64]
                               for base in (0, 1024, 2048) for hh in heads])
        maps.append({"xnT": C(xn[b * T:(b + 1) * T].T), "wqkvT": C(slc.T),
                     "bqkv": C(bias.reshape(-1, 1)), "bqkvr": C(bias.reshape(1, -1)),
                     "masks": masks, "iden64": np.eye(64, dtype=np.float32)})
    res = run_phase(build_l0attn, maps)
    ys = [r["y"] for r in res]

    owT = C(f32(L0_attn_out_w).T)
    ob = f32(L0_attn_out_b).reshape(1, -1)
    maps = []
    for c in range(NCORES):
        b = c // 4
        Yb = np.concatenate([ys[4 * b + i] for i in range(4)], 0)
        colsl = slice((c % 4) * SLAB, (c % 4 + 1) * SLAB)
        maps.append({"yfm": C(Yb[:, colsl]), "owT": owT, "ob": ob,
                     "xs": sl(x, c), "ffn": sl(ffn, c)})
    res = run_phase(build_resid, maps)
    x1 = np.concatenate([r["x1"] for r in res], 0)

    wr = f32(L1_ffn_Wr)
    res = run_phase(build_ln1, [{"xs": sl(x1, c), "wr": wr} for c in range(NCORES)])
    xn1 = np.concatenate([r["xn"] for r in res], 0)
    rparts = np.stack([r["rpart"][0] for r in res], 0)

    Win = f32(L1_attn_Win)
    consts = _consts_l1()
    common = {"winxiT": C(Win[:DI].T), "wlT": C(f32(L1_attn_Wl).T),
              "wvT": C(f32(L1_attn_Wv).T), "omc": _fmcol(f32(L1_attn_omega).reshape(-1)),
              "blc": _fmcol(f32(L1_attn_bl)), "bvc": _fmcol(f32(L1_attn_bv)),
              "consts": consts}
    Z0 = np.zeros((128, 1), np.float32)
    O1 = np.ones((128, 1), np.float32)
    maps = [dict(common, xn1T=C(xn1[c * SLAB:(c + 1) * SLAB].T),
                 m0col=(Z0 if c % 4 == 0 else O1)) for c in range(NCORES)]
    res6 = run_phase(build_l1a, maps)
    maps7 = []
    for b in range(2):
        cm = {}
        for ki, ko in (("aGt", "aGt"), ("bfm", "bfm"), ("TU", "TUc"),
                       ("Gd", "Gc"), ("dv", "dvc")):
            cm[ko] = C(np.concatenate([res6[4 * b + i][ki] for i in range(4)], 0))
        maps7.append(cm)
    res7 = run_phase(build_carry, maps7)
    carr = [res7[b]["carr"] for b in range(2)]
    maps8 = []
    for c in range(NCORES):
        b, q = c // 4, c % 4
        maps8.append({"xn1T": C(xn1[c * SLAB:(c + 1) * SLAB].T),
                      "winzT": C(Win[DI:].T), "woTq": C(f32(L1_attn_Wout).T),
                      "a_d": res6[c]["a_d"], "bfm": res6[c]["bfm"],
                      "TU": res6[c]["TU"], "E_d": res6[c]["E_d"],
                      "LI_d": res6[c]["LI_d"], "mc_d": res6[c]["mc_d"],
                      "carries": C(carr[b][16 * q:16 * q + 16]),
                      "consts": consts})
    res8 = run_phase(build_l1rec2, maps8)
    at1 = np.concatenate([r["at1"] for r in res8], 0)

    kron64T = np.zeros((64, DM), np.float32)
    for hh in range(16):
        kron64T[:, hh * 64:(hh + 1) * 64] = np.eye(64)
    idc = np.eye(128, dtype=np.float32)
    maps = []
    for c in range(NCORES):
        b, k = c // 4, c % 4
        m8 = np.zeros((8, 1), np.float32)
        m8[4 * b:4 * b + 4] = 1.0
        maps.append({"xnT": C(xn1[b * T:(b + 1) * T].T),
                     "wqT": C(f32(L1_ffn_Wq).T), "wkT": C(f32(L1_ffn_Wk).T),
                     "wvT": C(f32(L1_ffn_Wv).T),
                     "bqc": C(f32(L1_ffn_bq).reshape(8, 128).T),
                     "bkc": C(f32(L1_ffn_bk).reshape(8, 128).T),
                     "bvc": C(f32(L1_ffn_bv).reshape(8, 128).T),
                     "r8": C(rparts), "mask8": m8,
                     "kcol": np.full((16, 1), float(k), np.float32),
                     "brc": C(f32(L1_ffn_br).reshape(16, 1)),
                     "kron64T": kron64T, "idc": idc})
    res = run_phase(build_moh, maps)
    ohs = [r["oh"] for r in res]

    wo = f32(L1_ffn_Wo)
    bo = f32(L1_ffn_bo).reshape(1, -1)
    maps = []
    for c in range(NCORES):
        b = c // 4
        Hb = np.concatenate([ohs[4 * b + i] for i in range(4)], 0)
        colsl = slice((c % 4) * SLAB, (c % 4 + 1) * SLAB)
        maps.append({"x1": sl(x1, c), "at1": sl(at1, c), "hT": C(Hb[:, colsl]),
                     "wo": wo, "bo": bo})
    res = run_phase(build_final, maps)
    out = np.concatenate([r["out"] for r in res], 0)
    return out.reshape(2, T, DM)


def build_l1rec2():
    p = Ph("l1rec2")
    xn1T = p.dram("xn1T", (DM, SLAB))
    winzT = p.dram("winzT", (DM, DI))
    woTq = p.dram("woTq", (DI, DM))
    a_d = p.dram("a_d", (NCH_CORE, 128, DI), BF16)
    bfm_d = p.dram("bfm", (NCH_CORE, 128, DI), BF16)
    TU_d = p.dram("TU", (NCH_CORE, 128, 128))
    E_d = p.dram("E_d", (NCH_CORE, 128, DI), BF16)
    LI_d = p.dram("LI_d", (NCH_CORE, 128, DI), BF16)
    mc_d = p.dram("mc_d", (NCH_CORE, 128, 1))
    carries = p.dram("carries", (NCH_CORE, 128, 16))
    consts = p.dram("consts", (9, 128, 128))
    at1 = p.dram("at1", (SLAB, DM), kind="ExternalOutput")

    xnb = p.sb("xnb", (128, 8 * 128), BF16)
    wbuf = p.sb("wbuf", (128, 8 * DI + 16 * DM), BF16)
    cst = p.sb("cst", (128, 9 * 128))
    ab = [p.sb(f"ab{i}", (128, DI), BF16) for i in range(2)]
    bb = [p.sb(f"bb{i}", (128, DI), BF16) for i in range(2)]
    Eb = [p.sb(f"Eb{i}", (128, DI), BF16) for i in range(2)]
    LIb = [p.sb(f"LIb{i}", (128, DI), BF16) for i in range(2)]
    tub = [p.sb(f"tub{i}", (128, 128)) for i in range(2)]
    mcb = [p.sb(f"mcb{i}", (128, 1)) for i in range(2)]
    g0 = [p.sb(f"g0_{i}", (128, 16)) for i in range(2)]
    g0b = p.sb("g0b", (128, 16), BF16)
    qc = p.sb("qc", (128, 1))
    bcol = p.sb("bcol", (128, 1))
    brow = p.sb("brow", (1, 128))
    brb = p.sb("brb", (128, 128))
    Wt = p.sb("Wt", (128, DI), BF16)
    CumW = p.sb("CumW", (128, DI))
    hbuf = p.sb("hbuf", (128, DI), BF16)
    zb = p.sb("zb", (128, DI), BF16)
    xoA = p.sb("xoA", (128, DM))
    psB = p.ps("psB", (128, 128))
    psC = p.ps("psC", (128, 128))
    psA = p.ps("psA", (128, 512))
    CS = lambda i: cst[:, 128 * i:128 * (i + 1)]
    STRIL, IDENT, M8, D16, D32, D64, D128, ONES, STRIU = range(9)

    p.dma("sync", cst[:].rearrange("p (c n) -> p c n", c=9),
          consts.rearrange("c p n -> p c n"), inc="LD")
    p.dma("gpsimd", wbuf[:, :8 * DI].rearrange("p (c n) -> p c n", c=8),
          winzT.rearrange("(c p) n -> p c n", p=128), inc="GD")
    p.dma("gpsimd", wbuf[:, 8 * DI:].rearrange("p (c n) -> p c n", c=16),
          woTq.rearrange("(c p) n -> p c n", p=128), inc="GD")
    for ch in range(NCH_CORE):
        cb = ch % 2
        wso = [("ST", p.c("ST")), ("SV", p.c("SV"))]
        p.dma("sync", ab[cb][:], a_d[ch], wait=wso, inc="LD")
        p.dma("sync", bb[cb][:], bfm_d[ch], inc="LD")
        p.dma("sync", Eb[cb][:], E_d[ch], inc="LD")
        p.dma("sync", LIb[cb][:], LI_d[ch], inc="LD")
        p.dma("sync", tub[cb][:], TU_d[ch], inc="LD")
        p.dma("sync", mcb[cb][:], mc_d[ch], inc="LD")
        p.dma("sync", g0[cb][:], carries[ch], inc="LD")
        p.dma("gpsimd", xnb[:].rearrange("p (c n) -> p c n", c=8),
              xn1T.rearrange("(c p) n -> p c n", p=128)[:, :, 128 * ch:128 * (ch + 1)],
              wait=[("ST", p.c("ST"))], inc="GD")
        ldc = p.c("LD")
        p.emit("vector", lambda h, cb=cb: h.tensor_copy(out=g0b[:], in_=g0[cb][:]),
               wait=[("LD", ldc)])
        for pc in range(16):
            p.emit("tensor", lambda h, pc=pc, cb=cb: h.matmul(
                out=psC[:, 1:2], lhsT=bb[cb][:, 128 * pc:128 * (pc + 1)],
                rhs=g0b[:, pc:pc + 1], start=(pc == 0), stop=(pc == 15)),
                wait=(_xw(p, "SV") if pc == 0 else ()),
                inc=("ST" if pc == 15 else None))
        p.emit("vector", lambda h, cb=cb: h.tensor_add(out=qc[:], in0=psC[:, 1:2],
                                                       in1=mcb[cb][:]),
               wait=[("ST", p.c("ST"))])
        p.emit("tensor", lambda h, cb=cb: h.matmul(out=psC[:, 0:1], lhsT=tub[cb][:],
                                                   rhs=qc[:], start=True, stop=True),
               wait=_xw(p, "SV"), inc="ST")
        p.emit("vector", lambda h: h.tensor_copy(out=bcol[:], in_=psC[:, 0:1]),
               wait=[("ST", p.c("ST"))])
        p.emit("tensor", lambda h: h.transpose(out=psC[0:1, 0:128], in_=bcol[:],
                                               identity=CS(IDENT)),
               wait=_xw(p, "SV"), inc="ST")
        p.emit("vector", lambda h: h.tensor_copy(out=brow[:], in_=psC[0:1, 0:128]),
               wait=[("ST", p.c("ST"))])
        p.emit("tensor", lambda h: h.matmul(out=psC[:], lhsT=CS(ONES)[0:1, :],
                                            rhs=brow[:], start=True, stop=True),
               wait=_xw(p, "SV"), inc="ST")
        p.emit("vector", lambda h: h.tensor_copy(out=brb[:], in_=psC[:]),
               wait=[("ST", p.c("ST"))])
        p.emit("vector", lambda h, cb=cb: h.tensor_tensor(
            out=Wt[:].rearrange("p (c n) -> p c n", c=16),
            in0=ab[cb][:].rearrange("p (c n) -> p c n", c=16),
            in1=brb[:].unsqueeze(1).broadcast_to((128, 16, 128)),
            op=ALU.mult))
        _scan16(p, CumW, Wt, CS(ONES), [])
        p.emit("vector", lambda h, cb=cb: h.scalar_tensor_tensor(
            out=CumW[:], in0=CumW[:], scalar=-2.0, in1=Eb[cb][:],
            op0=ALU.mult, op1=ALU.add))
        for pc in range(16):
            p.emit("vector", lambda h, pc=pc, cb=cb: h.tensor_scalar_add(
                out=CumW[:, 128 * pc:128 * (pc + 1)],
                in0=CumW[:, 128 * pc:128 * (pc + 1)],
                scalar1=g0[cb][:, pc:pc + 1]))
        p.emit("vector", lambda h, cb=cb: h.tensor_mul(out=hbuf[:], in0=CumW[:],
                                                       in1=LIb[cb][:]))
        for es in range(16):
            for k in range(8):
                p.emit("tensor", lambda h, es=es, k=k: h.matmul(
                    out=psB[:],
                    lhsT=wbuf[:, k * DI + 128 * es:k * DI + 128 * (es + 1)],
                    rhs=xnb[:, k * 128:(k + 1) * 128],
                    start=(k == 0), stop=(k == 7)),
                    wait=(_xw(p, "GD", "SV") if k == 0 else ()),
                    inc=("ST" if k == 7 else None))
            p.emit("vector", lambda h, es=es: h.tensor_copy(
                out=zb[:, 128 * es:128 * (es + 1)], in_=psB[:]),
                wait=[("ST", p.c("ST"))])
        p.emit("scalar", lambda h: h.activation(out=zb[:], in_=zb[:], func=AF.Silu),
               wait=_xw(p, "SV"))
        p.emit("vector", lambda h: h.tensor_mul(out=zb[:], in0=zb[:], in1=hbuf[:]),
               wait=_xw(p, "SS"))
        for oc in range(2):
            for js in range(16):
                p.emit("tensor", lambda h, js=js, oc=oc: h.matmul(
                    out=psA[:],
                    lhsT=zb[:, 128 * js:128 * (js + 1)],
                    rhs=wbuf[:, 8 * DI + js * DM + 512 * oc:
                             8 * DI + js * DM + 512 * (oc + 1)],
                    start=(js == 0), stop=(js == 15)),
                    wait=(_xw(p, "SV") if js == 0 else ()),
                    inc=("ST" if js == 15 else None))
            p.emit("vector", lambda h, oc=oc: h.tensor_copy(
                out=xoA[:, 512 * oc:512 * (oc + 1)], in_=psA[:]),
                wait=[("ST", p.c("ST"))] + ([("SO", p.c("SO"))] if oc == 0 else []))
        p.dma("sync", at1[128 * ch:128 * (ch + 1), :], xoA[:],
              wait=_xw(p, "SV"), inc="SO")
    p.wait("sync", ("SO", p.c("SO")))
    return p


# version 09 (final)
# speedup vs baseline: 1.0175x; 1.0175x over previous
import contextlib
import numpy as np
import ml_dtypes
import concourse.bass as bass
import concourse.mybir as mybir
from concourse.bass_utils import run_bass_kernel_spmd

F32 = mybir.dt.float32
BF16 = mybir.dt.bfloat16
I32 = mybir.dt.int32
AF = mybir.ActivationFunctionType
ALU = mybir.AluOpType
AX = mybir.AxisListType
NPBF16 = ml_dtypes.bfloat16

NCORES = 8
B, T, DM = 2, 8192, 1024
SLAB = 2048
DI = 2048
CH = 128
NCH_CORE = SLAB // CH
NCH = T // CH
LN2 = float(np.log(2.0))



class Ph:
    ENGS = ("sync", "gpsimd", "vector", "scalar", "tensor")

    def __init__(self, name):
        self.nc = bass.Bass("TRN2", target_bir_lowering=False,
                            detect_race_conditions=False)
        self.name = name
        self.q = {e: [] for e in self.ENGS}
        self.cnt = {}
        self.sems = {}
        self._stack = contextlib.ExitStack()
        self._uid = 0

    def dram(self, name, shape, dt=F32, kind="ExternalInput"):
        return self.nc.dram_tensor(name, list(shape), dt, kind=kind).ap()

    def sb(self, name, shape, dt=F32):
        return self._stack.enter_context(self.nc.sbuf_tensor(name, list(shape), dt))

    def ps(self, name, shape, dt=F32):
        return self._stack.enter_context(self.nc.psum_tensor(name, list(shape), dt))

    def sem(self, name):
        if name not in self.sems:
            self.sems[name] = self._stack.enter_context(self.nc.semaphore(name))
            self.cnt[name] = 0
        return self.sems[name]

    CHAIN = {"vector": "SV", "scalar": "SS"}

    def emit(self, eng, fn, wait=(), inc=None, inc_amt=1):
        for s, _ in wait:
            self.sem(s)
        chain = self.CHAIN.get(eng)
        prev = 0
        after = None
        if chain is not None:
            self.sem(chain)
            assert inc is None or inc == chain, (eng, inc)
            prev = self.cnt[chain]
            self.cnt[chain] = prev + 1
            after = prev + 1
            incname = None
            amt = 0
        else:
            incname = inc
            amt = inc_amt
            if inc is not None:
                self.sem(inc)
                self.cnt[inc] += inc_amt
                after = self.cnt[inc]
        waits = tuple(wait)

        def thunk(h):
            if chain is not None and prev > 0:
                h.wait_ge(self.sems[chain], prev)
            for s, v in waits:
                h.wait_ge(self.sems[s], v)
            r = fn(h)
            if r is None:
                r = h.nop()
            if chain is not None:
                r.then_inc(self.sems[chain], 1)
            elif incname is not None:
                r.then_inc(self.sems[incname], amt)
        self.q[eng].append(thunk)
        return after

    def dma(self, eng, out, in_, wait=(), inc=None, **kw):
        return self.emit(eng, lambda h: h.dma_start(out=out, in_=in_, **kw),
                         wait=wait, inc=inc, inc_amt=16)

    def wait(self, eng, *waits):
        for s, _ in waits:
            self.sem(s)
        self.q[eng].append(lambda h, w=tuple(waits): [h.wait_ge(self.sems[s], v) for s, v in w])

    def c(self, sem):
        self.sem(sem)
        return self.cnt[sem]

    def finish(self):
        nc = self.nc
        with nc.Block() as block:
            for e in self.ENGS:
                deco = getattr(block, e)
                qe = self.q[e]
                def body(h, qe=qe):
                    for t in qe:
                        t(h)
                deco(body)
        self._stack.close()
        return nc


_PROG_CACHE = {}


def run_phase(build_fn, in_maps, cache_key=None):
    key = cache_key or build_fn.__name__
    if key not in _PROG_CACHE:
        ph = build_fn()
        _PROG_CACHE[key] = ph.finish()
    nc = _PROG_CACHE[key]
    res = run_bass_kernel_spmd(nc, in_maps, core_ids=list(range(len(in_maps))))
    return res.results


def bf(x):
    return np.ascontiguousarray(x).astype(NPBF16)


def f32(x):
    return np.ascontiguousarray(x, dtype=np.float32)


def build_ln():
    p = Ph("ln")
    xs = p.dram("xs", (SLAB, DM))
    xn = p.dram("xn", (SLAB, DM), kind="ExternalOutput")
    NT = SLAB // 128
    xt = [p.sb(f"xt{i}", (128, DM)) for i in range(2)]
    xo = [p.sb(f"xo{i}", (128, DM)) for i in range(2)]
    sq = p.sb("sq", (128, DM))
    ssum = p.sb("ssum", (128, 1))
    msq = p.sb("msq", (128, 1))
    mean = p.sb("mean", (128, 1))
    var = p.sb("var", (128, 1))
    rstd = p.sb("rstd", (128, 1))
    eps = p.sb("eps", (128, 1))
    p.emit("vector", lambda h: h.memset(eps[:], 1e-5), inc="SV")
    for i in range(NT):
        b = i % 2
        w = [("SO", 16 * (i - 1))] if i >= 2 else []
        p.dma("sync", xt[b][:], xs[128 * i:128 * (i + 1), :], wait=w, inc="LD")
        ld = p.c("LD")
        p.emit("vector", lambda h, b=b: h.tensor_mul(out=sq[:], in0=xt[b][:], in1=xt[b][:]),
               wait=[("LD", ld), ("SS", p.c("SS")), ("SV", p.c("SV"))])
        p.emit("vector", lambda h: h.tensor_reduce(out=ssum[:], in_=sq[:],
                                                   axis=AX.X, op=ALU.add, opt_input=False, opt_output=False),
               inc="SV")
        p.emit("vector", lambda h, b=b: h.tensor_reduce(out=mean[:], in_=xt[b][:],
                                                        axis=AX.X, op=ALU.add, opt_input=False, opt_output=False),
               inc="SV")
        p.emit("vector", lambda h: h.tensor_scalar_mul(out=mean[:], in0=mean[:],
                                                       scalar1=1.0 / DM),
               wait=[("SV", p.c("SV"))])
        p.emit("vector", lambda h: h.tensor_mul(out=msq[:], in0=mean[:], in1=mean[:]))
        p.emit("vector", lambda h: h.scalar_tensor_tensor(out=var[:], in0=ssum[:],
                                                          scalar=1.0 / DM, in1=msq[:],
                                                          op0=ALU.mult, op1=ALU.subtract),
               inc="SV")
        p.emit("scalar", lambda h: h.activation(out=var[:], in_=var[:], func=AF.Ln,
                                                bias=eps[:]),
               wait=[("SV", p.c("SV"))])
        p.emit("scalar", lambda h: h.activation(out=rstd[:], in_=var[:], func=AF.Exp,
                                                scale=-0.5), inc="SS")
        p.emit("vector", lambda h, b=b: h.tensor_scalar(out=xo[b][:], in0=xt[b][:],
                                                        scalar1=mean[:], scalar2=rstd[:],
                                                        op0=ALU.subtract, op1=ALU.mult),
               wait=[("SS", p.c("SS"))], inc="SV")
        p.dma("sync", xn[128 * i:128 * (i + 1), :], xo[b][:],
              wait=[("SV", p.c("SV"))], inc="SO")
    p.wait("sync", ("SO", p.c("SO")))
    return p


def quant_scratch(p, tag=""):
    NB = DM // 32
    names = [("ab", DM, F32), ("amax", NB, F32), ("ei", NB, I32), ("ef", NB, F32),
             ("exf", NB, F32),
             ("sc", NB, F32), ("inv", NB, F32), ("xs_", DM, F32), ("sgn", DM, F32),
             ("aa", DM, F32), ("g2", DM, F32), ("g4", DM, F32), ("rec", DM, F32),
             ("stp", DM, F32), ("tq", DM, F32), ("ri", DM, I32), ("rf", DM, F32)]
    return {n: p.sb(f"q_{n}{tag}", (128, d), t) for n, d, t in names}


def _quant_tile(p, src_ap, dst_ap, nrows, delta_col, sc_):
    NB = DM // 32
    ab, amax, ei, ef = sc_["ab"], sc_["amax"], sc_["ei"], sc_["ef"]
    exf = sc_["exf"]
    sc, inv, xs_, sgn = sc_["sc"], sc_["inv"], sc_["xs_"], sc_["sgn"]
    aa, g2, g4, rec = sc_["aa"], sc_["g2"], sc_["g4"], sc_["rec"]
    stp, tq, ri, rf = sc_["stp"], sc_["tq"], sc_["ri"], sc_["rf"]

    r = slice(0, nrows)
    v = lambda t: t[r, :]
    p.emit("vector", lambda h: h.tensor_reduce(
        out=v(amax), in_=src_ap[r, :].rearrange("p (a b) -> p a b", b=32),
        axis=AX.X, op=ALU.max, apply_absolute_value=True,
        opt_input=False, opt_output=False), inc="SV")
    p.emit("vector", lambda h: h.tensor_scalar_max(out=v(amax), in0=v(amax),
                                                   scalar1=1e-30),
           wait=[("SV", p.c("SV"))], inc="SV")
    p.emit("scalar", lambda h: h.activation(out=v(ef), in_=v(amax), func=AF.Ln),
           wait=[("SV", p.c("SV"))], inc="SS")
    p.emit("vector", lambda h: h.tensor_scalar(out=v(ei), in0=v(ef), scalar1=1.0 / LN2,
                                               scalar2=0.5, op0=ALU.mult, op1=ALU.subtract),
           wait=[("SS", p.c("SS"))])
    p.emit("vector", lambda h: h.tensor_copy(out=v(exf), in_=v(ei)))
    p.emit("vector", lambda h: h.tensor_sub(out=v(exf), in0=v(exf), in1=v(delta_col)),
           inc="SV")
    p.emit("scalar", lambda h: h.activation(out=v(sc), in_=v(exf), func=AF.Exp, scale=LN2),
           wait=[("SV", p.c("SV"))])
    p.emit("scalar", lambda h: h.activation(out=v(inv), in_=v(exf), func=AF.Exp, scale=-LN2),
           inc="SS")
    p.emit("vector", lambda h: h.tensor_tensor(
        out=xs_[r, :].rearrange("p (a b) -> p a b", b=32),
        in0=src_ap[r, :].rearrange("p (a b) -> p a b", b=32),
        in1=inv[r, :].unsqueeze(2).broadcast_to((nrows, NB, 32)),
        op=ALU.mult), wait=[("SS", p.c("SS"))], inc="SV")
    p.emit("scalar", lambda h: h.activation(out=v(sgn), in_=v(xs_), func=AF.Sign),
           wait=[("SV", p.c("SV"))])
    p.emit("scalar", lambda h: h.activation(out=v(aa), in_=v(xs_), func=AF.Abs))
    p.emit("vector", lambda h: h.tensor_scalar_min(out=v(aa), in0=v(aa), scalar1=6.0),
           wait=[("SS", p.c("SS"))])
    p.emit("vector", lambda h: h.tensor_single_scalar(out=v(g2), in_=v(aa), scalar=2.0,
                                                      op=ALU.is_ge))
    p.emit("vector", lambda h: h.tensor_single_scalar(out=v(g4), in_=v(aa), scalar=4.0,
                                                      op=ALU.is_ge))
    p.emit("vector", lambda h: h.tensor_scalar(out=v(rec), in0=v(g2), scalar1=-1.0,
                                               scalar2=2.0, op0=ALU.mult, op1=ALU.add))
    p.emit("vector", lambda h: h.scalar_tensor_tensor(out=v(rec), in0=v(g4), scalar=-0.5,
                                                      in1=v(rec), op0=ALU.mult, op1=ALU.add))
    p.emit("vector", lambda h: h.tensor_scalar(out=v(stp), in0=v(g2), scalar1=0.5,
                                               scalar2=0.5, op0=ALU.mult, op1=ALU.add))
    p.emit("vector", lambda h: h.tensor_add(out=v(stp), in0=v(stp), in1=v(g4)))
    p.emit("vector", lambda h: h.tensor_mul(out=v(tq), in0=v(aa), in1=v(rec)))
    p.emit("vector", lambda h: h.tensor_copy(out=v(ri), in_=v(tq)))
    p.emit("vector", lambda h: h.tensor_copy(out=v(rf), in_=v(ri)))
    p.emit("vector", lambda h: h.tensor_mul(out=v(rf), in0=v(rf), in1=v(stp)))
    p.emit("vector", lambda h: h.tensor_mul(out=v(rf), in0=v(rf), in1=v(sgn)),
           wait=[("SS", p.c("SS"))])
    p.emit("vector", lambda h: h.tensor_tensor(
        out=dst_ap[r, :].rearrange("p (a b) -> p a b", b=32),
        in0=rf[r, :].rearrange("p (a b) -> p a b", b=32),
        in1=sc[r, :].unsqueeze(2).broadcast_to((nrows, NB, 32)),
        op=ALU.mult), inc="SV")


def build_quant():
    p = Ph("quant")
    NB = DM // 32
    xq_in = p.dram("xq_in", (SLAB, DM))
    wrows = p.dram("wrows", (128, DM))
    dact = p.dram("dact", (1, 1))
    dw = p.dram("dw", (128, NB))
    onesr = p.dram("onesr", (1, 128))
    xq = p.dram("xq", (SLAB, DM), BF16, kind="ExternalOutput")
    wq = p.dram("wq", (128, DM), BF16, kind="ExternalOutput")

    xt = p.sb("xt", (128, DM))
    xo = p.sb("xo", (128, DM), BF16)
    wt = p.sb("wt", (128, DM))
    wo = p.sb("wo", (128, DM), BF16)
    dac = p.sb("dac", (1, 1))
    onr = p.sb("onr", (1, 128))
    daccol = p.sb("daccol", (128, 1))
    dacb = p.sb("dacb", (128, NB))
    dwt = p.sb("dwt", (128, NB))
    dwc = p.sb("dwc", (128, NB))
    pdb = p.ps("pdb", (128, 1))

    p.dma("sync", dac[:], dact[:], inc="LD")
    p.dma("sync", dwt[:], dw[:], inc="LD")
    p.dma("sync", onr[:], onesr[:], inc="LD")
    p.emit("scalar", lambda h: h.activation(out=dac[:], in_=dac[:], func=AF.Tanh, scale=4.0),
           wait=[("LD", p.c("LD"))])
    p.emit("vector", lambda h: h.tensor_scalar(out=dac[:], in0=dac[:], scalar1=-0.5,
                                               scalar2=2.0, op0=ALU.mult, op1=ALU.add),
           wait=[("SS", p.c("SS"))])
    p.emit("tensor", lambda h: h.matmul(out=pdb[:], lhsT=onr[:], rhs=dac[:],
                                        start=True, stop=True),
           wait=[("SV", p.c("SV"))], inc="ST")
    p.emit("vector", lambda h: h.tensor_copy(out=daccol[:], in_=pdb[:]),
           wait=[("ST", p.c("ST"))], inc="SV")
    p.emit("vector", lambda h: h.memset(dacb[:], 0.0))
    p.emit("vector", lambda h: h.tensor_scalar_add(out=dacb[:], in0=dacb[:],
                                                   scalar1=daccol[:]),
           wait=[("SV", p.c("SV"))], inc="SV")
    p.emit("scalar", lambda h: h.activation(out=dwc[:], in_=dwt[:], func=AF.Tanh, scale=4.0),
           inc="SS")
    p.emit("vector", lambda h: h.tensor_scalar(out=dwc[:], in0=dwc[:], scalar1=-0.5,
                                               scalar2=2.0, op0=ALU.mult, op1=ALU.add),
           wait=[("SS", p.c("SS"))], inc="SV")

    p.dma("sync", wt[:], wrows[:], inc="LD")
    p.wait("vector", ("LD", p.c("LD")), ("SV", p.c("SV")))
    p.wait("scalar", ("LD", p.c("LD")), ("SV", p.c("SV")))
    qs = quant_scratch(p)
    _quant_tile(p, wt, wo, 128, dwc, qs)
    p.dma("sync", wq[:], wo[:], wait=[("SV", p.c("SV"))], inc="SO")

    for i in range(SLAB // 128):
        p.dma("sync", xt[:], xq_in[128 * i:128 * (i + 1), :],
              wait=[("SV", p.c("SV")), ("SO", p.c("SO"))], inc="LD")
        p.wait("vector", ("LD", p.c("LD")))
        p.wait("scalar", ("LD", p.c("LD")))
        _quant_tile(p, xt, xo, 128, dacb, qs)
        p.dma("sync", xq[128 * i:128 * (i + 1), :], xo[:],
              wait=[("SV", p.c("SV"))], inc="SO")
    p.wait("sync", ("SO", p.c("SO")))
    return p


def build_ffnmm():
    p = Ph("ffnmm")
    xqT = p.dram("xqT", (DM, SLAB), BF16)
    wqT = p.dram("wqT", (DM, DM), BF16)
    fbias = p.dram("fbias", (1, DM))
    ffn = p.dram("ffn", (SLAB, DM), kind="ExternalOutput")
    NT = SLAB // 128
    xqs = p.sb("xqs", (128, 8 * SLAB), BF16)
    wqs = p.sb("wqs", (128, 8 * DM), BF16)
    br = p.sb("br", (1, DM), BF16)
    ones1 = p.sb("ones1", (1, 128), BF16)
    xo = [p.sb(f"xo{i}", (128, DM)) for i in range(2)]
    pss = [p.ps(f"ps{i}", (128, DM)) for i in range(2)]
    p.dma("sync", xqs[:].rearrange("p (c n) -> p c n", c=8), xqT.rearrange("(c p) n -> p c n", p=128), inc="LD")
    p.dma("sync", wqs[:].rearrange("p (c n) -> p c n", c=8), wqT.rearrange("(c p) n -> p c n", p=128), inc="LD")
    p.dma("gpsimd", br[:], fbias[:], inc="GD")
    p.emit("vector", lambda h: h.memset(ones1[:], 1.0))
    ld0 = p.c("LD")
    gd0 = p.c("GD")
    sv0 = p.c("SV")
    evac_sv = {}
    for t in range(NT):
        pb = t % 2
        w = [("LD", ld0), ("GD", gd0), ("SV", sv0)]
        if t >= 2:
            w.append(("SV", evac_sv[t - 2]))
        for oc in range(2):
            for k in range(8):
                p.emit("tensor", lambda h, t=t, pb=pb, oc=oc, k=k: h.matmul(
                    out=pss[pb][:, 512 * oc:512 * (oc + 1)],
                    lhsT=xqs[:, k * SLAB + t * 128:k * SLAB + (t + 1) * 128],
                    rhs=wqs[:, k * DM + 512 * oc:k * DM + 512 * (oc + 1)],
                    start=(k == 0), stop=False), wait=(w if k == 0 and oc == 0 else ()))
            p.emit("tensor", lambda h, pb=pb, oc=oc: h.matmul(
                out=pss[pb][:, 512 * oc:512 * (oc + 1)], lhsT=ones1[:],
                rhs=br[:, 512 * oc:512 * (oc + 1)], start=False, stop=True),
                inc="ST")
        st = p.c("ST")
        wev = [("ST", st)] + ([("SO", 16 * (t - 1))] if t >= 2 else [])
        evac_sv[t] = p.emit("vector", lambda h, pb=pb: h.tensor_copy(out=xo[pb][:],
                                                                     in_=pss[pb][:]),
                            wait=wev)
        p.dma("sync", ffn[128 * t:128 * (t + 1), :], xo[pb][:],
              wait=[("SV", evac_sv[t])], inc="SO")
    p.wait("sync", ("SO", p.c("SO")))
    return p


def build_resid():
    p = Ph("resid")
    yfm = p.dram("yfm", (DM, SLAB), BF16)
    owT = p.dram("owT", (DM, DM))
    ob = p.dram("ob", (1, DM))
    xs = p.dram("xs", (SLAB, DM))
    ffn = p.dram("ffn", (SLAB, DM))
    x1 = p.dram("x1", (SLAB, DM), kind="ExternalOutput")
    NT = SLAB // 128
    ys = p.sb("ys", (128, 8 * SLAB), BF16)
    ws = p.sb("ws", (128, 8 * DM), BF16)
    br = p.sb("br", (1, DM), BF16)
    ones1 = p.sb("ones1", (1, 128), BF16)
    xt = [p.sb(f"xt{i}", (128, DM)) for i in range(2)]
    ft = [p.sb(f"ft{i}", (128, DM)) for i in range(2)]
    xo = [p.sb(f"xo{i}", (128, DM)) for i in range(2)]
    pss = [p.ps(f"ps{i}", (128, DM)) for i in range(2)]
    p.dma("sync", ys[:].rearrange("p (c n) -> p c n", c=8), yfm.rearrange("(c p) n -> p c n", p=128), inc="LD")
    p.dma("gpsimd", ws[:].rearrange("p (c n) -> p c n", c=8), owT.rearrange("(c p) n -> p c n", p=128), inc="GD")
    p.dma("gpsimd", br[:], ob[:], inc="GD")
    p.emit("vector", lambda h: h.memset(ones1[:], 1.0))
    ld0 = p.c("LD")
    gd0 = p.c("GD")
    sv0 = p.c("SV")
    for t in range(NT):
        pb = t % 2
        wso = [("SO", 16 * (t - 1))] if t >= 2 else []
        p.dma("sync", xt[pb][:], xs[128 * t:128 * (t + 1), :], wait=wso, inc="LD")
        p.dma("sync", ft[pb][:], ffn[128 * t:128 * (t + 1), :], inc="LD")
        ldt = p.c("LD")
        w = [("LD", ld0), ("GD", gd0), ("SV", sv0)]
        for oc in range(2):
            for k in range(8):
                p.emit("tensor", lambda h, t=t, pb=pb, oc=oc, k=k: h.matmul(
                    out=pss[pb][:, 512 * oc:512 * (oc + 1)],
                    lhsT=ys[:, k * SLAB + t * 128:k * SLAB + (t + 1) * 128],
                    rhs=ws[:, k * DM + 512 * oc:k * DM + 512 * (oc + 1)],
                    start=(k == 0), stop=False),
                    wait=(w + ([("SV", p.c("SV"))] if t >= 2 else [])
                          if k == 0 and oc == 0 else ()))
            p.emit("tensor", lambda h, pb=pb, oc=oc: h.matmul(
                out=pss[pb][:, 512 * oc:512 * (oc + 1)], lhsT=ones1[:],
                rhs=br[:, 512 * oc:512 * (oc + 1)], start=False, stop=True),
                inc="ST")
        st = p.c("ST")
        p.emit("vector", lambda h, pb=pb: h.tensor_add(out=xo[pb][:], in0=pss[pb][:],
                                                       in1=xt[pb][:]),
               wait=[("ST", st), ("LD", ldt)])
        p.emit("vector", lambda h, pb=pb: h.tensor_add(out=xo[pb][:], in0=xo[pb][:],
                                                       in1=ft[pb][:]))
        p.dma("sync", x1[128 * t:128 * (t + 1), :], xo[pb][:],
              wait=[("SV", p.c("SV"))], inc="SO")
    p.wait("sync", ("SO", p.c("SO")))
    return p


def build_residln1():
    p = Ph("residln1")
    yfm = p.dram("yfm", (DM, SLAB), BF16)
    owT = p.dram("owT", (DM, DM))
    ob = p.dram("ob", (1, DM))
    xs = p.dram("xs", (SLAB, DM))
    ffn = p.dram("ffn", (SLAB, DM))
    wr = p.dram("wr", (DM, 16))
    x1 = p.dram("x1", (SLAB, DM), kind="ExternalOutput")
    xn = p.dram("xn", (SLAB, DM), kind="ExternalOutput")
    rpart = p.dram("rpart", (1, 16), kind="ExternalOutput")
    NT = SLAB // 128
    ys = p.sb("ys", (128, 8 * SLAB), BF16)
    ws = p.sb("ws", (128, 8 * DM), BF16)
    br = p.sb("br", (1, DM), BF16)
    ones1 = p.sb("ones1", (1, 128), BF16)
    xt = [p.sb(f"xt{i}", (128, DM)) for i in range(2)]
    ft = [p.sb(f"ft{i}", (128, DM)) for i in range(2)]
    xo = [p.sb(f"xo{i}", (128, DM)) for i in range(2)]
    lxo = [p.sb(f"lxo{i}", (128, DM)) for i in range(2)]
    lxob = [p.sb(f"lxob{i}", (128, DM), BF16) for i in range(2)]
    sq = p.sb("sq", (128, DM))
    ssum = p.sb("ssum", (128, 1))
    msq = p.sb("msq", (128, 1))
    mean = p.sb("mean", (128, 1))
    var = p.sb("var", (128, 1))
    rstd = p.sb("rstd", (128, 1))
    eps = p.sb("eps", (128, 1))
    onesc = p.sb("onesc", (128, 1), BF16)
    wrs = p.sb("wrs", (128, 8 * 16), BF16)
    csum = p.sb("csum", (128, 8), BF16)
    csumf = p.sb("csumf", (128, 8))
    rp = p.sb("rp", (1, 16))
    pss = [p.ps(f"ps{i}", (128, DM)) for i in range(2)]
    psc = p.ps("psc", (128, 8))
    psr = p.ps("psr", (1, 16))
    p.dma("sync", ys[:].rearrange("p (c n) -> p c n", c=8), yfm.rearrange("(c p) n -> p c n", p=128), inc="LD")
    p.dma("gpsimd", ws[:].rearrange("p (c n) -> p c n", c=8), owT.rearrange("(c p) n -> p c n", p=128), inc="GD")
    p.dma("gpsimd", br[:], ob[:], inc="GD")
    p.dma("gpsimd", wrs[:].rearrange("p (c n) -> p c n", c=8), wr.rearrange("(c p) n -> p c n", p=128), inc="GD")
    p.emit("vector", lambda h: h.memset(ones1[:], 1.0))
    p.emit("vector", lambda h: h.memset(eps[:], 1e-5))
    p.emit("vector", lambda h: h.memset(onesc[:], 1.0))
    p.emit("vector", lambda h: h.memset(csumf[:], 0.0))
    ld0 = p.c("LD")
    gd0 = p.c("GD")
    sv0 = p.c("SV")
    for t in range(NT):
        pb = t % 2
        wso = [("SO", 32 * (t - 1))] if t >= 2 else []
        p.dma("sync", xt[pb][:], xs[128 * t:128 * (t + 1), :], wait=wso, inc="LD")
        p.dma("sync", ft[pb][:], ffn[128 * t:128 * (t + 1), :], inc="LD")
        ldt = p.c("LD")
        w = [("LD", ld0), ("GD", gd0), ("SV", sv0)]
        for oc in range(2):
            for k in range(8):
                p.emit("tensor", lambda h, t=t, pb=pb, oc=oc, k=k: h.matmul(
                    out=pss[pb][:, 512 * oc:512 * (oc + 1)],
                    lhsT=ys[:, k * SLAB + t * 128:k * SLAB + (t + 1) * 128],
                    rhs=ws[:, k * DM + 512 * oc:k * DM + 512 * (oc + 1)],
                    start=(k == 0), stop=False),
                    wait=(w + ([("SV", p.c("SV"))] if t >= 2 else [])
                          if k == 0 and oc == 0 else ()))
            p.emit("tensor", lambda h, pb=pb, oc=oc: h.matmul(
                out=pss[pb][:, 512 * oc:512 * (oc + 1)], lhsT=ones1[:],
                rhs=br[:, 512 * oc:512 * (oc + 1)], start=False, stop=True),
                inc="ST")
        st = p.c("ST")
        p.emit("vector", lambda h, pb=pb: h.tensor_add(out=xo[pb][:], in0=pss[pb][:],
                                                       in1=xt[pb][:]),
               wait=[("ST", st), ("LD", ldt)])
        p.emit("vector", lambda h, pb=pb: h.tensor_add(out=xo[pb][:], in0=xo[pb][:],
                                                       in1=ft[pb][:]))
        p.dma("sync", x1[128 * t:128 * (t + 1), :], xo[pb][:],
              wait=[("SV", p.c("SV"))], inc="SO")
        p.emit("vector", lambda h, pb=pb: h.tensor_mul(out=sq[:], in0=xo[pb][:],
                                                       in1=xo[pb][:]),
               wait=[("SS", p.c("SS"))])
        p.emit("vector", lambda h: h.tensor_reduce(out=ssum[:], in_=sq[:],
                                                   axis=AX.X, op=ALU.add,
                                                   opt_input=False, opt_output=False))
        p.emit("vector", lambda h, pb=pb: h.tensor_reduce(out=mean[:], in_=xo[pb][:],
                                                          axis=AX.X, op=ALU.add,
                                                          opt_input=False, opt_output=False))
        p.emit("vector", lambda h: h.tensor_scalar_mul(out=mean[:], in0=mean[:],
                                                       scalar1=1.0 / DM))
        p.emit("vector", lambda h: h.tensor_mul(out=msq[:], in0=mean[:], in1=mean[:]))
        p.emit("vector", lambda h: h.scalar_tensor_tensor(out=var[:], in0=ssum[:],
                                                          scalar=1.0 / DM, in1=msq[:],
                                                          op0=ALU.mult, op1=ALU.subtract))
        p.emit("scalar", lambda h: h.activation(out=var[:], in_=var[:], func=AF.Ln,
                                                bias=eps[:]),
               wait=[("SV", p.c("SV"))])
        p.emit("scalar", lambda h: h.activation(out=rstd[:], in_=var[:], func=AF.Exp,
                                                scale=-0.5))
        p.emit("vector", lambda h, pb=pb: h.tensor_scalar(out=lxo[pb][:], in0=xo[pb][:],
                                                          scalar1=mean[:], scalar2=rstd[:],
                                                          op0=ALU.subtract, op1=ALU.mult),
               wait=[("SS", p.c("SS")), ("ST", p.c("ST"))])
        p.emit("vector", lambda h, pb=pb: h.tensor_copy(out=lxob[pb][:], in_=lxo[pb][:]))
        svx = p.c("SV")
        p.dma("sync", xn[128 * t:128 * (t + 1), :], lxo[pb][:],
              wait=[("SV", svx)], inc="SO")
        for k in range(8):
            p.emit("tensor", lambda h, pb=pb, k=k: h.matmul(
                out=psc[:, k:k + 1], lhsT=lxob[pb][:, 128 * k:128 * (k + 1)],
                rhs=onesc[:], start=True, stop=True),
                wait=([("SV", svx)] if k == 0 else ()),
                inc=("ST" if k == 7 else None))
        p.emit("vector", lambda h: h.tensor_add(out=csumf[:], in0=csumf[:],
                                                in1=psc[:]),
               wait=[("ST", p.c("ST"))])
    p.emit("vector", lambda h: h.tensor_copy(out=csum[:], in_=csumf[:]))
    for k in range(8):
        p.emit("tensor", lambda h, k=k: h.matmul(
            out=psr[:], lhsT=csum[:, k:k + 1], rhs=wrs[:, 16 * k:16 * (k + 1)],
            start=(k == 0), stop=(k == 7)),
            wait=([("SV", p.c("SV")), ("GD", p.c("GD"))] if k == 0 else ()),
            inc=("ST" if k == 7 else None))
    p.emit("vector", lambda h: h.tensor_copy(out=rp[:], in_=psr[:]),
           wait=[("ST", p.c("ST"))])
    p.dma("sync", rpart[:], rp[:], wait=[("SV", p.c("SV"))], inc="SO")
    p.wait("sync", ("SO", p.c("SO")))
    return p


def build_ln1():
    p = Ph("ln1")
    xs = p.dram("xs", (SLAB, DM))
    wr = p.dram("wr", (DM, 16))
    xn = p.dram("xn", (SLAB, DM), kind="ExternalOutput")
    rpart = p.dram("rpart", (1, 16), kind="ExternalOutput")
    NT = SLAB // 128
    xt = [p.sb(f"xt{i}", (128, DM)) for i in range(2)]
    xo = [p.sb(f"xo{i}", (128, DM)) for i in range(2)]
    xob = [p.sb(f"xob{i}", (128, DM), BF16) for i in range(2)]
    sq = p.sb("sq", (128, DM))
    ssum = p.sb("ssum", (128, 1))
    msq = p.sb("msq", (128, 1))
    mean = p.sb("mean", (128, 1))
    var = p.sb("var", (128, 1))
    rstd = p.sb("rstd", (128, 1))
    eps = p.sb("eps", (128, 1))
    onesc = p.sb("onesc", (128, 1), BF16)
    wrs = p.sb("wrs", (128, 8 * 16), BF16)
    csum = p.sb("csum", (128, 8), BF16)
    csumf = p.sb("csumf", (128, 8))
    rp = p.sb("rp", (1, 16))
    psc = p.ps("psc", (128, 8))
    psr = p.ps("psr", (1, 16))
    p.emit("vector", lambda h: h.memset(eps[:], 1e-5))
    p.emit("vector", lambda h: h.memset(onesc[:], 1.0))
    p.emit("vector", lambda h: h.memset(csumf[:], 0.0))
    p.dma("gpsimd", wrs[:].rearrange("p (c n) -> p c n", c=8), wr.rearrange("(c p) n -> p c n", p=128), inc="GD")
    for i in range(NT):
        b = i % 2
        w = [("SO", 16 * (i - 1))] if i >= 2 else []
        p.dma("sync", xt[b][:], xs[128 * i:128 * (i + 1), :], wait=w, inc="LD")
        ld = p.c("LD")
        p.emit("vector", lambda h, b=b: h.tensor_mul(out=sq[:], in0=xt[b][:], in1=xt[b][:]),
               wait=[("LD", ld), ("SS", p.c("SS"))])
        p.emit("vector", lambda h: h.tensor_reduce(out=ssum[:], in_=sq[:],
                                                   axis=AX.X, op=ALU.add,
                                                   opt_input=False, opt_output=False))
        p.emit("vector", lambda h, b=b: h.tensor_reduce(out=mean[:], in_=xt[b][:],
                                                        axis=AX.X, op=ALU.add,
                                                        opt_input=False, opt_output=False))
        p.emit("vector", lambda h: h.tensor_scalar_mul(out=mean[:], in0=mean[:],
                                                       scalar1=1.0 / DM))
        p.emit("vector", lambda h: h.tensor_mul(out=msq[:], in0=mean[:], in1=mean[:]))
        p.emit("vector", lambda h: h.scalar_tensor_tensor(out=var[:], in0=ssum[:],
                                                          scalar=1.0 / DM, in1=msq[:],
                                                          op0=ALU.mult, op1=ALU.subtract))
        p.emit("scalar", lambda h: h.activation(out=var[:], in_=var[:], func=AF.Ln,
                                                bias=eps[:]),
               wait=[("SV", p.c("SV"))])
        p.emit("scalar", lambda h: h.activation(out=rstd[:], in_=var[:], func=AF.Exp,
                                                scale=-0.5))
        p.emit("vector", lambda h, b=b: h.tensor_scalar(out=xo[b][:], in0=xt[b][:],
                                                        scalar1=mean[:], scalar2=rstd[:],
                                                        op0=ALU.subtract, op1=ALU.mult),
               wait=[("SS", p.c("SS")), ("ST", p.c("ST"))])
        p.emit("vector", lambda h, b=b: h.tensor_copy(out=xob[b][:], in_=xo[b][:]))
        svx = p.c("SV")
        p.dma("sync", xn[128 * i:128 * (i + 1), :], xo[b][:],
              wait=[("SV", svx)], inc="SO")
        for k in range(8):
            p.emit("tensor", lambda h, b=b, k=k: h.matmul(
                out=psc[:, k:k + 1], lhsT=xob[b][:, 128 * k:128 * (k + 1)],
                rhs=onesc[:], start=True, stop=True),
                wait=([("SV", svx)] if k == 0 else ()),
                inc=("ST" if k == 7 else None))
        p.emit("vector", lambda h: h.tensor_add(out=csumf[:], in0=csumf[:],
                                                in1=psc[:]),
               wait=[("ST", p.c("ST"))])
    p.emit("vector", lambda h: h.tensor_copy(out=csum[:], in_=csumf[:]))
    for k in range(8):
        p.emit("tensor", lambda h, k=k: h.matmul(
            out=psr[:], lhsT=csum[:, k:k + 1], rhs=wrs[:, 16 * k:16 * (k + 1)],
            start=(k == 0), stop=(k == 7)),
            wait=([("SV", p.c("SV")), ("GD", p.c("GD"))] if k == 0 else ()),
            inc=("ST" if k == 7 else None))
    p.emit("vector", lambda h: h.tensor_copy(out=rp[:], in_=psr[:]),
           wait=[("ST", p.c("ST"))])
    p.dma("sync", rpart[:], rp[:], wait=[("SV", p.c("SV"))], inc="SO")
    p.wait("sync", ("SO", p.c("SO")))
    return p


def build_final():
    p = Ph("final")
    x1 = p.dram("x1", (SLAB, DM))
    at1 = p.dram("at1", (SLAB, DM))
    hT = p.dram("hT", (256, SLAB), BF16)
    wo = p.dram("wo", (256, DM))
    bo = p.dram("bo", (1, DM))
    out = p.dram("out", (SLAB, DM), kind="ExternalOutput")
    NT = SLAB // 128
    hs = p.sb("hs", (128, 2 * SLAB), BF16)
    ws = p.sb("ws", (128, 2 * DM), BF16)
    br = p.sb("br", (1, DM), BF16)
    ones1 = p.sb("ones1", (1, 128), BF16)
    xt = [p.sb(f"xt{i}", (128, DM)) for i in range(2)]
    at = [p.sb(f"att{i}", (128, DM)) for i in range(2)]
    xo = [p.sb(f"xo{i}", (128, DM)) for i in range(2)]
    pss = [p.ps(f"ps{i}", (128, DM)) for i in range(2)]
    p.dma("sync", hs[:].rearrange("p (c n) -> p c n", c=2), hT.rearrange("(c p) n -> p c n", p=128), inc="LD")
    p.dma("gpsimd", ws[:].rearrange("p (c n) -> p c n", c=2), wo.rearrange("(c p) n -> p c n", p=128), inc="GD")
    p.dma("gpsimd", br[:], bo[:], inc="GD")
    p.emit("vector", lambda h: h.memset(ones1[:], 1.0))
    ld0 = p.c("LD")
    gd0 = p.c("GD")
    sv0 = p.c("SV")
    for t in range(NT):
        pb = t % 2
        wso = [("SO", 16 * (t - 1))] if t >= 2 else []
        p.dma("sync", xt[pb][:], x1[128 * t:128 * (t + 1), :], wait=wso, inc="LD")
        p.dma("sync", at[pb][:], at1[128 * t:128 * (t + 1), :], inc="LD")
        ldt = p.c("LD")
        w = [("LD", ld0), ("GD", gd0), ("SV", sv0)]
        for oc in range(2):
            for k in range(2):
                p.emit("tensor", lambda h, t=t, pb=pb, oc=oc, k=k: h.matmul(
                    out=pss[pb][:, 512 * oc:512 * (oc + 1)],
                    lhsT=hs[:, k * SLAB + t * 128:k * SLAB + (t + 1) * 128],
                    rhs=ws[:, k * DM + 512 * oc:k * DM + 512 * (oc + 1)],
                    start=(k == 0), stop=False),
                    wait=(w + ([("SV", p.c("SV"))] if t >= 2 else [])
                          if k == 0 and oc == 0 else ()))
            p.emit("tensor", lambda h, pb=pb, oc=oc: h.matmul(
                out=pss[pb][:, 512 * oc:512 * (oc + 1)], lhsT=ones1[:],
                rhs=br[:, 512 * oc:512 * (oc + 1)], start=False, stop=True),
                inc="ST")
        st = p.c("ST")
        p.emit("vector", lambda h, pb=pb: h.tensor_add(out=xo[pb][:], in0=pss[pb][:],
                                                       in1=xt[pb][:]),
               wait=[("ST", st), ("LD", ldt)])
        p.emit("vector", lambda h, pb=pb: h.tensor_add(out=xo[pb][:], in0=xo[pb][:],
                                                       in1=at[pb][:]))
        p.dma("sync", out[128 * t:128 * (t + 1), :], xo[pb][:],
              wait=[("SV", p.c("SV"))], inc="SO")
    p.wait("sync", ("SO", p.c("SO")))
    return p


def build_l0attn():
    p = Ph("l0attn")
    TT = T
    xnT = p.dram("xnT", (DM, TT))
    wqkvT = p.dram("wqkvT", (DM, 768))
    bqkv = p.dram("bqkv", (768, 1))
    bqkvr = p.dram("bqkvr", (1, 768))
    masks = p.dram("masks", (4, 128, 512))
    y = p.dram("y", (256, TT), BF16, kind="ExternalOutput")

    xns = p.sb("xns", (128, 8 * TT), BF16)
    wqs = p.sb("wqs", (128, 8 * 768), BF16)
    bcol = p.sb("bcol", (128, 6))
    q_sb = p.sb("q_sb", (128, TT), BF16)
    k_sb = p.sb("k_sb", (128, TT), BF16)
    v_sb = p.sb("v_sb", (128, 64 * 64), BF16)
    msk = p.sb("msk", (128, 4 * 512))
    s_sb = [p.sb(f"s_sb{i}", (128, 512), BF16) for i in range(4)]
    o_sb = p.sb("o_sb", (64, 512))
    o_bf = p.sb("o_bf", (64, 512), BF16)
    ones1 = p.sb("ones1", (1, 128), BF16)
    brow = p.sb("brow", (1, 768), BF16)
    ps_s = [p.ps(f"ps_s{i}", (128, 512)) for i in range(4)]
    ps_oA = p.ps("ps_oA", (128, 512))
    ps_oB = p.ps("ps_oB", (128, 512))
    ps_p = p.ps("ps_p", (128, 512))
    ps_v = p.ps("ps_v", (128, 64))

    p.dma("gpsimd", xns[:].rearrange("p (c n) -> p c n", c=8),
          xnT.rearrange("(c p) n -> p c n", p=128), inc="GD")
    p.dma("gpsimd", wqs[:].rearrange("p (c n) -> p c n", c=8),
          wqkvT.rearrange("(c p) n -> p c n", p=128), inc="GD")
    p.dma("sync", msk[:].rearrange("p (c n) -> p c n", c=4),
          masks.rearrange("c p n -> p c n"), inc="LD")
    p.dma("gpsimd", brow[:], bqkvr[:], inc="GD")
    p.emit("vector", lambda h: h.memset(ones1[:], 1.0))
    gd0, ld0 = p.c("GD"), p.c("LD")

    NSTRIP = TT // 512
    for hh in range(4):
        for j, off in enumerate((hh * 64, 256 + hh * 64, 512 + hh * 64)):
            p.dma("sync", bcol[0:64, j:j + 1], bqkv[off:off + 64, :], inc="LD")
        ldb = p.c("LD")
        for s in range(NSTRIP):
            for k in range(8):
                p.emit("tensor", lambda h, s=s, k=k, hh=hh: h.matmul(
                    out=ps_p[0:64, :],
                    lhsT=wqs[:, k * 768 + hh * 64:k * 768 + hh * 64 + 64],
                    rhs=xns[:, k * TT + 512 * s:k * TT + 512 * (s + 1)],
                    start=(k == 0), stop=(k == 7)),
                    wait=([("GD", gd0), ("LD", ldb), ("SV", p.c("SV"))]
                          if k == 0 else ()), inc=("ST" if k == 7 else None))
            p.emit("vector", lambda h, s=s: h.tensor_scalar_add(
                out=q_sb[0:64, 512 * s:512 * (s + 1)], in0=ps_p[0:64, :],
                scalar1=bcol[0:64, 0:1]), wait=[("ST", p.c("ST"))])
            p.emit("vector", lambda h, s=s: h.tensor_copy(
                out=q_sb[64:128, 512 * s:512 * (s + 1)],
                in_=q_sb[0:64, 512 * s:512 * (s + 1)]))
            for k in range(8):
                p.emit("tensor", lambda h, s=s, k=k, hh=hh: h.matmul(
                    out=ps_p[0:64, :],
                    lhsT=wqs[:, k * 768 + 256 + hh * 64:k * 768 + 256 + hh * 64 + 64],
                    rhs=xns[:, k * TT + 512 * s:k * TT + 512 * (s + 1)],
                    start=(k == 0), stop=(k == 7)),
                    wait=([("SV", p.c("SV"))] if k == 0 else ()),
                    inc=("ST" if k == 7 else None))
            p.emit("vector", lambda h, s=s: h.tensor_scalar(
                out=k_sb[0:64, 512 * s:512 * (s + 1)], in0=ps_p[0:64, :],
                scalar1=bcol[0:64, 1:2], scalar2=0.125,
                op0=ALU.add, op1=ALU.mult), wait=[("ST", p.c("ST"))])
            p.emit("vector", lambda h, s=s: h.tensor_copy(
                out=k_sb[64:128, 512 * s:512 * (s + 1)],
                in_=k_sb[0:64, 512 * s:512 * (s + 1)]))
        for t in range(TT // 128):
            for k in range(8):
                p.emit("tensor", lambda h, t=t, k=k, hh=hh: h.matmul(
                    out=ps_v[:, :],
                    lhsT=xns[:, k * TT + 128 * t:k * TT + 128 * (t + 1)],
                    rhs=wqs[:, k * 768 + 512 + hh * 64:k * 768 + 512 + hh * 64 + 64],
                    start=(k == 0), stop=False),
                    wait=([("SV", p.c("SV"))] if k == 0 else ()))
            p.emit("tensor", lambda h, t=t, hh=hh: h.matmul(
                out=ps_v[:, :], lhsT=ones1[:],
                rhs=brow[:, 512 + hh * 64:512 + hh * 64 + 64],
                start=False, stop=True),
                inc="ST")
            p.emit("vector", lambda h, t=t: h.tensor_copy(
                out=v_sb[:, 64 * t:64 * (t + 1)], in_=ps_v[:, :]),
                wait=[("ST", p.c("ST"))])
        for s in range(NSTRIP):
            nkb = 4 * s + 4
            evac_cnt = {}
            evac_eng = {}
            for step in range(nkb + 3):
                kb = step
                if kb < nkb:
                    buf = kb % 4
                    wv = []
                    if kb >= 4:
                        e4, c4 = evac_eng[kb - 4], evac_cnt[kb - 4]
                        wv.append(("SV" if e4 == "vector" else "SS", c4))
                    if kb == 0:
                        wv += [("SV", p.c("SV")), ("SS", p.c("SS"))]
                    p.emit("tensor", lambda h, kb=kb, s=s, buf=buf: h.matmul(
                        out=ps_s[buf][0:128, :],
                        lhsT=k_sb[0:64, 128 * kb:128 * (kb + 1)],
                        rhs=q_sb[0:64, 512 * s:512 * (s + 1)],
                        start=True, stop=True),
                        wait=wv, inc="ST")
                    st = p.c("ST")
                    rel = kb - 4 * s
                    if rel >= 0:
                        evac_cnt[kb] = p.emit("vector", lambda h, buf=buf, rel=rel: h.tensor_mul(
                            out=s_sb[buf][:], in0=ps_s[buf][:],
                            in1=msk[:, 512 * rel:512 * (rel + 1)]),
                            wait=[("ST", st)])
                        evac_eng[kb] = "vector"
                    elif kb % 2 == 0:
                        evac_cnt[kb] = p.emit("vector", lambda h, buf=buf: h.tensor_copy(
                            out=s_sb[buf][:], in_=ps_s[buf][:]),
                            wait=[("ST", st)])
                        evac_eng[kb] = "vector"
                    else:
                        evac_cnt[kb] = p.emit("scalar", lambda h, buf=buf: h.copy(
                            out=s_sb[buf][:], in_=ps_s[buf][:]),
                            wait=[("ST", st)])
                        evac_eng[kb] = "scalar"
                kbav = step - 3
                if 0 <= kbav < nkb:
                    half = kbav % 2
                    po = ps_oA if half == 0 else ps_oB
                    e, cnt = evac_eng[kbav], evac_cnt[kbav]
                    p.emit("tensor", lambda h, kbav=kbav, half=half, po=po, nkb=nkb: h.matmul(
                        out=po[64 * half:64 * (half + 1), :],
                        lhsT=v_sb[:, 64 * kbav:64 * (kbav + 1)],
                        rhs=s_sb[kbav % 4][:],
                        start=(kbav < 2), stop=(kbav >= nkb - 2),
                        tile_position=(0, 64 * half)),
                        wait=[("SV" if e == "vector" else "SS", cnt)], inc="ST")
            st = p.c("ST")
            p.emit("vector", lambda h: h.tensor_copy(out=o_sb[:], in_=ps_oB[64:128, :]),
                   wait=[("ST", st)])
            p.emit("vector", lambda h: h.tensor_add(out=o_bf[:], in0=ps_oA[0:64, :],
                                                    in1=o_sb[:]))
            p.dma("sync", y[64 * hh:64 * (hh + 1), 512 * s:512 * (s + 1)], o_bf[:],
                  wait=[("SV", p.c("SV"))], inc="SO")
    p.wait("sync", ("SO", p.c("SO")))
    return p


def _xw(p, *sems):
    return [(s, p.c(s)) for s in sems]


def _scan16(p, out_t, data1_t, ones_t, wait0):
    for pc in range(16):
        p.emit("vector", lambda h, pc=pc: h.tensor_tensor_scan(
            out=out_t[:, 128 * pc:128 * (pc + 1)],
            data0=ones_t[:], data1=data1_t[:, 128 * pc:128 * (pc + 1)],
            initial=0.0, op0=ALU.mult, op1=ALU.add),
            wait=(wait0 if pc == 0 else ()))


def build_l1(recon):
    p = Ph("l1rec" if recon else "l1a")
    xn1T = p.dram("xn1T", (DM, SLAB))
    winxiT = p.dram("winxiT", (DM, DI))
    wlT = p.dram("wlT", (DI, DI))
    wvT = p.dram("wvT", (DI, DI))
    omc = p.dram("omc", (128, 16))
    blc = p.dram("blc", (128, 16))
    bvc = p.dram("bvc", (128, 16))
    m0col = p.dram("m0col", (128, 1))
    consts = p.dram("consts", (9, 128, 128))
    if recon:
        carries = p.dram("carries", (NCH_CORE, 128, 16))
        winzT = p.dram("winzT", (DM, DI))
        woTq = p.dram("woTq", (DI, DM))
        at1 = p.dram("at1", (SLAB, DM), kind="ExternalOutput")
    else:
        aGt_d = p.dram("aGt", (NCH_CORE, 128, DI), BF16, kind="ExternalOutput")
        bfm_d = p.dram("bfm", (NCH_CORE, 128, DI), BF16, kind="ExternalOutput")
        TU_d = p.dram("TU", (NCH_CORE, 128, 128), kind="ExternalOutput")
        Gd_d = p.dram("Gd", (NCH_CORE, 128, 16), kind="ExternalOutput")
        dv_d = p.dram("dv", (NCH_CORE, 128, 16), kind="ExternalOutput")
        a_d = p.dram("a_d", (NCH_CORE, 128, DI), BF16, kind="ExternalOutput")
        E_d = p.dram("E_d", (NCH_CORE, 128, DI), BF16, kind="ExternalOutput")
        LI_d = p.dram("LI_d", (NCH_CORE, 128, DI), BF16, kind="ExternalOutput")
        mc_d = p.dram("mc_d", (NCH_CORE, 128, 1), kind="ExternalOutput")
    Ldram = p.dram("Lscr", (NCH_CORE, 128, DI), BF16, kind="Internal")
    vdram = p.dram("vscr", (NCH_CORE, 128, DI), BF16, kind="Internal")

    xnb = p.sb("xnb", (128, 8 * 512), BF16)
    sst4 = p.sb("sst4", (128, 4 * DI), BF16)
    wbuf = p.sb("wbuf", (128, 16 * DI), BF16)
    xi = p.sb("xi", (128, NCH_CORE * DI), BF16)
    cst = p.sb("cst", (128, 9 * 128))
    cstb = p.sb("cstb", (128, 128), BF16)
    som = p.sb("som", (128, 16))
    blc_s = p.sb("blc_s", (128, 16))
    bvc_s = p.sb("bvc_s", (128, 16))
    m0 = p.sb("m0", (128, 1))
    Lc = p.sb("Lc", (128, DI), BF16)
    CumI = sst4[:].bitcast(F32)[:, 0:DI]
    Ef = sst4[:].bitcast(F32)[:, DI:2 * DI]
    tmpf = p.sb("tmpf", (128, DI))
    lamt = p.sb("lamt", (128, DI), BF16)
    LEb = p.sb("LEb", (128, DI), BF16)
    LEi = p.sb("LEi", (128, DI), BF16)
    vcb = p.sb("vcb", (128, DI), BF16)
    vnb = p.sb("vnb", (128, DI), BF16)
    a_b = p.sb("a_b", (128, DI), BF16)
    b_b = p.sb("b_b", (128, DI), BF16)
    LIi = LEi
    e_b = lamt
    Gf = p.sb("Gf", (128, 16))
    rnb = p.sb("rnb", (128, 128))
    rnr = p.sb("rnr", (1, 128))
    onesc = p.sb("onesc", (128, 1))
    mrow = p.sb("mrow", (1, 128))
    mcol = p.sb("mcol", (128, 1))
    blocf = p.sb("blocf", (128, 1))
    blocb = p.sb("blocb", (128, 1), BF16)
    names = ("SL SU PL PU Q2L Q2U Q4L Q4U TLa TUa TLb TUb MLa MUa MLb MUb "
             "CLt CUt").split()
    hier = {n: p.sb(n, (128, 128)) for n in names}
    TUf = tmpf[:, 0:128]
    if recon:
        g0 = p.sb("g0", (128, 16))
        g0b = p.sb("g0b", (128, 16), BF16)
        qc = p.sb("qc", (128, 1))
        brow = p.sb("brow", (1, 128))
        brb = p.sb("brb", (128, 128))
        CumW = tmpf
        zb = vnb
        xoA = tmpf
        Wt = a_b
        ub = zb
    else:
        aG = vnb
        aGtk = vcb
        dvf = p.sb("dvf", (128, 16))
    psA = p.ps("psA", (128, 512))
    psB = p.ps("psB", (128, 128))
    psC = p.ps("psC", (128, 128))
    psR = p.ps("psR", (1, 128))
    psT = p.ps("psT", (128, 128), BF16)
    psM = p.ps("psM", (128, 16))

    CS = lambda i: cst[:, 128 * i:128 * (i + 1)]
    STRIL, IDENT, M8, D16, D32, D64, D128, ONES, STRIU = range(9)

    p.dma("sync", cst[:].rearrange("p (c n) -> p c n", c=9),
          consts.rearrange("c p n -> p c n"), inc="LD")
    p.dma("sync", som[:], omc[:], inc="LD")
    p.dma("sync", blc_s[:], blc[:], inc="LD")
    p.dma("sync", bvc_s[:], bvc[:], inc="LD")
    p.dma("sync", m0[:], m0col[:], inc="LD")
    p.dma("gpsimd", wbuf[:, :8 * DI].rearrange("p (c n) -> p c n", c=8),
          winxiT.rearrange("(c p) n -> p c n", p=128), inc="GD")
    p.emit("vector", lambda h: h.memset(onesc[:], 1.0))
    p.emit("vector", lambda h: h.tensor_copy(out=cstb[:], in_=CS(IDENT)),
           wait=_xw(p, "LD"))
    sp1 = p.sb("sp1", (128, 16))
    sp2 = p.sb("sp2", (128, 16))
    p.emit("scalar", lambda h: h.activation(out=sp1[:], in_=som[:], func=AF.Relu),
           wait=_xw(p, "LD"))
    p.emit("scalar", lambda h: h.activation(out=sp2[:], in_=som[:], func=AF.Abs))
    p.emit("scalar", lambda h: h.activation(out=sp2[:], in_=sp2[:], func=AF.Exp,
                                            scale=-1.0))
    p.emit("scalar", lambda h: h.activation(out=sp2[:], in_=sp2[:], func=AF.Ln,
                                            bias=1.0))
    p.emit("vector", lambda h: h.tensor_add(out=som[:], in0=sp1[:], in1=sp2[:]),
           wait=_xw(p, "SS"))
    p.emit("vector", lambda h: h.tensor_scalar_mul(out=som[:], in0=som[:],
                                                   scalar1=-8.0))

    for sc in range(NCH_CORE // 4):
        p.dma("gpsimd", xnb[:].rearrange("p (c n) -> p c n", c=8),
              xn1T.rearrange("(c p) n -> p c n", p=128)[:, :, 512 * sc:512 * (sc + 1)],
              wait=[("ST", p.c("ST"))], inc="GD")
        for es in range(16):
            for k in range(8):
                p.emit("tensor", lambda h, es=es, k=k: h.matmul(
                    out=psA[:],
                    lhsT=wbuf[:, k * DI + 128 * es:k * DI + 128 * (es + 1)],
                    rhs=xnb[:, k * 512:(k + 1) * 512],
                    start=(k == 0), stop=(k == 7)),
                    wait=(_xw(p, "GD", "LD", "SV") if k == 0 else ()),
                    inc=("ST" if k == 7 else None))
            p.emit("vector", lambda h, sc=sc, es=es: h.tensor_copy(
                out=xi[:].rearrange("p (c n) -> p c n", c=NCH_CORE)
                [:, 4 * sc:4 * sc + 4, 128 * es:128 * (es + 1)],
                in_=psA[:].rearrange("p (c n) -> p c n", c=4)),
                wait=[("ST", p.c("ST"))])

    for dram_w, outdram, isL in ((wlT, Ldram, True), (wvT, vdram, False)):
        p.dma("gpsimd", wbuf[:].rearrange("p (c n) -> p c n", c=16),
              dram_w.rearrange("(c p) n -> p c n", p=128),
              wait=_xw(p, "SV", "ST"), inc="GD")
        for sc in range(NCH_CORE // 4):
            wso = [("SO", p.c("SO"))]
            for js in range(16):
                for k in range(16):
                    p.emit("tensor", lambda h, sc=sc, js=js, k=k: h.matmul(
                        out=psA[:],
                        lhsT=wbuf[:, k * DI + 128 * js:k * DI + 128 * (js + 1)],
                        rhs=xi[:].rearrange("p (c n) -> p c n", c=NCH_CORE)
                        [:, 4 * sc:4 * sc + 4, 128 * k:128 * (k + 1)],
                        start=(k == 0), stop=(k == 15)),
                        wait=(_xw(p, "GD", "SV") if k == 0 else ()),
                        inc=("ST" if k == 15 else None))
                st = p.c("ST")
                if isL:
                    p.emit("scalar", lambda h, js=js: h.activation(
                        out=tmpf[:, 0:512], in_=psA[:],
                        func=AF.Sigmoid, bias=blc_s[:, js:js + 1]),
                        wait=[("ST", st)])
                    p.emit("vector", lambda h, js=js: h.tensor_scalar_mul(
                        out=sst4[:].rearrange("p (c n) -> p c n", c=4)[:, :, 128 * js:128 * (js + 1)],
                        in0=tmpf[:, 0:512].rearrange("p (c n) -> p c n", c=4),
                        scalar1=som[:, js:js + 1]),
                        wait=_xw(p, "SS") + (wso if js == 0 else []))
                else:
                    p.emit("vector", lambda h, js=js: h.tensor_scalar_add(
                        out=sst4[:].rearrange("p (c n) -> p c n", c=4)[:, :, 128 * js:128 * (js + 1)],
                        in0=psA[:].rearrange("p (c n) -> p c n", c=4),
                        scalar1=bvc_s[:, js:js + 1]),
                        wait=[("ST", st)] + (wso if js == 0 else []))
            for j4 in range(4):
                tgt = (Ldram if isL else vdram)[4 * sc + j4]
                if isL:
                    p.dma("sync", tgt, sst4[:, 2048 * j4:2048 * (j4 + 1)],
                          wait=_xw(p, "SV"), inc="SO")
                else:
                    p.dma("gpsimd", tgt, sst4[:, 2048 * j4:2048 * (j4 + 1)],
                          wait=_xw(p, "SV"), inc="GD")

    for ch in range(NCH_CORE):
        ch0 = (ch == 0)
        p.dma("sync", Lc[:], Ldram[ch], wait=_xw(p, "SV", "SS", "SO"), inc="LD")
        p.dma("sync", vcb[:], vdram[ch], inc="LD")
        if recon:
            p.dma("sync", g0[:], carries[ch], inc="LD")
        ldc = p.c("LD")
        p.wait("vector", ("LD", ldc))
        if ch0:
            p.emit("vector", lambda h: h.tensor_scalar_mul(
                out=Lc[:].rearrange("p (c n) -> p c n", c=16)[:, :, 0:1],
                in0=Lc[:].rearrange("p (c n) -> p c n", c=16)[:, :, 0:1], scalar1=m0[:]))
        _scan16(p, CumI, Lc, CS(ONES), [])
        p.emit("vector", lambda h: h.tensor_sub(out=tmpf[:], in0=CumI, in1=Lc[:]))
        p.emit("scalar", lambda h: h.activation(out=LEb[:], in_=tmpf[:], func=AF.Exp),
               wait=_xw(p, "SV"))
        p.emit("scalar", lambda h: h.activation(out=LEi[:], in_=tmpf[:], func=AF.Exp,
                                                scale=-1.0))
        p.emit("scalar", lambda h: h.activation(out=lamt[:], in_=Lc[:], func=AF.Exp))
        p.emit("scalar", lambda h: h.activation(
            out=Gf[:].unsqueeze(2),
            in_=CumI.rearrange("p (c n) -> p c n", c=16)[:, :, 127:128],
            func=AF.Exp))
        p.emit("scalar", lambda h: h.activation(out=tmpf[:], in_=vcb[:], func=AF.Square))
        for pc in range(16):
            p.emit("tensor", lambda h, pc=pc: h.matmul(
                out=psR[:], lhsT=onesc[:], rhs=tmpf[:, 128 * pc:128 * (pc + 1)],
                start=(pc == 0), stop=(pc == 15)),
                wait=(_xw(p, "SS", "SV") if pc == 0 else ()),
                inc=("ST" if pc == 15 else None))
        p.emit("vector", lambda h: h.tensor_copy(out=rnr[:], in_=psR[:]),
               wait=[("ST", p.c("ST"))])
        p.emit("scalar", lambda h: h.activation(out=rnr[:], in_=rnr[:], func=AF.Ln),
               wait=_xw(p, "SV"))
        p.emit("scalar", lambda h: h.activation(out=rnr[:], in_=rnr[:], func=AF.Exp,
                                                scale=-0.5))
        p.emit("tensor", lambda h: h.matmul(out=psC[:], lhsT=CS(ONES)[0:1, :],
                                            rhs=rnr[:], start=True, stop=True),
               wait=_xw(p, "SS"), inc="ST")
        p.emit("vector", lambda h: h.tensor_copy(out=rnb[:], in_=psC[:]),
               wait=[("ST", p.c("ST"))])
        p.emit("vector", lambda h: h.tensor_tensor(
            out=vnb[:].rearrange("p (c n) -> p c n", c=16),
            in0=vcb[:].rearrange("p (c n) -> p c n", c=16),
            in1=rnb[:].unsqueeze(1).broadcast_to((128, 16, 128)),
            op=ALU.mult))
        if ch0:
            p.emit("vector", lambda h: h.tensor_scalar_mul(
                out=vnb[:].rearrange("p (c n) -> p c n", c=16)[:, :, 0:1],
                in0=vnb[:].rearrange("p (c n) -> p c n", c=16)[:, :, 0:1], scalar1=m0[:]))
            p.emit("vector", lambda h: h.tensor_scalar_mul(
                out=lamt[:].rearrange("p (c n) -> p c n", c=16)[:, :, 0:1],
                in0=lamt[:].rearrange("p (c n) -> p c n", c=16)[:, :, 0:1], scalar1=m0[:]),
                wait=_xw(p, "SS"))
        p.emit("vector", lambda h: h.tensor_mul(out=a_b[:], in0=vnb[:], in1=LEi[:]),
               wait=_xw(p, "SS"))
        p.emit("vector", lambda h: h.tensor_mul(out=b_b[:], in0=vnb[:], in1=LEb[:]))
        p.emit("scalar", lambda h: h.activation(out=LIi[:], in_=CumI, func=AF.Exp,
                                                scale=-1.0), wait=_xw(p, "SV"))
        p.emit("vector", lambda h: h.tensor_scalar(out=lamt[:], in0=lamt[:],
                                                   scalar1=-1.0, scalar2=1.0,
                                                   op0=ALU.mult, op1=ALU.add))
        p.emit("vector", lambda h, ch=ch: h.tensor_mul(out=lamt[:], in0=lamt[:],
                                                       in1=xi[:, ch * DI:(ch + 1) * DI]))
        p.emit("vector", lambda h: h.tensor_mul(out=e_b[:], in0=lamt[:], in1=LIi[:]),
               wait=_xw(p, "SS"))
        _scan16(p, Ef, e_b, CS(ONES), [])
        p.emit("vector", lambda h: h.tensor_sub(out=tmpf[:], in0=Ef, in1=e_b[:]))
        p.emit("vector", lambda h: h.tensor_mul(out=tmpf[:], in0=b_b[:], in1=tmpf[:]))
        for pc in range(16):
            p.emit("tensor", lambda h, pc=pc: h.matmul(
                out=psR[:], lhsT=onesc[:], rhs=tmpf[:, 128 * pc:128 * (pc + 1)],
                start=(pc == 0), stop=(pc == 15)),
                wait=(_xw(p, "SV") if pc == 0 else ()),
                inc=("ST" if pc == 15 else None))
        p.emit("vector", lambda h: h.tensor_copy(out=mrow[:], in_=psR[:]),
               wait=[("ST", p.c("ST"))])
        for pc in range(16):
            p.emit("tensor", lambda h, pc=pc: h.matmul(
                out=psB[:], lhsT=b_b[:, 128 * pc:128 * (pc + 1)],
                rhs=a_b[:, 128 * pc:128 * (pc + 1)],
                start=(pc == 0), stop=(pc == 15)),
                wait=(_xw(p, "SV") if pc == 0 else ()),
                inc=("ST" if pc == 15 else None))
        st = p.c("ST")
        p.emit("vector", lambda h: h.scalar_tensor_tensor(
            out=hier["SL"][:], in0=psB[:], scalar=2.0, in1=CS(STRIL),
            op0=ALU.mult, op1=ALU.mult), wait=[("ST", st)])
        for pc in range(16):
            p.emit("tensor", lambda h, pc=pc: h.matmul(
                out=psC[:], lhsT=a_b[:, 128 * pc:128 * (pc + 1)],
                rhs=b_b[:, 128 * pc:128 * (pc + 1)],
                start=(pc == 0), stop=(pc == 15)),
                wait=(_xw(p, "SV") if pc == 0 else ()),
                inc=("ST" if pc == 15 else None))
        st = p.c("ST")
        p.emit("vector", lambda h: h.scalar_tensor_tensor(
            out=hier["SU"][:], in0=psC[:], scalar=2.0, in1=CS(STRIU),
            op0=ALU.mult, op1=ALU.mult), wait=[("ST", st)])

        def mmT(dst, lhsT_t, rhs_t, mask=None, add_t=None, sub=False):
            p.emit("tensor", lambda h: h.matmul(out=psB[:], lhsT=lhsT_t, rhs=rhs_t,
                                                start=True, stop=True),
                   wait=_xw(p, "SV"), inc="ST")
            st = p.c("ST")
            if add_t is None:
                if mask is None:
                    p.emit("vector", lambda h: h.tensor_copy(out=dst, in_=psB[:]),
                           wait=[("ST", st)])
                else:
                    p.emit("vector", lambda h: h.tensor_mul(out=dst, in0=psB[:],
                                                            in1=mask),
                           wait=[("ST", st)])
            elif sub:
                p.emit("vector", lambda h: h.tensor_sub(out=dst, in0=add_t,
                                                        in1=psB[:]),
                       wait=[("ST", st)])
            else:
                p.emit("vector", lambda h: h.tensor_add(out=dst, in0=add_t,
                                                        in1=psB[:]),
                       wait=[("ST", st)])

        H = hier
        p.emit("vector", lambda h: h.tensor_mul(out=H["PL"][:], in0=H["SL"][:], in1=CS(M8)))
        p.emit("vector", lambda h: h.tensor_mul(out=H["PU"][:], in0=H["SU"][:], in1=CS(M8)))
        mmT(H["Q2L"][:], H["PU"][:], H["PL"][:])
        mmT(H["Q2U"][:], H["PL"][:], H["PU"][:])
        p.emit("vector", lambda h: h.tensor_sub(out=H["TLa"][:], in0=cstb[:], in1=H["PL"][:]))
        p.emit("vector", lambda h: h.tensor_add(out=H["TLa"][:], in0=H["TLa"][:], in1=H["Q2L"][:]))
        mmT(H["TLa"][:], H["PU"][:], H["Q2L"][:], add_t=H["TLa"][:], sub=True)
        p.emit("vector", lambda h: h.tensor_sub(out=H["TUa"][:], in0=cstb[:], in1=H["PU"][:]))
        p.emit("vector", lambda h: h.tensor_add(out=H["TUa"][:], in0=H["TUa"][:], in1=H["Q2U"][:]))
        mmT(H["TUa"][:], H["Q2L"][:], H["PU"][:], add_t=H["TUa"][:], sub=True)
        mmT(H["Q4L"][:], H["Q2U"][:], H["Q2L"][:])
        mmT(H["Q4U"][:], H["Q2L"][:], H["Q2U"][:])
        mmT(H["TLb"][:], H["TUa"][:], H["Q4L"][:], add_t=H["TLa"][:])
        mmT(H["TUb"][:], H["Q4L"][:], H["TUa"][:], add_t=H["TUa"][:])
        p.emit("vector", lambda h: h.tensor_mul(out=H["TLb"][:], in0=H["TLb"][:], in1=CS(M8)))
        p.emit("vector", lambda h: h.tensor_mul(out=H["TUb"][:], in0=H["TUb"][:], in1=CS(M8)))
        mmT(H["MLa"][:], H["TUb"][:], H["SL"][:], add_t=H["TLb"][:])
        mmT(H["MUa"][:], H["SL"][:], H["TUb"][:], add_t=H["TUb"][:])
        TL_cur, TU_cur = H["TLb"], H["TUb"]
        ML_cur, MU_cur = H["MLa"], H["MUa"]
        ML_alt, MU_alt = H["MLb"], H["MUb"]
        TL_alt, TU_alt = H["TLa"], H["TUa"]
        for li, dmask in enumerate((D16, D32, D64, D128)):
            p.emit("vector", lambda h, ML_cur=ML_cur, dmask=dmask: h.tensor_mul(
                out=H["CLt"][:], in0=ML_cur[:], in1=CS(dmask)))
            p.emit("vector", lambda h, MU_cur=MU_cur, dmask=dmask: h.tensor_mul(
                out=H["CUt"][:], in0=MU_cur[:], in1=CS(dmask)))
            mmT(TL_alt[:], H["CUt"][:], TL_cur[:], add_t=TL_cur[:], sub=True)
            mmT(TU_alt[:], TL_cur[:], H["CUt"][:], add_t=TU_cur[:], sub=True)
            if li < 3:
                mmT(ML_alt[:], H["CUt"][:], ML_cur[:], add_t=ML_cur[:], sub=True)
                mmT(MU_alt[:], ML_cur[:], H["CUt"][:], add_t=MU_cur[:], sub=True)
                ML_cur, ML_alt = ML_alt, ML_cur
                MU_cur, MU_alt = MU_alt, MU_cur
            TL_cur, TL_alt = TL_alt, TL_cur
            TU_cur, TU_alt = TU_alt, TU_cur
        p.emit("vector", lambda h, TU_cur=TU_cur: h.tensor_copy(out=TUf,
                                                                in_=TU_cur[:]))
        p.emit("tensor", lambda h: h.transpose(out=psC[0:128, 0:1], in_=mrow[:],
                                               identity=CS(ONES)[0:1, 0:1]),
               wait=_xw(p, "SV"), inc="ST")
        p.emit("vector", lambda h: h.tensor_copy(out=mcol[:], in_=psC[0:128, 0:1]),
               wait=[("ST", p.c("ST"))])
        if recon:
            p.emit("vector", lambda h: h.tensor_copy(out=g0b[:], in_=g0[:]))
            for pc in range(16):
                p.emit("tensor", lambda h, pc=pc: h.matmul(
                    out=psC[:, 1:2], lhsT=b_b[:, 128 * pc:128 * (pc + 1)],
                    rhs=g0b[:, pc:pc + 1], start=(pc == 0), stop=(pc == 15)),
                    wait=(_xw(p, "SV") if pc == 0 else ()),
                    inc=("ST" if pc == 15 else None))
            p.emit("vector", lambda h: h.tensor_copy(out=qc[:], in_=psC[:, 1:2]),
                   wait=[("ST", p.c("ST"))])
            p.emit("vector", lambda h: h.tensor_add(out=mcol[:], in0=mcol[:], in1=qc[:]))
        p.emit("tensor", lambda h: h.matmul(out=psC[:, 0:1], lhsT=TUf, rhs=mcol[:],
                                            start=True, stop=True),
               wait=_xw(p, "SV"), inc="ST")
        p.emit("vector", lambda h: h.tensor_copy(out=blocf[:], in_=psC[:, 0:1]),
               wait=[("ST", p.c("ST"))])
        p.emit("vector", lambda h: h.tensor_copy(out=blocb[:], in_=blocf[:]))
        if not recon:
            for pc in range(16):
                p.emit("vector", lambda h, pc=pc: h.tensor_scalar_mul(
                    out=aG[:, 128 * pc:128 * (pc + 1)],
                    in0=a_b[:, 128 * pc:128 * (pc + 1)],
                    scalar1=Gf[:, pc:pc + 1]))
            for pc in range(16):
                p.emit("tensor", lambda h, pc=pc: h.transpose(
                    out=psT[:], in_=aG[:, 128 * pc:128 * (pc + 1)], identity=cstb[:]),
                    wait=[("SV", p.c("SV"))], inc="ST")
                p.emit("vector", lambda h, pc=pc: h.tensor_copy(
                    out=aGtk[:, 128 * pc:128 * (pc + 1)], in_=psT[:]),
                    wait=[("ST", p.c("ST"))])
            for pc in range(16):
                p.emit("tensor", lambda h, pc=pc: h.matmul(
                    out=psM[:, pc:pc + 1], lhsT=aGtk[:, 128 * pc:128 * (pc + 1)],
                    rhs=blocb[:], start=(pc == 0), stop=(pc == 15)),
                    wait=(_xw(p, "SV") if pc == 0 else ()),
                    inc=("ST" if pc == 15 else None))
            p.emit("vector", lambda h: h.tensor_tensor(
                out=dvf[:].unsqueeze(2),
                in0=Ef.rearrange("p (c n) -> p c n", c=16)[:, :, 127:128],
                in1=Gf[:].unsqueeze(2), op=ALU.mult),
                wait=[("ST", p.c("ST"))])
            p.emit("vector", lambda h: h.scalar_tensor_tensor(
                out=dvf[:], in0=psM[:], scalar=-2.0, in1=dvf[:],
                op0=ALU.mult, op1=ALU.add))
            p.emit("vector", lambda h: h.tensor_copy(out=LEb[:], in_=Ef))
            p.emit("scalar", lambda h: h.activation(out=LEi[:], in_=CumI,
                                                    func=AF.Exp),
                   wait=_xw(p, "SV"))
            p.wait("vector", ("SS", p.c("SS")))
            sv = p.c("SV")
            p.dma("sync", aGt_d[ch], aGtk[:], wait=[("SV", sv), ("SS", p.c("SS"))],
                  inc="SO")
            p.dma("sync", bfm_d[ch], b_b[:], inc="SO")
            p.dma("sync", TU_d[ch], TUf, inc="SO")
            p.dma("sync", Gd_d[ch], Gf[:], inc="SO")
            p.dma("sync", dv_d[ch], dvf[:], inc="SO")
            p.dma("sync", a_d[ch], a_b[:], inc="SO")
            p.dma("sync", E_d[ch], LEb[:], inc="SO")
            p.dma("sync", LI_d[ch], LEi[:], inc="SO")
            p.dma("sync", mc_d[ch], mcol[:], inc="SO")
        else:
            p.emit("tensor", lambda h: h.transpose(out=psC[0:1, 0:128],
                                                   in_=blocf[:],
                                                   identity=CS(IDENT)),
                   wait=_xw(p, "SV"), inc="ST")
            p.emit("vector", lambda h: h.tensor_copy(out=brow[:], in_=psC[0:1, 0:128]),
                   wait=[("ST", p.c("ST"))])
            p.emit("tensor", lambda h: h.matmul(out=psC[:], lhsT=CS(ONES)[0:1, :],
                                                rhs=brow[:], start=True, stop=True),
                   wait=_xw(p, "SV"), inc="ST")
            p.emit("vector", lambda h: h.tensor_copy(out=brb[:], in_=psC[:]),
                   wait=[("ST", p.c("ST"))])
            p.emit("vector", lambda h: h.tensor_tensor(
                out=Wt[:].rearrange("p (c n) -> p c n", c=16),
                in0=a_b[:].rearrange("p (c n) -> p c n", c=16),
                in1=brb[:].unsqueeze(1).broadcast_to((128, 16, 128)),
                op=ALU.mult))
            _scan16(p, CumW, Wt, CS(ONES), [])
            p.emit("vector", lambda h: h.scalar_tensor_tensor(
                out=CumW[:], in0=CumW[:], scalar=-2.0, in1=Ef,
                op0=ALU.mult, op1=ALU.add))
            for pc in range(16):
                p.emit("vector", lambda h, pc=pc: h.tensor_scalar_add(
                    out=CumW[:, 128 * pc:128 * (pc + 1)],
                    in0=CumW[:, 128 * pc:128 * (pc + 1)],
                    scalar1=g0[:, pc:pc + 1]))
            p.emit("scalar", lambda h: h.activation(out=LEb[:], in_=CumI,
                                                    func=AF.Exp), wait=_xw(p, "SV"))
            p.emit("vector", lambda h, ch=ch: h.tensor_mul(
                out=xi[:, ch * DI:(ch + 1) * DI], in0=CumW[:], in1=LEb[:]),
                wait=_xw(p, "SS"))
    if recon:
        p.dma("gpsimd", wbuf[:, :8 * DI].rearrange("p (c n) -> p c n", c=8),
              winzT.rearrange("(c p) n -> p c n", p=128),
              wait=_xw(p, "SV", "ST"), inc="GD")
        p.dma("gpsimd", wbuf[:, 8 * DI:].rearrange("p (c n) -> p c n", c=16),
              woTq.rearrange("(c p) n -> p c n", p=128), inc="GD")
        for ch in range(NCH_CORE):
            wso = [("SO", p.c("SO"))]
            p.dma("gpsimd", xnb[:].rearrange("p (c n) -> p c n", c=8),
                  xn1T.rearrange("(c p) n -> p c n", p=128)[:, :, 128 * ch:128 * (ch + 1)],
                  wait=[("ST", p.c("ST"))], inc="GD")
            for es in range(16):
                for k in range(8):
                    p.emit("tensor", lambda h, es=es, k=k: h.matmul(
                        out=psB[:],
                        lhsT=wbuf[:, k * DI + 128 * es:k * DI + 128 * (es + 1)],
                        rhs=xnb[:, k * 128:(k + 1) * 128],
                        start=(k == 0), stop=(k == 7)),
                        wait=(_xw(p, "GD", "SV") if k == 0 else ()),
                        inc=("ST" if k == 7 else None))
                p.emit("vector", lambda h, es=es: h.tensor_copy(
                    out=zb[:, 128 * es:128 * (es + 1)], in_=psB[:]),
                    wait=[("ST", p.c("ST"))])
            p.emit("scalar", lambda h: h.activation(out=zb[:], in_=zb[:], func=AF.Silu),
                   wait=_xw(p, "SV"))
            p.emit("vector", lambda h, ch=ch: h.tensor_mul(
                out=ub[:], in0=zb[:], in1=xi[:, ch * DI:(ch + 1) * DI]),
                wait=_xw(p, "SS"))
            for oc in range(2):
                for js in range(16):
                    p.emit("tensor", lambda h, js=js, oc=oc: h.matmul(
                        out=psA[:],
                        lhsT=ub[:, 128 * js:128 * (js + 1)],
                        rhs=wbuf[:, 8 * DI + js * DM + 512 * oc:
                                 8 * DI + js * DM + 512 * (oc + 1)],
                        start=(js == 0), stop=(js == 15)),
                        wait=(_xw(p, "SV") if js == 0 else ()),
                        inc=("ST" if js == 15 else None))
                p.emit("vector", lambda h, oc=oc: h.tensor_copy(
                    out=xoA[:, 512 * oc:512 * (oc + 1)], in_=psA[:]),
                    wait=[("ST", p.c("ST"))] + (wso if oc == 0 else []))
            p.dma("sync", at1[128 * ch:128 * (ch + 1), :], xoA[:, 0:DM],
                  wait=_xw(p, "SV"), inc="SO")
    p.wait("sync", ("SO", p.c("SO")))
    return p


def build_l1a():
    return build_l1(False)


def build_l1rec():
    return build_l1(True)


def build_carry():
    p = Ph("carry")
    aGt = p.dram("aGt", (NCH, 128, DI), BF16)
    bfm = p.dram("bfm", (NCH, 128, DI), BF16)
    TUc = p.dram("TUc", (NCH, 128, 128))
    Gc = p.dram("Gc", (NCH, 128, 16))
    dvc = p.dram("dvc", (NCH, 128, 16))
    carr = p.dram("carr", (NCH, 128, 16), kind="ExternalOutput")
    ag = [p.sb(f"ag{i}", (128, DI), BF16) for i in range(2)]
    bf_ = [p.sb(f"bf{i}", (128, DI), BF16) for i in range(2)]
    tu = [p.sb(f"tu{i}", (128, 128)) for i in range(2)]
    gg = [p.sb(f"gg{i}", (128, 16)) for i in range(2)]
    dv_ = [p.sb(f"dv{i}", (128, 16)) for i in range(2)]
    hh = p.sb("hh", (128, 16))
    hbb = p.sb("hbb", (128, 16), BF16)
    qc = p.sb("qc", (128, 1))
    q2 = p.sb("q2", (128, 1), BF16)
    wf = p.sb("wf", (128, 16))
    psQ = p.ps("psQ", (128, 2))
    psW = p.ps("psW", (128, 16))
    p.emit("vector", lambda h: h.memset(hh[:], 0.0))
    for ch in range(NCH):
        b = ch % 2
        p.dma("sync", ag[b][:], aGt[ch], wait=[("ST", p.c("ST"))], inc="LD")
        p.dma("sync", bf_[b][:], bfm[ch], inc="LD")
        p.dma("sync", tu[b][:], TUc[ch], inc="LD")
        p.dma("sync", gg[b][:], Gc[ch], inc="LD")
        p.dma("sync", dv_[b][:], dvc[ch], inc="LD")
        ldc = p.c("LD")
        p.dma("sync", carr[ch], hh[:], wait=[("SV", p.c("SV"))], inc="SO")
        so = p.c("SO")
        p.emit("vector", lambda h: h.tensor_copy(out=hbb[:], in_=hh[:]),
               wait=[("LD", ldc)])
        for pc in range(16):
            p.emit("tensor", lambda h, pc=pc, b=b: h.matmul(
                out=psQ[:, 0:1], lhsT=bf_[b][:, 128 * pc:128 * (pc + 1)],
                rhs=hbb[:, pc:pc + 1], start=(pc == 0), stop=(pc == 15)),
                wait=(_xw(p, "SV") if pc == 0 else ()),
                inc=("ST" if pc == 15 else None))
        p.emit("vector", lambda h: h.tensor_copy(out=qc[:], in_=psQ[:, 0:1]),
               wait=[("ST", p.c("ST"))])
        p.emit("tensor", lambda h, b=b: h.matmul(out=psQ[:, 1:2], lhsT=tu[b][:],
                                                 rhs=qc[:], start=True, stop=True),
               wait=_xw(p, "SV"), inc="ST")
        p.emit("vector", lambda h: h.tensor_copy(out=q2[:], in_=psQ[:, 1:2]),
               wait=[("ST", p.c("ST"))])
        for pc in range(16):
            p.emit("tensor", lambda h, pc=pc, b=b: h.matmul(
                out=psW[:, pc:pc + 1], lhsT=ag[b][:, 128 * pc:128 * (pc + 1)],
                rhs=q2[:], start=(pc == 0), stop=(pc == 15)),
                wait=(_xw(p, "SV") if pc == 0 else ()),
                inc=("ST" if pc == 15 else None))
        p.emit("vector", lambda h, b=b: h.tensor_mul(out=hh[:], in0=hh[:],
                                                     in1=gg[b][:]),
               wait=[("ST", p.c("ST")), ("SO", so)])
        p.emit("vector", lambda h: h.scalar_tensor_tensor(
            out=hh[:], in0=psW[:], scalar=-2.0, in1=hh[:],
            op0=ALU.mult, op1=ALU.add))
        p.emit("vector", lambda h, b=b: h.tensor_add(out=hh[:], in0=hh[:],
                                                     in1=dv_[b][:]))
    p.wait("sync", ("SO", p.c("SO")))
    return p


def build_moh():
    p = Ph("moh")
    TT = T
    xnT = p.dram("xnT", (DM, TT))
    wqT = p.dram("wqT", (DM, DM))
    wkT = p.dram("wkT", (DM, DM))
    wvT = p.dram("wvT", (DM, DM))
    bqc = p.dram("bqc", (128, 8))
    bkc = p.dram("bkc", (128, 8))
    bvc = p.dram("bvc", (128, 8))
    r8 = p.dram("r8", (8, 16))
    mask8 = p.dram("mask8", (8, 1))
    kcol = p.dram("kcol", (16, 1))
    brc = p.dram("brc", (16, 1))
    kron64T = p.dram("kron64T", (64, DM))
    idc = p.dram("idc", (128, 128))
    oh = p.dram("oh", (64, TT), BF16, kind="ExternalOutput")
    dbg9 = p.dram("dbg9", (16, 8), kind="ExternalOutput")

    xs_ = [p.sb(f"xs{i}", (128, 8 * 512), BF16) for i in range(2)]
    wq_s = p.sb("wq_s", (128, 8 * DM), BF16)
    wk_s = p.sb("wk_s", (128, 8 * DM), BF16)
    wv_s = p.sb("wv_s", (128, 8 * DM), BF16)
    bq_s = p.sb("bq_s", (128, 8))
    q_sb = p.sb("q_sb", (128, TT), BF16)
    k_sb = p.sb("k_sb", (128, TT), BF16)
    v_sb = p.sb("v_sb", (128, 64 * 65), BF16)
    s_sb = [p.sb(f"s_sb{i}", (128, 512), BF16) for i in range(4)]
    ident = p.sb("ident", (128, 128))
    kron = p.sb("kron", (64, DM))
    rr = p.sb("rr", (8, 16))
    m8s = p.sb("m8s", (8, 1))
    kc = p.sb("kc", (16, 1))
    brs = p.sb("brs", (16, 1))
    dcol = p.sb("dcol", (16, 1))
    drow = p.sb("drow", (1, 16))
    dmax = p.sb("dmax", (1, 1))
    dsum = p.sb("dsum", (1, 1))
    Mt = p.sb("Mt", (16, 16))
    rankc = p.sb("rankc", (16, 1))
    selc = p.sb("selc", (16, 1))
    selr = p.sb("selr", (1, 16))
    topm = p.sb("topm", (16, 1))
    expd = p.sb("expd", (16, 1))
    gsc = p.sb("gsc", (1, 1))
    g64 = p.sb("g64", (64, 1))
    pgT = p.sb("pgT", (64, DM))
    pg = p.sb("pg", (128, 8 * 64), BF16)
    wsel = {t: p.sb(f"wsel_{t}", (128, 8 * 64), BF16) for t in "qkv"}
    bsel = {t: p.sb(f"bsel_{t}", (64, 1)) for t in "qkv"}
    bselr = {t: p.sb(f"bselr_{t}", (1, 64)) for t in "qkv"}
    onessm = p.sb("onessm", (128, 1))
    den_r = p.sb("den_r", (1, 512))
    denb = p.sb("denb", (64, 512))
    o_bf = p.sb("o_bf", (64, 512), BF16)
    psP = p.ps("psP", (128, 512))
    psO = p.ps("psO", (65, 512))
    psS = [p.ps(f"psS{i}", (128, 512)) for i in range(4)]
    psX = p.ps("psX", (128, 128))

    p.dma("sync", rr[:], r8[:], inc="LD")
    p.dma("sync", m8s[:], mask8[:], inc="LD")
    p.dma("sync", kc[:], kcol[:], inc="LD")
    p.dma("sync", brs[:], brc[:], inc="LD")
    p.dma("sync", ident[:], idc[:], inc="LD")
    p.dma("sync", kron[:], kron64T[:], inc="LD")
    p.dma("gpsimd", wq_s[:].rearrange("p (c n) -> p c n", c=8),
          wqT.rearrange("(c p) n -> p c n", p=128), inc="GD")
    p.dma("gpsimd", wk_s[:].rearrange("p (c n) -> p c n", c=8),
          wkT.rearrange("(c p) n -> p c n", p=128), inc="GD")
    p.dma("gpsimd", wv_s[:].rearrange("p (c n) -> p c n", c=8),
          wvT.rearrange("(c p) n -> p c n", p=128), inc="GD")
    p.emit("vector", lambda h: h.memset(onessm[:], 1.0), wait=_xw(p, "LD"))

    p.emit("vector", lambda h: h.tensor_scalar_mul(out=rr[:], in0=rr[:], scalar1=m8s[:]),
           wait=_xw(p, "LD"))
    p.emit("tensor", lambda h: h.matmul(out=psX[0:16, 0:1], lhsT=rr[:], rhs=onessm[0:8, :],
                                        start=True, stop=True),
           wait=_xw(p, "SV"), inc="ST")
    p.emit("vector", lambda h: h.tensor_copy(out=dcol[:], in_=psX[0:16, 0:1]),
           wait=[("ST", p.c("ST"))])
    p.emit("vector", lambda h: h.tensor_scalar_mul(out=dcol[:], in0=dcol[:],
                                                   scalar1=1.0 / T))
    p.emit("vector", lambda h: h.tensor_add(out=dcol[:], in0=dcol[:], in1=brs[:]))
    p.emit("tensor", lambda h: h.transpose(out=psX[0:1, 0:16], in_=dcol[:],
                                           identity=ident[0:16, 0:16]),
           wait=_xw(p, "SV"), inc="ST")
    p.emit("vector", lambda h: h.tensor_copy(out=drow[:], in_=psX[0:1, 0:16]),
           wait=[("ST", p.c("ST"))])
    p.emit("vector", lambda h: h.tensor_reduce(out=dmax[:], in_=drow[:], axis=AX.X,
                                               op=ALU.max, opt_input=False,
                                               opt_output=False))
    p.emit("vector", lambda h: h.tensor_scalar_mul(out=dmax[:], in0=dmax[:], scalar1=-1.0))
    p.emit("scalar", lambda h: h.activation(out=drow[:], in_=drow[:], func=AF.Exp,
                                            bias=dmax[0:1, :]), wait=_xw(p, "SV"))
    p.emit("vector", lambda h: h.tensor_reduce(out=dsum[:], in_=drow[:], axis=AX.X,
                                               op=ALU.add, opt_input=False,
                                               opt_output=False), wait=_xw(p, "SS"))
    p.emit("vector", lambda h: h.reciprocal(out=dsum[:], in_=dsum[:]))
    p.emit("vector", lambda h: h.tensor_scalar_mul(out=drow[:], in0=drow[:],
                                                   scalar1=dsum[0:1, :]))
    p.emit("tensor", lambda h: h.transpose(out=psX[0:16, 0:1], in_=drow[:],
                                           identity=ident[0:1, 0:1]),
           wait=_xw(p, "SV"), inc="ST")
    p.emit("vector", lambda h: h.tensor_copy(out=dcol[:], in_=psX[0:16, 0:1]),
           wait=[("ST", p.c("ST"))])
    p.emit("vector", lambda h: h.memset(Mt[:, 0:1], 1.0))
    p.emit("vector", lambda h: h.memset(selr[:], 1.0))
    p.emit("tensor", lambda h: h.matmul(out=psX[0:16, 0:16], lhsT=selr[:],
                                        rhs=drow[:], start=True, stop=True),
           wait=_xw(p, "SV"), inc="ST")
    p.emit("vector", lambda h: h.tensor_scalar(out=Mt[:], in0=psX[0:16, 0:16],
                                               scalar1=dcol[:, 0:1], scalar2=None,
                                               op0=ALU.is_lt),
           wait=[("ST", p.c("ST"))])
    p.emit("tensor", lambda h: h.matmul(out=psX[0:16, 1:2], lhsT=Mt[:],
                                        rhs=onessm[0:16, :], start=True, stop=True),
           wait=_xw(p, "SV"), inc="ST")
    p.emit("vector", lambda h: h.tensor_copy(out=rankc[:], in_=psX[0:16, 1:2]),
           wait=[("ST", p.c("ST"))])
    p.emit("vector", lambda h: h.tensor_tensor(out=selc[:], in0=rankc[:], in1=kc[:],
                                               op=ALU.is_equal))
    p.emit("vector", lambda h: h.tensor_single_scalar(out=topm[:], in_=rankc[:],
                                                      scalar=4.0, op=ALU.is_lt))
    p.emit("scalar", lambda h: h.activation(out=expd[:], in_=dcol[:], func=AF.Exp),
           wait=_xw(p, "SV"))
    p.emit("vector", lambda h: h.tensor_mul(out=expd[:], in0=expd[:], in1=topm[:]),
           wait=_xw(p, "SS"))
    p.emit("tensor", lambda h: h.matmul(out=psX[0:1, 2:3], lhsT=expd[:],
                                        rhs=onessm[0:16, :], start=True, stop=True),
           wait=_xw(p, "SV"), inc="ST")
    p.emit("vector", lambda h: h.tensor_copy(out=dsum[:], in_=psX[0:1, 2:3]),
           wait=[("ST", p.c("ST"))])
    p.emit("vector", lambda h: h.tensor_scalar_add(out=dsum[:], in0=dsum[:], scalar1=12.0))
    p.emit("vector", lambda h: h.reciprocal(out=dsum[:], in_=dsum[:]))
    p.emit("vector", lambda h: h.tensor_mul(out=expd[:], in0=dcol[:], in1=selc[:]))
    p.emit("tensor", lambda h: h.matmul(out=psX[0:1, 3:4], lhsT=expd[:],
                                        rhs=onessm[0:16, :], start=True, stop=True),
           wait=_xw(p, "SV"), inc="ST")
    p.emit("vector", lambda h: h.tensor_copy(out=gsc[:], in_=psX[0:1, 3:4]),
           wait=[("ST", p.c("ST"))])
    p.emit("scalar", lambda h: h.activation(out=gsc[:], in_=gsc[:], func=AF.Exp),
           wait=_xw(p, "SV"))
    p.emit("vector", lambda h: h.tensor_mul(out=gsc[:], in0=gsc[:], in1=dsum[:]),
           wait=_xw(p, "SS"))
    p.emit("vector", lambda h: h.memset(den_r[:], 1.0))
    p.emit("tensor", lambda h: h.matmul(out=psX[0:64, 4:5], lhsT=den_r[0:1, 0:64],
                                        rhs=gsc[:], start=True, stop=True),
           wait=_xw(p, "SV"), inc="ST")
    p.emit("vector", lambda h: h.tensor_copy(out=g64[:], in_=psX[0:64, 4:5]),
           wait=[("ST", p.c("ST"))])
    dbg9t = p.sb("dbg9t", (16, 8))
    p.emit("vector", lambda h: h.memset(dbg9t[:], 0.0))
    p.emit("vector", lambda h: h.tensor_copy(out=dbg9t[:, 0:1], in_=dcol[:]))
    p.emit("vector", lambda h: h.tensor_copy(out=dbg9t[:, 1:2], in_=rankc[:]))
    p.emit("vector", lambda h: h.tensor_copy(out=dbg9t[:, 2:3], in_=selc[:]))
    p.emit("vector", lambda h: h.tensor_copy(out=dbg9t[:, 3:4], in_=topm[:]))
    p.emit("vector", lambda h: h.tensor_copy(out=dbg9t[0:1, 4:5], in_=gsc[:]))
    p.emit("vector", lambda h: h.tensor_copy(out=dbg9t[0:1, 5:6], in_=dsum[:]))
    p.emit("vector", lambda h: h.tensor_copy(out=dbg9t[0:64, 6:7].rearrange("(a b) c -> a (b c)", b=1)[0:16, :], in_=g64[0:16, :])) if False else None
    p.dma("sync", dbg9[:], dbg9t[:], wait=_xw(p, "SV"), inc="SO")
    p.emit("tensor", lambda h: h.transpose(out=psX[0:1, 0:16], in_=selc[:],
                                           identity=ident[0:16, 0:16]),
           wait=_xw(p, "SV"), inc="ST")
    p.emit("vector", lambda h: h.tensor_copy(out=selr[:], in_=psX[0:1, 0:16]),
           wait=[("ST", p.c("ST"))])
    selbig = p.sb("selbig", (1, DM))
    p.emit("vector", lambda h: h.tensor_copy(
        out=selbig[:].rearrange("a (c n) -> a c n", c=16),
        in_=selr[:].unsqueeze(2).broadcast_to((1, 16, 64))))
    for half in range(2):
        p.emit("tensor", lambda h, half=half: h.matmul(
            out=psP[0:64, :], lhsT=den_r[0:1, 0:64],
            rhs=selbig[:, 512 * half:512 * (half + 1)],
            start=True, stop=True),
            wait=_xw(p, "SV"), inc="ST")
        p.emit("vector", lambda h, half=half: h.tensor_mul(
            out=pgT[:, 512 * half:512 * (half + 1)], in0=psP[0:64, :],
            in1=kron[:, 512 * half:512 * (half + 1)]),
            wait=[("ST", p.c("ST"))])
    p.emit("vector", lambda h: h.tensor_scalar_mul(out=pgT[:], in0=pgT[:],
                                                   scalar1=g64[:, 0:1]))
    for ac in range(8):
        p.emit("tensor", lambda h, ac=ac: h.transpose(
            out=psX[:, 0:64], in_=pgT[:, 128 * ac:128 * (ac + 1)],
            identity=ident[0:64, 0:64]),
            wait=[("SV", p.c("SV"))], inc="ST")
        p.emit("vector", lambda h, ac=ac: h.tensor_copy(
            out=pg[:, 64 * ac:64 * (ac + 1)], in_=psX[:, 0:64]),
            wait=[("ST", p.c("ST"))])
    for tn, wt, bt in (("q", wq_s, bqc), ("k", wk_s, bkc), ("v", wv_s, bvc)):
        for dmc in range(8):
            for ac in range(8):
                p.emit("tensor", lambda h, tn=tn, wt=wt, dmc=dmc, ac=ac: h.matmul(
                    out=psX[:, 0:64],
                    lhsT=wt[:, ac * DM + 128 * dmc:ac * DM + 128 * (dmc + 1)],
                    rhs=pg[:, 64 * ac:64 * (ac + 1)],
                    start=(ac == 0), stop=(ac == 7)),
                    wait=(_xw(p, "GD", "SV") if ac == 0 else ()),
                    inc=("ST" if ac == 7 else None))
            p.emit("vector", lambda h, tn=tn, dmc=dmc: h.tensor_copy(
                out=wsel[tn][:, 64 * dmc:64 * (dmc + 1)], in_=psX[:, 0:64]),
                wait=[("ST", p.c("ST"))])
        bcolsb = p.sb(f"bcol_{tn}", (128, 8), BF16)
        p.dma("gpsimd", bcolsb[:], bt[:], inc="GD")
        for ac in range(8):
            p.emit("tensor", lambda h, ac=ac, bcolsb=bcolsb: h.matmul(
                out=psX[0:1, 64:128], lhsT=bcolsb[:, ac:ac + 1],
                rhs=pg[:, 64 * ac:64 * (ac + 1)], start=(ac == 0), stop=(ac == 7)),
                wait=(_xw(p, "SV", "GD") if ac == 0 else ()),
                inc=("ST" if ac == 7 else None))
        p.emit("vector", lambda h, tn=tn: h.tensor_copy(out=bselr[tn][:],
                                                        in_=psX[0:1, 64:128]),
               wait=[("ST", p.c("ST"))])
        p.emit("tensor", lambda h, tn=tn: h.transpose(
            out=psX[0:64, 5:6], in_=bselr[tn][:], identity=ident[0:1, 0:1]),
            wait=_xw(p, "SV"), inc="ST")
        p.emit("vector", lambda h, tn=tn: h.tensor_copy(out=bsel[tn][:],
                                                        in_=psX[0:64, 5:6]),
               wait=[("ST", p.c("ST"))])
    NSTRIP = TT // 512
    for s in range(NSTRIP):
        sb = xs_[s % 2]
        p.dma("gpsimd", sb[:].rearrange("p (c n) -> p c n", c=8),
              xnT.rearrange("(c p) n -> p c n", p=128)[:, :, 512 * s:512 * (s + 1)],
              wait=[("ST", p.c("ST"))], inc="GD")
        gds = p.c("GD")
        for k in range(8):
            p.emit("tensor", lambda h, s=s, k=k, sb=sb: h.matmul(
                out=psP[0:64, :], lhsT=wsel["q"][:, 64 * k:64 * (k + 1)],
                rhs=sb[:, 512 * k:512 * (k + 1)],
                start=(k == 0), stop=(k == 7)),
                wait=(_xw(p, "SV") + [("GD", gds)] if k == 0 else ()),
                inc=("ST" if k == 7 else None))
        p.emit("vector", lambda h, s=s: h.tensor_scalar_add(
            out=q_sb[0:64, 512 * s:512 * (s + 1)], in0=psP[0:64, :],
            scalar1=bsel["q"][:, 0:1]), wait=[("ST", p.c("ST"))])
        p.emit("vector", lambda h, s=s: h.tensor_copy(
            out=q_sb[64:128, 512 * s:512 * (s + 1)],
            in_=q_sb[0:64, 512 * s:512 * (s + 1)]))
        for k in range(8):
            p.emit("tensor", lambda h, s=s, k=k, sb=sb: h.matmul(
                out=psP[0:64, :], lhsT=wsel["k"][:, 64 * k:64 * (k + 1)],
                rhs=sb[:, 512 * k:512 * (k + 1)],
                start=(k == 0), stop=(k == 7)),
                wait=(_xw(p, "SV") if k == 0 else ()),
                inc=("ST" if k == 7 else None))
        p.emit("vector", lambda h, s=s: h.tensor_scalar(
            out=k_sb[0:64, 512 * s:512 * (s + 1)], in0=psP[0:64, :],
            scalar1=bsel["k"][:, 0:1], scalar2=0.125,
            op0=ALU.add, op1=ALU.mult), wait=[("ST", p.c("ST"))])
        p.emit("vector", lambda h, s=s: h.tensor_copy(
            out=k_sb[64:128, 512 * s:512 * (s + 1)],
            in_=k_sb[0:64, 512 * s:512 * (s + 1)]))
        for t4 in range(4):
            t = 4 * s + t4
            for k in range(8):
                p.emit("tensor", lambda h, t4=t4, k=k, sb=sb: h.matmul(
                    out=psX[:, 0:64],
                    lhsT=sb[:, 512 * k + 128 * t4:512 * k + 128 * (t4 + 1)],
                    rhs=wsel["v"][:, 64 * k:64 * (k + 1)],
                    start=(k == 0), stop=False),
                    wait=(_xw(p, "SV") if k == 0 else ()))
            p.emit("tensor", lambda h: h.matmul(
                out=psX[:, 0:64], lhsT=den_r[0:1, 0:128],
                rhs=bselr["v"][:], start=False, stop=True), inc="ST")
            p.emit("vector", lambda h, t=t: h.tensor_copy(
                out=v_sb[:, 65 * t:65 * t + 64], in_=psX[:, 0:64]),
                wait=[("ST", p.c("ST"))])
            p.emit("vector", lambda h, t=t: h.memset(v_sb[:, 65 * t + 64:65 * (t + 1)], 1.0))
    NKB = TT // 128
    for s in range(NSTRIP):
        mins = {}
        for step in range(NKB + 3):
            kb = step
            if kb < NKB:
                buf = kb % 4
                half = kb % 2
                wv = [("SV", mins[kb - 4])] if kb >= 4 else []
                if kb == 0:
                    wv += [("SV", p.c("SV")), ("SS", p.c("SS"))]
                p.emit("tensor", lambda h, kb=kb, s=s, buf=buf, half=half: h.matmul(
                    out=psS[buf][:],
                    lhsT=k_sb[64 * half:64 * (half + 1), 128 * kb:128 * (kb + 1)],
                    rhs=q_sb[64 * half:64 * (half + 1), 512 * s:512 * (s + 1)],
                    start=True, stop=True, tile_position=(64 * half, 0)),
                    wait=wv, inc="ST")
                st = p.c("ST")
                p.emit("scalar", lambda h, buf=buf: h.activation(
                    out=s_sb[buf][:], in_=psS[buf][:], func=AF.Exp),
                    wait=[("ST", st)])
                mins[kb] = p.emit("vector", lambda h, buf=buf: h.tensor_scalar_min(
                    out=s_sb[buf][:], in0=s_sb[buf][:], scalar1=1e30),
                    wait=[("SS", p.c("SS"))])
            kbav = step - 3
            if 0 <= kbav < NKB:
                p.emit("tensor", lambda h, kbav=kbav: h.matmul(
                    out=psO[:], lhsT=v_sb[:, 65 * kbav:65 * (kbav + 1)],
                    rhs=s_sb[kbav % 4][:],
                    start=(kbav == 0), stop=(kbav == NKB - 1)),
                    wait=[("SV", mins[kbav])], inc="ST")
        st = p.c("ST")
        p.emit("vector", lambda h: h.reciprocal(out=den_r[:], in_=psO[64:65, :]),
               wait=[("ST", st)])
        p.emit("vector", lambda h: h.memset(denb[0:1, :], 1.0))
        p.emit("tensor", lambda h: h.matmul(out=psP[0:64, :], lhsT=denb[0:1, 0:64],
                                            rhs=den_r[:], start=True, stop=True),
               wait=_xw(p, "SV"), inc="ST")
        p.emit("vector", lambda h: h.tensor_copy(out=denb[:], in_=psP[0:64, :]),
               wait=[("ST", p.c("ST"))])
        p.emit("vector", lambda h: h.tensor_mul(out=o_bf[:], in0=psO[0:64, :],
                                                in1=denb[:]))
        p.dma("sync", oh[:, 512 * s:512 * (s + 1)], o_bf[:],
              wait=[("SV", p.c("SV"))], inc="SO")
    p.wait("sync", ("SO", p.c("SO")))
    return p



def _consts_l1():
    c = np.zeros((9, 128, 128), np.float32)
    idx = np.arange(128)
    c[0] = (idx[:, None] > idx[None, :])
    c[1] = np.eye(128)
    def bd(m):
        M = np.zeros((128, 128), np.float32)
        for i in range(0, 128, m):
            M[i:i + m, i:i + m] = 1
        return M
    c[2] = bd(8); c[3] = bd(16) - bd(8); c[4] = bd(32) - bd(16)
    c[5] = bd(64) - bd(32); c[6] = bd(128) - bd(64)
    c[7] = 1.0
    c[8] = (idx[:, None] < idx[None, :])
    return c


def _fmcol(v):
    return np.ascontiguousarray(v.reshape(16, 128).T.astype(np.float32))


def kernel(L0_attn_x, L0_attn_Wqkv_w, L0_attn_Wqkv_b, L0_attn_out_w, L0_attn_out_b,
           L0_ffn_weight, L0_ffn_bias, L0_ffn_delta_w, L0_ffn_delta_act,
           L1_attn_omega, L1_attn_Win, L1_attn_Wl, L1_attn_bl, L1_attn_Wv,
           L1_attn_bv, L1_attn_Wout, L1_ffn_Wq, L1_ffn_bq, L1_ffn_Wk, L1_ffn_bk,
           L1_ffn_Wv, L1_ffn_bv, L1_ffn_Wr, L1_ffn_br, L1_ffn_Wo, L1_ffn_bo):
    x = f32(L0_attn_x).reshape(-1, DM)
    C = np.ascontiguousarray

    def sl(a, c):
        return C(a[c * SLAB:(c + 1) * SLAB])

    res = run_phase(build_ln, [{"xs": sl(x, c)} for c in range(NCORES)])
    xn = np.concatenate([r["xn"] for r in res], 0)

    W = f32(L0_ffn_weight)
    dw = f32(L0_ffn_delta_w)
    maps = [{"xq_in": sl(xn, c), "wrows": C(W[c * 128:(c + 1) * 128]),
             "dact": f32(L0_ffn_delta_act).reshape(1, 1),
             "dw": C(dw[c * 128:(c + 1) * 128]),
             "onesr": np.ones((1, 128), np.float32)} for c in range(NCORES)]
    res = run_phase(build_quant, maps)
    xq = np.concatenate([r["xq"] for r in res], 0)
    wq = np.concatenate([r["wq"] for r in res], 0)

    wqT = C(wq.T)
    fb = f32(L0_ffn_bias).reshape(1, -1)
    maps = [{"xqT": C(xq[c * SLAB:(c + 1) * SLAB].T), "wqT": wqT, "fbias": fb}
            for c in range(NCORES)]
    res = run_phase(build_ffnmm, maps)
    ffn = np.concatenate([r["ffn"] for r in res], 0)

    Wqkv = f32(L0_attn_Wqkv_w)
    bqkv = f32(L0_attn_Wqkv_b)
    masks = np.zeros((4, 128, 512), np.float32)
    for r_ in range(4):
        for kk in range(128):
            masks[r_, kk, :] = (kk + 128 * r_ <= np.arange(512)).astype(np.float32)
    maps = []
    for c in range(NCORES):
        b = c // 4
        heads = [4 * (c % 4) + j for j in range(4)]
        rows = [Wqkv[base + hh * 64:base + (hh + 1) * 64]
                for base in (0, 1024, 2048) for hh in heads]
        slc = np.concatenate(rows, 0)
        bias = np.concatenate([bqkv[base + hh * 64:base + (hh + 1) * 64]
                               for base in (0, 1024, 2048) for hh in heads])
        maps.append({"xnT": C(xn[b * T:(b + 1) * T].T), "wqkvT": C(slc.T),
                     "bqkv": C(bias.reshape(-1, 1)), "bqkvr": C(bias.reshape(1, -1)),
                     "masks": masks})
    res = run_phase(build_l0attn, maps)
    ys = [r["y"] for r in res]

    owT = C(f32(L0_attn_out_w).T)
    ob = f32(L0_attn_out_b).reshape(1, -1)
    maps = []
    for c in range(NCORES):
        b = c // 4
        Yb = np.concatenate([ys[4 * b + i] for i in range(4)], 0)
        colsl = slice((c % 4) * SLAB, (c % 4 + 1) * SLAB)
        maps.append({"yfm": C(Yb[:, colsl]), "owT": owT, "ob": ob,
                     "xs": sl(x, c), "ffn": sl(ffn, c), "wr": f32(L1_ffn_Wr)})
    res = run_phase(build_residln1, maps)
    x1 = np.concatenate([r["x1"] for r in res], 0)
    xn1 = np.concatenate([r["xn"] for r in res], 0)
    rparts = np.stack([r["rpart"][0] for r in res], 0)

    Win = f32(L1_attn_Win)
    consts = _consts_l1()
    common = {"winxiT": C(Win[:DI].T), "wlT": C(f32(L1_attn_Wl).T),
              "wvT": C(f32(L1_attn_Wv).T), "omc": _fmcol(f32(L1_attn_omega).reshape(-1)),
              "blc": _fmcol(f32(L1_attn_bl)), "bvc": _fmcol(f32(L1_attn_bv)),
              "consts": consts}
    Z0 = np.zeros((128, 1), np.float32)
    O1 = np.ones((128, 1), np.float32)
    maps = [dict(common, xn1T=C(xn1[c * SLAB:(c + 1) * SLAB].T),
                 m0col=(Z0 if c % 4 == 0 else O1)) for c in range(NCORES)]
    res6 = run_phase(build_l1a, maps)
    maps7 = []
    for b in range(2):
        cm = {}
        for ki, ko in (("aGt", "aGt"), ("bfm", "bfm"), ("TU", "TUc"),
                       ("Gd", "Gc"), ("dv", "dvc")):
            cm[ko] = C(np.concatenate([res6[4 * b + i][ki] for i in range(4)], 0))
        maps7.append(cm)
    res7 = run_phase(build_carry, maps7)
    carr = [res7[b]["carr"] for b in range(2)]
    maps8 = []
    for c in range(NCORES):
        b, q = c // 4, c % 4
        maps8.append({"xn1T": C(xn1[c * SLAB:(c + 1) * SLAB].T),
                      "winzT": C(Win[DI:].T), "woTq": C(f32(L1_attn_Wout).T),
                      "a_d": res6[c]["a_d"], "bfm": res6[c]["bfm"],
                      "TU": res6[c]["TU"], "E_d": res6[c]["E_d"],
                      "LI_d": res6[c]["LI_d"], "mc_d": res6[c]["mc_d"],
                      "carries": C(carr[b][16 * q:16 * q + 16]),
                      "consts": consts})
    res8 = run_phase(build_l1rec2, maps8)
    at1 = np.concatenate([r["at1"] for r in res8], 0)

    kron64T = np.zeros((64, DM), np.float32)
    for hh in range(16):
        kron64T[:, hh * 64:(hh + 1) * 64] = np.eye(64)
    idc = np.eye(128, dtype=np.float32)
    maps = []
    for c in range(NCORES):
        b, k = c // 4, c % 4
        m8 = np.zeros((8, 1), np.float32)
        m8[4 * b:4 * b + 4] = 1.0
        maps.append({"xnT": C(xn1[b * T:(b + 1) * T].T),
                     "wqT": C(f32(L1_ffn_Wq).T), "wkT": C(f32(L1_ffn_Wk).T),
                     "wvT": C(f32(L1_ffn_Wv).T),
                     "bqc": C(f32(L1_ffn_bq).reshape(8, 128).T),
                     "bkc": C(f32(L1_ffn_bk).reshape(8, 128).T),
                     "bvc": C(f32(L1_ffn_bv).reshape(8, 128).T),
                     "r8": C(rparts), "mask8": m8,
                     "kcol": np.full((16, 1), float(k), np.float32),
                     "brc": C(f32(L1_ffn_br).reshape(16, 1)),
                     "kron64T": kron64T, "idc": idc})
    res = run_phase(build_moh, maps)
    ohs = [r["oh"] for r in res]

    wo = f32(L1_ffn_Wo)
    bo = f32(L1_ffn_bo).reshape(1, -1)
    maps = []
    for c in range(NCORES):
        b = c // 4
        Hb = np.concatenate([ohs[4 * b + i] for i in range(4)], 0)
        colsl = slice((c % 4) * SLAB, (c % 4 + 1) * SLAB)
        maps.append({"x1": sl(x1, c), "at1": sl(at1, c), "hT": C(Hb[:, colsl]),
                     "wo": wo, "bo": bo})
    res = run_phase(build_final, maps)
    out = np.concatenate([r["out"] for r in res], 0)
    return out.reshape(2, T, DM)


def build_l1rec2():
    p = Ph("l1rec2")
    xn1T = p.dram("xn1T", (DM, SLAB))
    winzT = p.dram("winzT", (DM, DI))
    woTq = p.dram("woTq", (DI, DM))
    a_d = p.dram("a_d", (NCH_CORE, 128, DI), BF16)
    bfm_d = p.dram("bfm", (NCH_CORE, 128, DI), BF16)
    TU_d = p.dram("TU", (NCH_CORE, 128, 128))
    E_d = p.dram("E_d", (NCH_CORE, 128, DI), BF16)
    LI_d = p.dram("LI_d", (NCH_CORE, 128, DI), BF16)
    mc_d = p.dram("mc_d", (NCH_CORE, 128, 1))
    carries = p.dram("carries", (NCH_CORE, 128, 16))
    consts = p.dram("consts", (9, 128, 128))
    at1 = p.dram("at1", (SLAB, DM), kind="ExternalOutput")

    xnb = p.sb("xnb", (128, 8 * 128), BF16)
    wbuf = p.sb("wbuf", (128, 8 * DI + 16 * DM), BF16)
    cst = p.sb("cst", (128, 9 * 128))
    ab = [p.sb(f"ab{i}", (128, DI), BF16) for i in range(2)]
    bb = [p.sb(f"bb{i}", (128, DI), BF16) for i in range(2)]
    Eb = [p.sb(f"Eb{i}", (128, DI), BF16) for i in range(2)]
    LIb = [p.sb(f"LIb{i}", (128, DI), BF16) for i in range(2)]
    tub = [p.sb(f"tub{i}", (128, 128)) for i in range(2)]
    mcb = [p.sb(f"mcb{i}", (128, 1)) for i in range(2)]
    g0 = [p.sb(f"g0_{i}", (128, 16)) for i in range(2)]
    g0b = p.sb("g0b", (128, 16), BF16)
    qc = p.sb("qc", (128, 1))
    bcol = p.sb("bcol", (128, 1))
    brow = p.sb("brow", (1, 128))
    brb = p.sb("brb", (128, 128))
    Wt = p.sb("Wt", (128, DI), BF16)
    CumW = p.sb("CumW", (128, DI))
    hbuf = p.sb("hbuf", (128, DI), BF16)
    zb = p.sb("zb", (128, DI), BF16)
    xoA = p.sb("xoA", (128, DM))
    psB = p.ps("psB", (128, 128))
    psC = p.ps("psC", (128, 128))
    psA = p.ps("psA", (128, 512))
    CS = lambda i: cst[:, 128 * i:128 * (i + 1)]
    STRIL, IDENT, M8, D16, D32, D64, D128, ONES, STRIU = range(9)

    p.dma("sync", cst[:].rearrange("p (c n) -> p c n", c=9),
          consts.rearrange("c p n -> p c n"), inc="LD")
    p.dma("gpsimd", wbuf[:, :8 * DI].rearrange("p (c n) -> p c n", c=8),
          winzT.rearrange("(c p) n -> p c n", p=128), inc="GD")
    p.dma("gpsimd", wbuf[:, 8 * DI:].rearrange("p (c n) -> p c n", c=16),
          woTq.rearrange("(c p) n -> p c n", p=128), inc="GD")
    for ch in range(NCH_CORE):
        cb = ch % 2
        wso = [("ST", p.c("ST")), ("SV", p.c("SV"))]
        p.dma("sync", ab[cb][:], a_d[ch], wait=wso, inc="LD")
        p.dma("sync", bb[cb][:], bfm_d[ch], inc="LD")
        p.dma("sync", Eb[cb][:], E_d[ch], inc="LD")
        p.dma("sync", LIb[cb][:], LI_d[ch], inc="LD")
        p.dma("sync", tub[cb][:], TU_d[ch], inc="LD")
        p.dma("sync", mcb[cb][:], mc_d[ch], inc="LD")
        p.dma("sync", g0[cb][:], carries[ch], inc="LD")
        p.dma("gpsimd", xnb[:].rearrange("p (c n) -> p c n", c=8),
              xn1T.rearrange("(c p) n -> p c n", p=128)[:, :, 128 * ch:128 * (ch + 1)],
              wait=[("ST", p.c("ST"))], inc="GD")
        ldc = p.c("LD")
        p.emit("vector", lambda h, cb=cb: h.tensor_copy(out=g0b[:], in_=g0[cb][:]),
               wait=[("LD", ldc)])
        for pc in range(16):
            p.emit("tensor", lambda h, pc=pc, cb=cb: h.matmul(
                out=psC[:, 1:2], lhsT=bb[cb][:, 128 * pc:128 * (pc + 1)],
                rhs=g0b[:, pc:pc + 1], start=(pc == 0), stop=(pc == 15)),
                wait=(_xw(p, "SV") if pc == 0 else ()),
                inc=("ST" if pc == 15 else None))
        p.emit("vector", lambda h, cb=cb: h.tensor_add(out=qc[:], in0=psC[:, 1:2],
                                                       in1=mcb[cb][:]),
               wait=[("ST", p.c("ST"))])
        p.emit("tensor", lambda h, cb=cb: h.matmul(out=psC[:, 0:1], lhsT=tub[cb][:],
                                                   rhs=qc[:], start=True, stop=True),
               wait=_xw(p, "SV"), inc="ST")
        p.emit("vector", lambda h: h.tensor_copy(out=bcol[:], in_=psC[:, 0:1]),
               wait=[("ST", p.c("ST"))])
        p.emit("tensor", lambda h: h.transpose(out=psC[0:1, 0:128], in_=bcol[:],
                                               identity=CS(IDENT)),
               wait=_xw(p, "SV"), inc="ST")
        p.emit("vector", lambda h: h.tensor_copy(out=brow[:], in_=psC[0:1, 0:128]),
               wait=[("ST", p.c("ST"))])
        p.emit("tensor", lambda h: h.matmul(out=psC[:], lhsT=CS(ONES)[0:1, :],
                                            rhs=brow[:], start=True, stop=True),
               wait=_xw(p, "SV"), inc="ST")
        p.emit("vector", lambda h: h.tensor_copy(out=brb[:], in_=psC[:]),
               wait=[("ST", p.c("ST"))])
        p.emit("vector", lambda h, cb=cb: h.tensor_tensor(
            out=Wt[:].rearrange("p (c n) -> p c n", c=16),
            in0=ab[cb][:].rearrange("p (c n) -> p c n", c=16),
            in1=brb[:].unsqueeze(1).broadcast_to((128, 16, 128)),
            op=ALU.mult))
        _scan16(p, CumW, Wt, CS(ONES), [])
        p.emit("vector", lambda h, cb=cb: h.scalar_tensor_tensor(
            out=CumW[:], in0=CumW[:], scalar=-2.0, in1=Eb[cb][:],
            op0=ALU.mult, op1=ALU.add))
        for pc in range(16):
            p.emit("vector", lambda h, pc=pc, cb=cb: h.tensor_scalar_add(
                out=CumW[:, 128 * pc:128 * (pc + 1)],
                in0=CumW[:, 128 * pc:128 * (pc + 1)],
                scalar1=g0[cb][:, pc:pc + 1]))
        p.emit("vector", lambda h, cb=cb: h.tensor_mul(out=hbuf[:], in0=CumW[:],
                                                       in1=LIb[cb][:]))
        for es in range(16):
            for k in range(8):
                p.emit("tensor", lambda h, es=es, k=k: h.matmul(
                    out=psB[:],
                    lhsT=wbuf[:, k * DI + 128 * es:k * DI + 128 * (es + 1)],
                    rhs=xnb[:, k * 128:(k + 1) * 128],
                    start=(k == 0), stop=(k == 7)),
                    wait=(_xw(p, "GD", "SV") if k == 0 else ()),
                    inc=("ST" if k == 7 else None))
            p.emit("vector", lambda h, es=es: h.tensor_copy(
                out=zb[:, 128 * es:128 * (es + 1)], in_=psB[:]),
                wait=[("ST", p.c("ST"))])
        p.emit("scalar", lambda h: h.activation(out=zb[:], in_=zb[:], func=AF.Silu),
               wait=_xw(p, "SV"))
        p.emit("vector", lambda h: h.tensor_mul(out=zb[:], in0=zb[:], in1=hbuf[:]),
               wait=_xw(p, "SS"))
        for oc in range(2):
            for js in range(16):
                p.emit("tensor", lambda h, js=js, oc=oc: h.matmul(
                    out=psA[:],
                    lhsT=zb[:, 128 * js:128 * (js + 1)],
                    rhs=wbuf[:, 8 * DI + js * DM + 512 * oc:
                             8 * DI + js * DM + 512 * (oc + 1)],
                    start=(js == 0), stop=(js == 15)),
                    wait=(_xw(p, "SV") if js == 0 else ()),
                    inc=("ST" if js == 15 else None))
            p.emit("vector", lambda h, oc=oc: h.tensor_copy(
                out=xoA[:, 512 * oc:512 * (oc + 1)], in_=psA[:]),
                wait=[("ST", p.c("ST"))] + ([("SO", p.c("SO"))] if oc == 0 else []))
        p.dma("sync", at1[128 * ch:128 * (ch + 1), :], xoA[:],
              wait=_xw(p, "SV"), inc="SO")
    p.wait("sync", ("SO", p.c("SO")))
    return p
